# version 04
# baseline (speedup 1.0000x reference)
import numpy as np
import ml_dtypes
import contextlib
import concourse.bass as bass
import concourse.mybir as mybir
from concourse.bass_utils import run_bass_kernel_spmd

F32 = mybir.dt.float32
BF16 = mybir.dt.bfloat16
I32 = mybir.dt.int32
AF = mybir.ActivationFunctionType
ALU = mybir.AluOpType
AX = mybir.AxisListType
bfd = ml_dtypes.bfloat16

NCORE = 8
B = 4
T = 4096
E = 1024
H = 16
NB = 10
SPAN = 1536
NKB = 12
NEG = -1.0e9
INNER = 4096
CROW = 2048
LPAD = 512
OWN0 = 1024

ENGS = ("t", "v", "s", "y", "g")


class Cell:
    __slots__ = ("eng", "idx", "inc", "val")
    def __init__(self, eng, idx):
        self.eng = eng; self.idx = idx; self.inc = False; self.val = None


class DmaTok:
    __slots__ = ("sem", "val")
    def __init__(self, sem, val):
        self.sem = sem; self.val = val


class Buf:
    __slots__ = ("name", "w", "readers")
    def __init__(self, name):
        self.name = name; self.w = None; self.readers = []


class P:
    def __init__(self, nc):
        self.nc = nc
        self.log = {}
        self.q = {e: [] for e in ENGS}
        self.opidx = {e: 0 for e in ENGS}
        self.lastdrain = {e: -1 for e in ENGS}
        self.wm = {}
        self.dma_wm = {}
        self.cells = {e: [] for e in ENGS}
        self.sems = {}
        self.dma_cnt = {}
        self.dma_names = {"y": ["dy0", "dy1"], "g": ["dg0", "dg1"], "s": ["ds0", "ds1"]}
        self.dma_rr = {"y": 0, "g": 0, "s": 0}
        self.bufs = {}

    def B(self, name):
        if name not in self.bufs:
            self.bufs[name] = Buf(name)
        return self.bufs[name]

    def _await(self, eng, tk):
        if isinstance(tk, DmaTok):
            key = (eng, tk.sem)
            if self.dma_wm.get(key, 0) >= tk.val:
                return
            self.dma_wm[key] = tk.val
            sem_name, val = tk.sem, tk.val
            self.log.setdefault(eng, []).append(("waitD", sem_name, val))
            self.q[eng].append(lambda h: h.wait_ge(self.sems[sem_name], val))
        elif tk.eng == eng:
            if eng == "t":
                return
            if tk.idx > self.lastdrain[eng]:
                self.log.setdefault(eng, []).append(("drain",))
                self.q[eng].append(lambda h: h.drain())
                self.lastdrain[eng] = self.opidx[eng] - 1
        else:
            tk.inc = True
            key = (eng, tk.eng)
            if self.wm.get(key, -1) >= tk.idx:
                return
            self.wm[key] = tk.idx
            peng = tk.eng
            self.log.setdefault(eng, []).append(("waitE", peng, tk))
            self.q[eng].append(lambda h, tk=tk, peng=peng: h.wait_ge(self.sems[peng], tk.val))

    def _deps(self, eng, reads, writes):
        for tk in getattr(self, "fence", ()):
            self._await(eng, tk)
        toks = []
        for b in reads:
            if b.w is not None:
                toks.append(b.w)
        for b in writes:
            if b.w is not None:
                toks.append(b.w)
            toks.extend(b.readers)
        for tk in toks:
            self._await(eng, tk)

    def op(self, eng, emitfn, reads=(), writes=()):
        self._deps(eng, reads, writes)
        cell = Cell(eng, self.opidx[eng])
        def run(h, cell=cell, emitfn=emitfn, self_=self):
            ins = emitfn(h)
            if cell.inc:
                ins.then_inc(self_.sems[cell.eng], 1)
        self.q[eng].append(run)
        self.cells[eng].append(cell)
        import traceback as _tb
        fr = _tb.extract_stack()[-2]
        self.log.setdefault(eng, []).append(("op", cell, f"{fr.name}:{fr.lineno}",
                                             [b.name for b in reads], [b.name for b in writes]))
        self.opidx[eng] += 1
        for b in writes:
            b.w = cell; b.readers = []
        for b in reads:
            b.readers.append(cell)
        return cell

    def dma(self, qeng, emitfn, reads=(), writes=()):
        self._deps(qeng, reads, writes)
        name = self.dma_names[qeng][self.dma_rr[qeng]]
        self.dma_rr[qeng] ^= 1
        val = self.dma_cnt.get(name, 0) + 16
        self.dma_cnt[name] = val
        tok = DmaTok(name, val)
        def run(h, emitfn=emitfn, name=name, self_=self):
            ins = emitfn(h)
            ins.then_inc(self_.sems[name], 16)
        self.q[qeng].append(run)
        self.log.setdefault(qeng, []).append(("dma", name, val,
                                              [b.name for b in reads], [b.name for b in writes]))
        self.opidx[qeng] += 1
        for b in writes:
            b.w = tok; b.readers = []
        for b in reads:
            b.readers.append(tok)
        return tok

    def barrier(self, dma=True):
        best = {}
        for b in self.bufs.values():
            toks = list(b.readers)
            if b.w is not None:
                toks.append(b.w)
            for tk in toks:
                if isinstance(tk, DmaTok):
                    key = ("d", tk.sem)
                    cur = best.get(key)
                    if cur is None or tk.val > cur.val:
                        best[key] = tk
                else:
                    key = ("e", tk.eng)
                    cur = best.get(key)
                    if cur is None or tk.idx > cur.idx:
                        best[key] = tk
        for sem, val in self.dma_cnt.items():
            key = ("d", sem)
            cur = best.get(key)
            if cur is None or val > cur.val:
                best[key] = DmaTok(sem, val)
        self.fence = list(best.values())

    def assign_values(self):
        for e in ENGS:
            v = 0
            for c in self.cells[e]:
                if c.inc:
                    v += 1
                    c.val = v


def build_program(stages=("wq", "attn", "ffn1", "attnb", "ffn2")):
    nc = bass.Bass("TRN2", target_bir_lowering=False, detect_race_conditions=False,
                   dynamic_dma_scratch_size=16384)

    xsp = nc.dram_tensor("xsp", [B, SPAN, E], F32, kind="ExternalInput").ap()
    wq_d = nc.dram_tensor("wq", [E, E], F32, kind="ExternalInput").ap()
    wk_d = nc.dram_tensor("wk", [E, E], F32, kind="ExternalInput").ap()
    wv_d = nc.dram_tensor("wv", [E, E], F32, kind="ExternalInput").ap()
    wrp_d = nc.dram_tensor("wrp", [E, 512], F32, kind="ExternalInput").ap()
    wpj_d = nc.dram_tensor("wpj", [E, E], F32, kind="ExternalInput").ap()
    bqc_d = nc.dram_tensor("bqc", [128, 8], F32, kind="ExternalInput").ap()
    brc_d = nc.dram_tensor("brc", [128, 4], F32, kind="ExternalInput").ap()
    bpj_d = nc.dram_tensor("bpj", [1, E], F32, kind="ExternalInput").ap()
    bndr_d = nc.dram_tensor("bndr", [NB, 1024], F32, kind="ExternalInput").ap()
    edge_d = nc.dram_tensor("edge", [1, 1024], F32, kind="ExternalInput").ap()
    winT_d = nc.dram_tensor("winT", [E, E], F32, kind="ExternalInput").ap()
    woutT_d = nc.dram_tensor("woutT", [E, E], F32, kind="ExternalInput").ap()
    binc_d = nc.dram_tensor("binc", [128, 8], F32, kind="ExternalInput").ap()
    bout_d = nc.dram_tensor("bout", [1, E], F32, kind="ExternalInput").ap()
    gwc_d = nc.dram_tensor("gwc", [128, 8], F32, kind="ExternalInput").ap()
    wgT_d = nc.dram_tensor("wgT", [E, INNER], F32, kind="ExternalInput").ap()
    wuT_d = nc.dram_tensor("wuT", [E, INNER], F32, kind="ExternalInput").ap()
    wdT_d = nc.dram_tensor("wdT", [INNER, E], F32, kind="ExternalInput").ap()
    id_d = nc.dram_tensor("idm", [128, 128], BF16, kind="ExternalInput").ap()
    out_d = nc.dram_tensor("out", [2048, E], F32, kind="ExternalOutput").ap()
    import os as _osd
    KDBG = int(_osd.environ.get("KDBG", "0"))
    if KDBG == 1:
        dbgE = nc.dram_tensor("dbgE", [128, 4 * 1152], BF16, kind="ExternalOutput").ap()
        dbgW = nc.dram_tensor("dbgW", [128, 4 * 512], BF16, kind="ExternalOutput").ap()
        dbgA = nc.dram_tensor("dbgA", [128, 4 * 65], F32, kind="ExternalOutput").ap()
        dbgS = nc.dram_tensor("dbgS", [128, 4 * 1024], BF16, kind="ExternalOutput").ap()
    if KDBG == 2:
        dbgR = nc.dram_tensor("dbgR", [128, 4 * 1024], F32, kind="ExternalOutput").ap()
        dbgT = nc.dram_tensor("dbgT", [128, 8 * 512], BF16, kind="ExternalOutput").ap()
        dbgO = nc.dram_tensor("dbgO", [128, 4 * 1024], F32, kind="ExternalOutput").ap()

    cdr = nc.dram_tensor("cdr", [2 * 512 * CROW], BF16, kind="Internal").ap()
    wgQ = nc.dram_tensor("wgQ", [E * INNER], BF16, kind="Internal").ap()
    wuQ = nc.dram_tensor("wuQ", [E * INNER], BF16, kind="Internal").ap()
    wdQ = nc.dram_tensor("wdQ", [INNER * E], BF16, kind="Internal").ap()
    scl_d = nc.dram_tensor("scl", [1, 4], F32, kind="Internal").ap()

    p = P(nc)
    B_ = p.B
    A = bass.AP

    PB = ((nc.sbuf_base + 63) // 64) * 64

    def at(name, shape, dt, off):
        return nc.alloc_sbuf_tensor_at(name, shape, dt, offset=PB + off)

    x_tc = at("x_tc", [128, 16, 1024], F32, 0)
    junk = at("junk", [128, 1024], BF16, 65536)
    ident = at("ident", [128, 128], BF16, 67584)
    zero128 = at("zero128", [128, 128], BF16, 67840)
    ones_row = at("ones_row", [1, 512], BF16, 68096)
    cols = at("cols", [128, 64], F32, 69120)
    bqc = at("bqc_s", [128, 8], F32, 69376)
    brc = at("brc_s", [128, 4], F32, 69408)
    binc = at("binc_s", [128, 8], F32, 69440)
    gwc = at("gwc_s", [128, 8], F32, 69472)
    kcv = at("kcv", [128, 32], F32, 69504)
    wscal = at("wscal", [128, 8], F32, 69632)

    OV = 69696
    qwf = [at(f"qwf{i}", [128, 512], F32, OV + i * 4096) for i in range(2)]
    qwq = [at(f"qwq{i}", [128, 512], BF16, OV + 8192 + i * 1024) for i in range(2)]
    qwt = [at(f"qwt{i}", [128, 512], BF16, OV + 10240 + i * 1024) for i in range(2)]

    xt0 = at("xt0", [128, 1024], F32, OV + 0)
    cst2 = at("cst2", [128, 1024], BF16, OV + 0)
    xt1 = at("xt1", [128, 1024], F32, OV + 4096)
    xlt = at("xlt", [128, 1024], BF16, OV + 8192)
    qpe = at("qpe", [128, 512], BF16, OV + 8192)
    qpo = at("qpo", [128, 512], BF16, OV + 9216)
    xct = at("xct", [128, 2 * 8 * 512], BF16, OV + 10240)
    wslot0 = at("wslot0", [128, 8, 1024], BF16, OV + 26624)
    wslot1 = at("wslot1", [128, 8, 512], BF16, OV + 43008)
    kp = at("kp", [128, 8, 1536], BF16, OV + 51200)
    vt = at("vt", [128, 12, 1040], BF16, OV + 75776)
    q_sb = at("q_sb", [128, 8, 512], BF16, OV + 100736)
    rr = at("rr", [128, 4, 512], BF16, OV + 108928)
    rst = at("rst", [128, 512], BF16, OV + 113024)
    wsb = at("wsb", [128, 4, 512], BF16, OV + 114048)
    esb = at("esb", [128, 4, 1152], BF16, OV + 118144)
    act_ = at("act_", [128, 8, 512], BF16, OV + 118144)
    asb = at("asb", [128, 4, 1024], BF16, OV + 127360)
    brev = at("brev", [128, 1024], BF16, OV + 135552)
    edge_b = at("edge_b", [1, 1024], BF16, OV + 137600)
    bpj = at("bpj_s", [1, 1024], BF16, OV + 139648)

    hb = at("hb", [128, 8, 4096], BF16, OV + 0)
    xqc = at("xqc", [128, 8 * 1024], BF16, OV + 65536)
    xql = [at(f"xql{i}", [128, 1024], F32, OV + 81920 + i * 4096) for i in range(2)]
    xqi = [at(f"xqi{i}", [128, 1024], BF16, OV + 90112 + i * 2048) for i in range(2)]
    rti = [at(f"rti{i}", [128, 512], I32, OV + 94208 + i * 2048) for i in range(2)]
    hot = at("hot", [128, 128, 128], BF16, OV + 65536)
    rtf = [at(f"rtf{i}", [128, 512], F32, OV + 98304 + i * 2048) for i in range(2)]
    rtg = [at(f"rtg{i}", [128, 512], BF16, OV + 102400 + i * 1024) for i in range(2)]
    gbuf = [at(f"gbf{i}", [128, 512], F32, OV + 104448 + i * 2048) for i in range(3)]
    wfs = [at(f"wfs{i}", [128, 8, 512], BF16, OV + 110592 + i * 8192) for i in range(3)]
    wdfs = [at(f"wdfs{i}", [128, 512], BF16, OV + 135168 + i * 1024) for i in range(2)]

    ub = at("ub", [128, 8, 2048], BF16, OV + 0)
    vbb = at("vbb", [128, 8, 512], BF16, OV + 32768)
    xctb = at("xctb", [128, 8, 512], BF16, OV + 40960)
    xltb2 = at("xltb2", [128, 1024], BF16, OV + 49152)
    wib = at("wib", [128, 8, 1024], BF16, OV + 51200)
    wob = at("wob", [128, 8, 1024], BF16, OV + 67584)
    cvt0 = at("cvt0", [128, 512], F32, OV + 83968)
    tts = [at(f"tts{i}", [128, 512], F32, OV + 86016 + i * 2048) for i in range(2)]
    bout = at("bout_s", [1, 1024], BF16, OV + 90112)

    stack = contextlib.ExitStack()
    pbig = [stack.enter_context(nc.psum_tensor(f"pbig{i}", [128, 1024], F32)) for i in range(4)]
    def pv(i, a=0, b2=512):
        return pbig[i // 2][:, (i % 2) * 512 + a:(i % 2) * 512 + b2]
    bank = [B_(f"bank{i}") for i in range(8)]

    for s_ in ("t", "v", "s", "y", "g", "dy0", "dy1", "dg0", "dg1", "ds0", "ds1"):
        p.sems[s_] = stack.enter_context(nc.semaphore("sem_" + s_))

    import os as _os
    KI = _os.environ.get("KI", "xin,smalls,pads").split(",")
    for _b in range(B if "xin" in KI else 0):
        p.dma("y", lambda h, _b=_b: h.dma_start(
            A(x_tc, _b * 4 * 1024, [[16 * 1024, 128], [1024, 4], [1, 1024]]),
            A(xsp.tensor, _b * SPAN * 1024 + OWN0 * 1024,
              [[1024, 128], [128 * 1024, 4], [1, 1024]])),
            writes=[B_("x_tc")])
    p.dma("y", lambda h: h.dma_start(ident[:, :], id_d), writes=[B_("ident")])
    if "smalls" in KI:
        p.dma("g", lambda h: h.dma_start(brev[0:NB, :], bndr_d), writes=[B_("brev")])
        p.dma("g", lambda h: h.dma_start(edge_b[:, :], edge_d), writes=[B_("edge_b")])
        p.dma("g", lambda h: h.dma_start(bpj[:, :], bpj_d), writes=[B_("bpj")])
        p.dma("y", lambda h: h.dma_start(bqc[:, :], bqc_d), writes=[B_("bqc")])
        p.dma("y", lambda h: h.dma_start(brc[:, :], brc_d), writes=[B_("brc")])
        p.dma("y", lambda h: h.dma_start(binc[:, :], binc_d), writes=[B_("binc")])
        p.dma("y", lambda h: h.dma_start(gwc[:, :], gwc_d), writes=[B_("gwc")])
    p.op("v", lambda h: h.memset(ones_row[:, :], 1.0), writes=[B_("ones_row")])
    p.op("v", lambda h: h.memset(zero128[:, :], 0.0), writes=[B_("zero128")])
    if "smalls" in KI:
        p.op("v", lambda h: h.tensor_scalar(bqc[:, :], bqc[:, :], 0.125, None, ALU.mult),
             reads=[B_("bqc")], writes=[B_("bqc")])
    p.op("v", lambda h: h.memset(junk[:, 0:512], NEG), writes=[B_("junk")])
    for par in range(2 if "pads" in KI else 0):
        for side in range(2):
            for rk in range(4):
                off = par * 512 * CROW + rk * 128 * CROW + side * 1536
                p.dma("g", lambda h, off=off: h.dma_start(
                    A(cdr.tensor, off, [[CROW, 128], [1, 512]]), junk[:, 0:512]),
                    reads=[B_("junk")], writes=[B_(f"cpad{par}")])

    def ln_stats(srcap, srcbuf, ci, colbuf):
        p.op("v", lambda h: h.reduce_sum(cols[:, ci + 2:ci + 3], srcap, axis=AX.X),
             reads=[srcbuf], writes=[colbuf])
        p.op("s", lambda h: h.activation(junk[:, :], srcap, AF.Square,
                                         accum_out=cols[:, ci + 3:ci + 4]),
             reads=[srcbuf], writes=[B_("junk"), colbuf])
        p.op("v", lambda h: h.tensor_scalar(cols[:, ci + 2:ci + 3], cols[:, ci + 2:ci + 3],
             1.0 / 1024, None, ALU.mult), reads=[colbuf], writes=[colbuf])
        p.op("v", lambda h: h.tensor_scalar(cols[:, ci + 3:ci + 4], cols[:, ci + 3:ci + 4],
             1.0 / 1024, None, ALU.mult), reads=[colbuf], writes=[colbuf])
        p.op("v", lambda h: h.tensor_tensor(cols[:, ci + 4:ci + 5], cols[:, ci + 2:ci + 3],
             cols[:, ci + 2:ci + 3], ALU.mult), reads=[colbuf], writes=[colbuf])
        p.op("v", lambda h: h.tensor_tensor(cols[:, ci + 3:ci + 4], cols[:, ci + 3:ci + 4],
             cols[:, ci + 4:ci + 5], ALU.subtract), reads=[colbuf], writes=[colbuf])
        p.op("v", lambda h: h.tensor_scalar(cols[:, ci + 3:ci + 4], cols[:, ci + 3:ci + 4],
             1e-5, None, ALU.add), reads=[colbuf], writes=[colbuf])
        p.op("s", lambda h: h.activation(cols[:, ci + 4:ci + 5], cols[:, ci + 3:ci + 4],
             AF.Ln), reads=[colbuf], writes=[colbuf])
        p.op("s", lambda h: h.activation(cols[:, ci:ci + 1], cols[:, ci + 4:ci + 5],
             AF.Exp, scale=-0.5), reads=[colbuf], writes=[colbuf])
        p.op("v", lambda h: h.tensor_tensor(cols[:, ci + 1:ci + 2], cols[:, ci + 2:ci + 3],
             cols[:, ci:ci + 1], ALU.mult), reads=[colbuf], writes=[colbuf])
        p.op("v", lambda h: h.tensor_scalar(cols[:, ci + 1:ci + 2], cols[:, ci + 1:ci + 2],
             -1.0, None, ALU.mult), reads=[colbuf], writes=[colbuf])

    def rnd_int8(srcap, srcbufs, scol, colbuf, dstap, dstbuf, ch):
        k = ch % 2
        ib, fb2, gb_ = rti[k], rtf[k], rtg[k]
        ibb, fbb, gbb = B_(f"rti{k}"), B_(f"rtf{k}"), B_(f"rtg{k}")
        p.op("v", lambda h: h.tensor_scalar(ib[:, :], srcap, scol, 384.0, ALU.mult, ALU.add),
             reads=list(srcbufs) + [colbuf], writes=[ibb])
        p.op("v", lambda h: h.scalar_tensor_tensor(fb2[:, :], srcap, scol, ib[:, :],
             ALU.mult, ALU.subtract), reads=list(srcbufs) + [colbuf, ibb], writes=[fbb])
        p.op("v", lambda h: h.tensor_single_scalar(gb_[:, :], fb2[:, :], -383.5, ALU.is_gt),
             reads=[fbb], writes=[gbb])
        p.op("v", lambda h: h.scalar_tensor_tensor(dstap, ib[:, :], -384.0, gb_[:, :],
             ALU.add, ALU.add), reads=[ibb, gbb], writes=[dstbuf])

    def plan_wquant():
        cw = B_("colW")
        mats = [(wgT_d, wgQ, 0), (wuT_d, wuQ, 1), (wdT_d, wdQ, 2)]
        csz = 128 * 512
        nch = (E * INNER) // csz
        p.op("v", lambda h: h.memset(kcv[:, 30:31], 1.0), writes=[B_("kcvo")])
        for (src, dst, mi) in mats:
            acc = 36 + mi
            p.op("v", lambda h, acc=acc: h.memset(cols[:, acc:acc + 1], 0.0), writes=[cw])
            for ch in range(nch):
                k = ch % 2
                fb = B_(("xt0", "xt1")[k])
                p.dma("g", lambda h, src=src, ch=ch, k=k: h.dma_start(
                    qwf[k][:, :], A(src.tensor, ch * csz, [[512, 128], [1, 512]])),
                    writes=[fb])
                p.op("v", lambda h, k=k: h.reduce_sum(cols[:, 39:40], qwf[k][:, :],
                     axis=AX.X, apply_absolute_value=True), reads=[fb], writes=[cw])
                p.op("v", lambda h, acc=acc: h.tensor_tensor(cols[:, acc:acc + 1],
                     cols[:, acc:acc + 1], cols[:, 39:40], ALU.add), reads=[cw], writes=[cw])
            bk = bank[7]
            p.op("t", lambda h, acc=acc: h.matmul(pv(7, 0, 1)[0:1, :], cols[:, acc:acc + 1],
                 kcv[:, 30:31], start=True, stop=True), reads=[cw, B_("kcvo")], writes=[bk])
            p.op("s", lambda h, acc=acc: h.copy(cols[0:1, 39:40], pv(7, 0, 1)[0:1, :]),
                 reads=[bk], writes=[cw])
            p.op("v", lambda h, acc=acc: h.tensor_scalar(cols[0:1, acc:acc + 1],
                 cols[0:1, 39:40], 1.0 / (E * INNER), None, ALU.mult),
                 reads=[cw], writes=[cw])
            p.op("v", lambda h, acc=acc: h.tensor_scalar(cols[0:1, acc:acc + 1],
                 cols[0:1, acc:acc + 1], 1e-5, None, ALU.max), reads=[cw], writes=[cw])
        p.dma("y", lambda h: h.dma_start(A(scl_d.tensor, 0, [[1, 1], [1, 3]]),
              cols[0:1, 36:39]), reads=[cw], writes=[B_("scl")])
        p.dma("y", lambda h: h.dma_start(A(wscal, 0, [[8, 128], [1, 3]]),
              A(scl_d.tensor, 0, [[0, 128], [1, 3]])),
              reads=[B_("scl")], writes=[B_("wscal")])
        wsc = B_("wscal")
        for mi in range(3):
            p.op("v", lambda h, mi=mi: h.reciprocal(wscal[:, 3 + mi:4 + mi],
                 wscal[:, mi:mi + 1]), reads=[wsc], writes=[wsc])
        p.op("v", lambda h: h.tensor_scalar(wscal[:, 6:7], wscal[:, 3:4], -1.0, None,
             ALU.mult), reads=[wsc], writes=[wsc])
        p.op("v", lambda h: h.tensor_scalar(wscal[:, 7:8], wscal[:, 4:5], -1.0, None,
             ALU.mult), reads=[wsc], writes=[wsc])
        p.op("v", lambda h: h.tensor_scalar(kcv[:, 29:30], wscal[:, 5:6], -1.0, None,
             ALU.mult), reads=[wsc], writes=[B_("kcvq")])
        for (src, dst, mi) in mats:
            scol = wscal[:, 3 + mi:4 + mi]
            nscol = kcv[:, 29:30] if mi == 2 else wscal[:, 6 + mi:7 + mi]
            nsb = B_("kcvq") if mi == 2 else wsc
            for ch in range(nch):
                k = ch % 2
                fb, qb, tb = B_(("xt0", "xt1")[k]), B_("qpxl"), B_("xct0")
                p.dma("g", lambda h, src=src, ch=ch, k=k: h.dma_start(
                    qwf[k][:, :], A(src.tensor, ch * csz, [[512, 128], [1, 512]])),
                    writes=[fb])
                p.op("v", lambda h, k=k, scol=scol: h.tensor_scalar(
                    qwq[k][:, :], qwf[k][:, :], scol, 0.5, ALU.mult, ALU.is_gt),
                    reads=[fb, wsc], writes=[qb])
                p.op("v", lambda h, k=k, nscol=nscol, nsb=nsb: h.tensor_scalar(
                    qwt[k][:, :], qwf[k][:, :], nscol, 0.5, ALU.mult, ALU.is_gt),
                    reads=[fb, nsb], writes=[tb])
                p.op("v", lambda h, k=k: h.tensor_tensor(
                    qwq[k][:, :], qwq[k][:, :], qwt[k][:, :], ALU.subtract),
                    reads=[qb, tb], writes=[qb])
                p.dma("g", lambda h, dst=dst, ch=ch, k=k: h.dma_start(
                    A(dst.tensor, ch * csz, [[512, 128], [1, 512]]), qwq[k][:, :]),
                    reads=[qb], writes=[B_(f"wQd{mi}")])

    def plan_attn():
        import os
        KB_ = int(os.environ.get("KBATCH", str(B)))
        KH_ = int(os.environ.get("KHEAD", str(H)))
        KPH = os.environ.get("KPH", "p1,p2,p3").split(",")
        KP1 = os.environ.get("KP1", "w,x,lns,lna,tr,xc,k,v,vm,qr").split(",")
        for b in range(KB_):
            if "w" in KP1:
                p.dma("g", lambda h: h.dma_start(
                    A(wslot0, 0, [[8 * 1024, 128], [1024, 8], [1, 1024]]),
                    A(wk_d.tensor, 0, [[1024, 128], [128 * 1024, 8], [1, 1024]])),
                    reads=[B_("wslot0")], writes=[B_("wslot0")])
            for nn in range(3):
                cx = nn % 2
                xcb = B_(f"xct{cx}")
                for stl in range(4):
                    st = nn * 4 + stl
                    k = st % 2
                    xt = (xt0, xt1)[k]
                    xtb = B_(f"xt{k}")
                    ci = k * 8
                    cb = B_(f"colL{k}")
                    p.dma("y", lambda h, b=b, st=st, xt=xt: h.dma_start(
                        xt[:, :], xsp[b, st * 128:(st + 1) * 128, :]), writes=[xtb])
                    if "lns" not in KP1:
                        continue
                    ln_stats(xt[:, :], xtb, ci, cb)
                    if "lna" not in KP1:
                        continue
                    p.op("s", lambda h, xt=xt, ci=ci: h.activation(
                        xlt[:, :], xt[:, :], AF.Identity,
                        bias=cols[:, ci + 1:ci + 2], scale=cols[:, ci:ci + 1]),
                        reads=[xtb, cb], writes=[B_("qpxl")])
                    if "tr" not in KP1:
                        continue
                    for half in range(2):
                        bki = 4 + (stl + half) % 2
                        bk = bank[bki]
                        for kc in range(4):
                            kg = half * 4 + kc
                            p.op("t", lambda h, kg=kg, kc=kc, bki=bki: h.matmul(
                                pv(bki, kc * 128, (kc + 1) * 128),
                                xlt[:, kg * 128:(kg + 1) * 128], ident[:, :],
                                start=True, stop=True),
                                reads=[B_("qpxl"), B_("ident")], writes=[bk])
                        p.op("s", lambda h, cx=cx, stl=stl, half=half, bki=bki: h.copy(
                            A(xct, cx * 8 * 512 + half * 4 * 512 + stl * 128,
                              [[2 * 8 * 512, 128], [512, 4], [1, 128]]),
                            pv(bki).rearrange("p (a c) -> p a c", a=4)),
                            reads=[bk], writes=[xcb])
                for ft in range(8 if "k" in KP1 else 0):
                    bki = ft % 2
                    bk = bank[bki]
                    for kc in range(8):
                        p.op("t", lambda h, ft=ft, kc=kc, bki=bki, cx=cx: h.matmul(
                            pv(bki), wslot0[:, kc, ft * 128:(ft + 1) * 128],
                            A(xct, cx * 8 * 512 + kc * 512, [[8192, 128], [1, 512]]),
                            start=(kc == 0), stop=(kc == 7)),
                            reads=[B_("wslot0"), xcb], writes=[bk])
                    p.op("s", lambda h, ft=ft, nn=nn, bki=bki: h.copy(
                        kp[:, ft, nn * 512:(nn + 1) * 512], pv(bki)),
                        reads=[bk], writes=[B_("kp")])
                for nn2 in range(2 if "v" in KP1 else 0):
                    p.dma("g", lambda h, nn2=nn2: h.dma_start(
                        A(wslot1, 0, [[8 * 512, 128], [512, 8], [1, 512]]),
                        A(wv_d.tensor, nn2 * 512, [[1024, 128], [128 * 1024, 8], [1, 512]])),
                        reads=[B_("wslot1")], writes=[B_("wslot1")])
                    for mtl in range(4):
                        mt = nn * 4 + mtl
                        bki = 2 + mtl % 2
                        bk = bank[bki]
                        for kc in range(8):
                            p.op("t", lambda h, mtl=mtl, kc=kc, bki=bki, cx=cx: h.matmul(
                                pv(bki),
                                A(xct, cx * 8 * 512 + kc * 512 + mtl * 128, [[8192, 128], [1, 128]]),
                                wslot1[:, kc, :], start=(kc == 0), stop=(kc == 7)),
                                reads=[B_("wslot1"), xcb], writes=[bk])
                        p.op("s", lambda h, mt=mt, nn2=nn2, bki=bki: h.copy(
                            A(vt, mt * 1040 + nn2 * 8 * 65, [[12 * 1040, 128], [65, 8], [1, 64]]),
                            pv(bki).rearrange("p (a c) -> p a c", a=8)),
                            reads=[bk], writes=[B_("vt")])
            if "vm" in KP1:
                p.op("v", lambda h: h.memset(
                    A(vt, 64, [[12 * 1040, 128], [1040, 12], [65, 16]]), 1.0),
                    writes=[B_("vt")])
            if "qr" not in KP1:
                continue
            p.dma("g", lambda h: h.dma_start(
                A(wslot0, 0, [[8 * 1024, 128], [1024, 8], [1, 1024]]),
                A(wq_d.tensor, 0, [[1024, 128], [128 * 1024, 8], [1, 1024]])),
                reads=[B_("wslot0")], writes=[B_("wslot0")])
            xcb2 = B_("xct0")
            for ft in range(8):
                bki = ft % 2
                bk = bank[bki]
                for kc in range(8):
                    p.op("t", lambda h, ft=ft, kc=kc, bki=bki: h.matmul(
                        pv(bki), wslot0[:, kc, ft * 128:(ft + 1) * 128],
                        A(xct, kc * 512, [[8192, 128], [1, 512]]),
                        start=(kc == 0), stop=(kc == 7)),
                        reads=[B_("wslot0"), xcb2], writes=[bk])
                p.op("s", lambda h, ft=ft, bki=bki: h.activation(
                    q_sb[:, ft, :], pv(bki), AF.Identity,
                    bias=bqc[:, ft:ft + 1], scale=0.125),
                    reads=[bk, B_("bqc")], writes=[B_("q_sb")])
            p.dma("g", lambda h: h.dma_start(
                A(wslot1, 0, [[8 * 512, 128], [512, 8], [1, 512]]),
                A(wrp_d.tensor, 0, [[512, 128], [128 * 512, 8], [1, 512]])),
                reads=[B_("wslot1")], writes=[B_("wslot1")])
            for rt in range(4):
                bki = 2 + rt % 2
                bk = bank[bki]
                for kc in range(8):
                    p.op("t", lambda h, rt=rt, kc=kc, bki=bki: h.matmul(
                        pv(bki), wslot1[:, kc, rt * 128:(rt + 1) * 128],
                        A(xct, kc * 512, [[8192, 128], [1, 512]]),
                        start=(kc == 0), stop=(kc == 7)),
                        reads=[B_("wslot1"), xcb2], writes=[bk])
                p.op("s", lambda h, rt=rt, bki=bki: h.activation(
                    rr[:, rt, :], pv(bki), AF.Identity, bias=brc[:, rt:rt + 1]),
                    reads=[bk, B_("brc")], writes=[B_("rr")])

            if "p2" not in KPH:
                continue
            p.op("v", lambda h: h.memset(qpe[64:128, :], 0.0), writes=[B_("qpxl")])
            p.op("v", lambda h: h.memset(qpo[0:64, :], 0.0), writes=[B_("qpxl")])
            for hh in range(KH_):
                par = hh % 2
                qpad = (qpe, qpo)[par]
                p.dma("g", lambda h, hh=hh: h.dma_start(
                    rst[0:10, :], rr[32 * (hh % 4):32 * (hh % 4) + 10, hh // 4, :]),
                    reads=[B_("rr")], writes=[B_("rst")])
                a0 = 64 * (hh % 2)
                p.op("v", lambda h, hh=hh, qpad=qpad, a0=a0: h.tensor_copy(
                    qpad[a0:a0 + 64, :], q_sb[a0:a0 + 64, hh // 2, :]),
                    reads=[B_("q_sb")], writes=[B_("qpxl")])
                cdb = B_(f"cdram{par}")
                for ct in range(4):
                    cstt = (junk, cst2)[ct % 2]
                    cbf = (B_("junk"), B_("xt0"))[ct % 2]
                    for cc in range(2):
                        bki = 4 + cc
                        bk = bank[bki]
                        p.op("t", lambda h, ct=ct, cc=cc, bki=bki: h.matmul(
                            pv(bki), rst[0:10, ct * 128:(ct + 1) * 128],
                            brev[0:10, cc * 512:(cc + 1) * 512], start=True, stop=True),
                            reads=[B_("rst"), B_("brev")], writes=[bk])
                        p.op("s", lambda h, cstt=cstt, cc=cc, bki=bki: h.copy(
                            cstt[:, cc * 512:(cc + 1) * 512], pv(bki)),
                            reads=[bk], writes=[cbf])
                    p.dma("y", lambda h, par=par, ct=ct, cstt=cstt: h.dma_start(
                        A(cdr.tensor, par * 512 * CROW + ct * 128 * CROW + LPAD,
                          [[CROW, 128], [1, 1024]]), cstt[:, :]),
                        reads=[cbf], writes=[cdb])
                esbb = B_("esb")
                for qt in range(4):
                    off = par * 512 * CROW + qt * 128 * CROW + 511
                    p.dma("y", lambda h, qt=qt, off=off: h.dma_start(
                        esb[:, qt, :], A(cdr.tensor, off, [[CROW - 1, 128], [1, 1152]])),
                        reads=[cdb, B_(f"cpad{par}")], writes=[esbb])
                if KDBG == 1 and b == 0 and hh == 0:
                    p.dma("y", lambda h: h.dma_start(dbgE, esb[:, :, :]),
                          reads=[esbb], writes=[B_("dbgE")])
                p.op("t", lambda h: h.matmul(pbig[3][:, 0:260], zero128[:, :],
                     vt[:, 0, 0:260], start=True, stop=False),
                     reads=[B_("zero128"), B_("vt")], writes=[bank[6], bank[7]])
                wsbb = B_("wsb")

                def emit_qke(kb, hh, qpad):
                    sj = (kb // 2) % 2
                    bki2 = 2 * sj + (kb % 2)
                    bk = bank[bki2]
                    qts = [qt for qt in range(4) if 0 <= kb - qt <= 8]
                    p.op("t", lambda h, hh=hh, kb=kb, bki2=bki2, qpad=qpad: h.matmul(
                        pv(bki2), kp[:, hh // 2, kb * 128:(kb + 1) * 128],
                        qpad[:, :], start=True, stop=False),
                        reads=[B_("kp"), B_("qpxl")], writes=[bk])
                    if kb < 8:
                        p.op("t", lambda h, kb=kb, bki2=bki2: h.matmul(
                            pv(bki2), edge_b[0:1, kb * 128:(kb + 1) * 128],
                            ones_row[0:1, :], start=False, stop=False),
                            reads=[B_("edge_b"), B_("ones_row")], writes=[bk])
                    for i, qt in enumerate(qts):
                        p.op("t", lambda h, qt=qt, kb=kb, bki2=bki2,
                             last=(i == len(qts) - 1): h.matmul(
                            pv(bki2, qt * 128, (qt + 1) * 128),
                            esb[:, qt, (kb - qt) * 128:(kb - qt) * 128 + 128], ident[:, :],
                            start=False, stop=last),
                            reads=[esbb, B_("ident")], writes=[bk])

                def emit_exp(kb2):
                    sj = kb2 % 2
                    p.op("s", lambda h, kb2=kb2, sj=sj: h.activation(
                        A(wsb, (kb2 * 2 % 4) * 512, [[4 * 512, 128], [1, 1024]]),
                        pbig[sj][:, :], AF.Exp),
                        reads=[bank[2 * sj], bank[2 * sj + 1]], writes=[wsbb])

                def emit_av(kb, hh):
                    qts = [qt for qt in range(4) if 0 <= kb - qt <= 8]
                    for i, qt in enumerate(qts):
                        p.op("t", lambda h, hh=hh, kb=kb, qt=qt,
                             last=(kb == NKB - 1 and qt == 3): h.matmul(
                            pbig[3][:, qt * 65:qt * 65 + 65],
                            wsb[:, kb % 4, qt * 128:(qt + 1) * 128],
                            vt[:, kb, 65 * hh:65 * hh + 65],
                            start=False, stop=last),
                            reads=[wsbb, B_("vt")], writes=[bank[6], bank[7]])

                for kb2 in range(6):
                    emit_qke(2 * kb2, hh, qpad)
                    emit_qke(2 * kb2 + 1, hh, qpad)
                    emit_exp(kb2)
                    if kb2 >= 1:
                        emit_av(2 * (kb2 - 1), hh)
                        emit_av(2 * (kb2 - 1) + 1, hh)
                emit_av(10, hh)
                emit_av(11, hh)
                if KDBG == 1 and b == 0 and hh == 0:
                    p.dma("y", lambda h: h.dma_start(dbgW, wsb[:, :, :]),
                          reads=[wsbb], writes=[B_("dbgW")])
                if KDBG == 1 and b == 0 and hh == H - 1:
                    pass
                    p.op("s", lambda h: h.copy(junk[:, 0:260].bitcast(F32) if False else xt0[:, 0:260], pbig[3][:, 0:260]),
                         reads=[bank[6], bank[7]], writes=[B_("xt0")])
                    p.dma("y", lambda h: h.dma_start(dbgA, xt0[:, 0:260]),
                          reads=[B_("xt0")], writes=[B_("dbgA")])
                for qt in range(4):
                    p.op("s", lambda h, qt=qt: h.copy(
                        cols[:, 44 + qt:45 + qt], pbig[3][:, qt * 65 + 64:qt * 65 + 65]),
                        reads=[bank[6], bank[7]], writes=[B_("colD")])
                p.op("v", lambda h: h.reciprocal(cols[:, 32:36], cols[:, 44:48]),
                     reads=[B_("colD")], writes=[B_("colD")])
                for qt in range(4):
                    p.op("s", lambda h, qt=qt, hh=hh: h.activation(
                        asb[:, qt, 64 * hh:64 * hh + 64], pbig[3][:, qt * 65:qt * 65 + 64],
                        AF.Identity, scale=cols[:, 32 + qt:33 + qt]),
                        reads=[bank[6], bank[7], B_("colD")], writes=[B_("asb")])

            if KDBG == 1 and b == 0:
                p.dma("y", lambda h: h.dma_start(dbgS, asb[:, :, :]),
                      reads=[B_("asb")], writes=[B_("dbgS")])
            if KDBG == 2 and b == 0:
                p.dma("y", lambda h: h.dma_start(dbgR, x_tc[:, 0:4, :]),
                      reads=[B_("x_tc")], writes=[B_("dbgR")])
            if "p3" not in KPH:
                continue
            p.dma("g", lambda h: h.dma_start(
                A(wslot0, 0, [[8 * 1024, 128], [1024, 8], [1, 1024]]),
                A(wpj_d.tensor, 0, [[1024, 128], [128 * 1024, 8], [1, 1024]])),
                reads=[B_("wslot0")], writes=[B_("wslot0")])
            for hc in range(8):
                bki = 4 + hc % 2
                bk = bank[bki]
                for qt in range(4):
                    p.op("t", lambda h, hc=hc, qt=qt, bki=bki: h.matmul(
                        pv(bki, qt * 128, (qt + 1) * 128),
                        asb[:, qt, hc * 128:(hc + 1) * 128], ident[:, :],
                        start=True, stop=True),
                        reads=[B_("asb"), B_("ident")], writes=[bk])
                p.op("s", lambda h, hc=hc, bki=bki: h.copy(
                    act_[:, hc, :], pv(bki)), reads=[bk], writes=[B_("esb")])
            if KDBG == 2 and b == 0:
                p.dma("y", lambda h: h.dma_start(dbgT, act_[:, :, :]),
                      reads=[B_("esb")], writes=[B_("dbgT")])
            for jt in range(4):
                for cc in range(2):
                    bki = (jt * 2 + cc) % 2
                    bk = bank[bki]
                    for hc in range(8):
                        p.op("t", lambda h, jt=jt, cc=cc, hc=hc, bki=bki: h.matmul(
                            pv(bki), act_[:, hc, jt * 128:(jt + 1) * 128],
                            wslot0[:, hc, cc * 512:(cc + 1) * 512],
                            start=(hc == 0), stop=False),
                            reads=[B_("esb"), B_("wslot0")], writes=[bk])
                    p.op("t", lambda h, cc=cc, bki=bki: h.matmul(
                        pv(bki), ones_row[0:1, 0:128], bpj[0:1, cc * 512:(cc + 1) * 512],
                        start=False, stop=True),
                        reads=[B_("ones_row"), B_("bpj")], writes=[bk])
                    j = b * 4 + jt
                    p.op("s", lambda h, cc=cc, bki=bki: h.copy(
                        xt1[:, bki * 512:(bki + 1) * 512], pv(bki)),
                        reads=[bk], writes=[B_("xt1")])
                    p.op("v", lambda h, j=j, cc=cc, bki=bki: h.tensor_tensor(
                        x_tc[:, j, cc * 512:(cc + 1) * 512],
                        x_tc[:, j, cc * 512:(cc + 1) * 512],
                        xt1[:, bki * 512:(bki + 1) * 512], ALU.add),
                        reads=[B_("xt1"), B_("x_tc")], writes=[B_("x_tc")])
            if KDBG == 2 and b == 0:
                p.dma("y", lambda h: h.dma_start(dbgO, x_tc[:, 0:4, :]),
                      reads=[B_("x_tc")], writes=[B_("dbgO")])

    def plan_ffn(call):
        wsc = B_("wscal")
        cs = B_("colS")
        cu = B_("colU")
        for g in range(2):
            tiles = list(range(8 * g, 8 * g + 8))
            for ii, jt in enumerate(tiles):
                k = jt % 2
                xq_, xi_ = xql[k], xqi[k]
                xqlb, xqib = B_(f"xql{k}"), B_(f"xqi{k}")
                ci = k * 8
                cb = B_(f"colL{k}")
                ln_stats(x_tc[:, jt, :], B_("x_tc"), ci, cb)
                p.op("s", lambda h, jt=jt, xq_=xq_, ci=ci: h.activation(
                    xq_[:, :], x_tc[:, jt, :], AF.Identity,
                    bias=cols[:, ci + 1:ci + 2], scale=cols[:, ci:ci + 1]),
                    reads=[B_("x_tc"), cb], writes=[xqlb])
                p.op("v", lambda h, xq_=xq_, ci=ci: h.reduce_max(
                    cols[:, ci + 5:ci + 6], xq_[:, :], axis=AX.X,
                    apply_absolute_value=True), reads=[xqlb], writes=[cb])
                p.op("v", lambda h, ci=ci: h.tensor_scalar(cols[:, ci + 5:ci + 6],
                     cols[:, ci + 5:ci + 6], 1e-5, None, ALU.max), reads=[cb], writes=[cb])
                p.op("v", lambda h, ci=ci, ii=ii: h.tensor_scalar(cols[:, 40 + ii:41 + ii],
                     cols[:, ci + 5:ci + 6], 1.0 / 127.0, None, ALU.mult),
                     reads=[cb], writes=[cs])
                p.op("v", lambda h, ci=ci: h.reciprocal(cols[:, ci + 6:ci + 7],
                     cols[:, ci + 5:ci + 6]), reads=[cb], writes=[cb])
                p.op("v", lambda h, ci=ci: h.tensor_scalar(cols[:, ci + 6:ci + 7],
                     cols[:, ci + 6:ci + 7], 127.0, None, ALU.mult), reads=[cb], writes=[cb])
                for ch in range(2):
                    sl = slice(ch * 512, (ch + 1) * 512)
                    rnd_int8(xq_[:, sl], [xqlb], cols[:, ci + 6:ci + 7], cb,
                             xi_[:, sl], xqib, ch)
                for half in range(2):
                    bki = 4 + (ii + half) % 2
                    bk = bank[bki]
                    for kc in range(4):
                        kg = half * 4 + kc
                        p.op("t", lambda h, xi_=xi_, kg=kg, kc=kc, bki=bki: h.matmul(
                            pv(bki, kc * 128, (kc + 1) * 128),
                            xi_[:, kg * 128:(kg + 1) * 128], ident[:, :],
                            start=True, stop=True),
                            reads=[xqib, B_("ident")], writes=[bk])
                    p.op("s", lambda h, ii=ii, half=half, bki=bki: h.copy(
                        A(xqc, ii * 1024 + half * 512, [[8192, 128], [1, 512]]),
                        pv(bki)), reads=[bk], writes=[B_("xqc")])
                p.op("v", lambda h, ii=ii: h.tensor_tensor(cols[:, 48 + ii:49 + ii],
                     cols[:, 40 + ii:41 + ii], wscal[:, 0:1], ALU.mult),
                     reads=[cs, wsc], writes=[cs])
                p.op("v", lambda h, ii=ii: h.tensor_tensor(cols[:, 24 + ii:25 + ii],
                     cols[:, 40 + ii:41 + ii], wscal[:, 1:2], ALU.mult),
                     reads=[cs, wsc], writes=[cu])
            for oc in range(8):
                k = oc % 2
                wg_s = wfs[k]
                wu_s = wfs[2]
                wgb, wub = B_(f"wfs{k}"), B_("wfs2")
                p.dma("g", lambda h, oc=oc, wg_s=wg_s: h.dma_start(
                    A(wg_s, 0, [[8 * 512, 128], [512, 8], [1, 512]]),
                    A(wgQ.tensor, oc * 512, [[INNER, 128], [128 * INNER, 8], [1, 512]])),
                    reads=[wgb], writes=[wgb])
                p.dma("g", lambda h, oc=oc, wu_s=wu_s: h.dma_start(
                    A(wu_s, 0, [[8 * 512, 128], [512, 8], [1, 512]]),
                    A(wuQ.tensor, oc * 512, [[INNER, 128], [128 * INNER, 8], [1, 512]])),
                    reads=[wub], writes=[wub])
                for ii in range(8):
                    bg = ii % 2
                    bu = 2 + ii % 2
                    for kc in range(8):
                        p.op("t", lambda h, ii=ii, kc=kc, wg_s=wg_s, bg=bg: h.matmul(
                            pv(bg), A(xqc, ii * 1024 + kc * 128, [[8192, 128], [1, 128]]),
                            wg_s[:, kc, :], start=(kc == 0), stop=(kc == 7)),
                            reads=[B_("xqc"), wgb], writes=[bank[bg]])
                    for kc in range(8):
                        p.op("t", lambda h, ii=ii, kc=kc, wu_s=wu_s, bu=bu: h.matmul(
                            pv(bu), A(xqc, ii * 1024 + kc * 128, [[8192, 128], [1, 128]]),
                            wu_s[:, kc, :], start=(kc == 0), stop=(kc == 7)),
                            reads=[B_("xqc"), wub], writes=[bank[bu]])
                    gb_, ub_ = gbuf[ii % 2], gbuf[2]
                    gbb, ubb = B_(f"gbf{ii % 2}"), B_("gbf2")
                    p.op("s", lambda h, gb_=gb_, ii=ii, bg=bg: h.activation(
                        gb_[:, :], pv(bg), AF.Silu, scale=cols[:, 48 + ii:49 + ii]),
                        reads=[bank[bg], cs], writes=[gbb])
                    p.op("s", lambda h, ub_=ub_, ii=ii, bu=bu: h.activation(
                        ub_[:, :], pv(bu), AF.Identity, scale=cols[:, 24 + ii:25 + ii]),
                        reads=[bank[bu], cu], writes=[ubb])
                    p.op("v", lambda h, gb_=gb_, ub_=ub_, ii=ii, oc=oc: h.tensor_tensor(
                        hb[:, ii, oc * 512:(oc + 1) * 512], gb_[:, :], ub_[:, :], ALU.mult),
                        reads=[gbb, ubb], writes=[B_("hb")])
            for ii in range(8):
                hm = 16 + (ii % 2) * 4
                cb = B_(f"colH{ii % 2}")
                p.op("v", lambda h, ii=ii, hm=hm: h.reduce_max(cols[:, hm:hm + 1],
                     hb[:, ii, :], axis=AX.X, apply_absolute_value=True),
                     reads=[B_("hb")], writes=[cb])
                p.op("v", lambda h, hm=hm: h.tensor_scalar(cols[:, hm:hm + 1],
                     cols[:, hm:hm + 1], 1e-5, None, ALU.max), reads=[cb], writes=[cb])
                p.op("v", lambda h, hm=hm, ii=ii: h.tensor_scalar(cols[:, 56 + ii:57 + ii],
                     cols[:, hm:hm + 1], 1.0 / 127.0, None, ALU.mult), reads=[cb], writes=[cs])
                p.op("v", lambda h, ii=ii: h.tensor_tensor(cols[:, 56 + ii:57 + ii],
                     cols[:, 56 + ii:57 + ii], wscal[:, 2:3], ALU.mult),
                     reads=[cs, wsc], writes=[cs])
                p.op("v", lambda h, hm=hm: h.reciprocal(cols[:, hm + 1:hm + 2],
                     cols[:, hm:hm + 1]), reads=[cb], writes=[cb])
                p.op("v", lambda h, hm=hm: h.tensor_scalar(cols[:, hm + 1:hm + 2],
                     cols[:, hm + 1:hm + 2], 127.0, None, ALU.mult), reads=[cb], writes=[cb])
                for ch in range(8):
                    sl = slice(ch * 512, (ch + 1) * 512)
                    rnd_int8(hb[:, ii, sl], [B_("hb")], cols[:, hm + 1:hm + 2], cb,
                             hb[:, ii, sl], B_("hb"), ch)
            for half in range(2):
                p.barrier(dma=KDMA)
                for ii in range(4):
                    ti = half * 4 + ii
                    for qtr in range(8):
                        bki = 4 + (ii * 8 + qtr) % 2
                        bk = bank[bki]
                        for kc in range(4):
                            oc32 = qtr * 4 + kc
                            p.op("t", lambda h, ti=ti, oc32=oc32, kc=kc, bki=bki: h.matmul(
                                pv(bki, kc * 128, (kc + 1) * 128),
                                hb[:, ti, oc32 * 128:(oc32 + 1) * 128], ident[:, :],
                                start=True, stop=True),
                                reads=[B_("hb"), B_("ident")], writes=[bk])
                        p.op("s", lambda h, ii=ii, qtr=qtr, bki=bki: h.copy(
                            A(hot, (ii * 32 + qtr * 4) * 128, [[16384, 128], [1, 512]]),
                            pv(bki)), reads=[bk], writes=[B_("hot")])
                for cc in range(2):
                    for oc32 in range(32):
                        k = oc32 % 2
                        wd_s = wdfs[k]
                        wdb = B_(f"wdfs{k}")
                        p.dma("g", lambda h, cc=cc, oc32=oc32, wd_s=wd_s: h.dma_start(
                            wd_s[:, :],
                            A(wdQ.tensor, oc32 * 128 * E + cc * 512, [[E, 128], [1, 512]])),
                            reads=[wdb], writes=[wdb])
                        for ii in range(4):
                            p.op("t", lambda h, ii=ii, oc32=oc32, wd_s=wd_s: h.matmul(
                                pv(ii), A(hot, (ii * 32 + oc32) * 128, [[16384, 128], [1, 128]]),
                                wd_s[:, :], start=(oc32 == 0), stop=(oc32 == 31)),
                                reads=[B_("hot"), wdb], writes=[bank[ii]])
                    for ii in range(4):
                        ti = half * 4 + ii
                        jt = tiles[ti]
                        gb_ = gbuf[ii % 2]
                        gbb = B_(f"gbf{ii % 2}")
                        p.op("s", lambda h, ti=ti, ii=ii, gb_=gb_: h.activation(
                            gb_[:, :], pv(ii), AF.Identity, scale=cols[:, 56 + ti:57 + ti]),
                            reads=[bank[ii], cs], writes=[gbb])
                        p.op("v", lambda h, jt=jt, cc=cc, gb_=gb_: h.tensor_tensor(
                            x_tc[:, jt, cc * 512:(cc + 1) * 512],
                            x_tc[:, jt, cc * 512:(cc + 1) * 512], gb_[:, :], ALU.add),
                            reads=[gbb, B_("x_tc")], writes=[B_("x_tc")])
                p.barrier(dma=KDMA)

    def plan_attnb():
        KSL = [0, 2, 3, 1]
        kb_ = B_("kcv")
        def kv(et, s):
            return A(kcv, et * 4 + s, [[32, 128], [1, 1]])
        kall = lambda s: A(kcv, s, [[32, 128], [4, 8], [1, 1]])
        p.op("s", lambda h: h.activation(kall(0), gwc[:, :], AF.Sigmoid),
             reads=[B_("gwc")], writes=[kb_])
        p.op("v", lambda h: h.tensor_scalar(kall(1), kall(0), -1.0, 1.0, ALU.mult, ALU.add),
             reads=[kb_], writes=[kb_])
        p.op("v", lambda h: h.tensor_tensor(kall(2), kall(0), kall(1), ALU.mult),
             reads=[kb_], writes=[kb_])
        p.op("v", lambda h: h.tensor_tensor(kall(3), kall(2), kall(1), ALU.mult),
             reads=[kb_], writes=[kb_])
        p.op("v", lambda h: h.tensor_tensor(kall(1), kall(3), kall(1), ALU.mult),
             reads=[kb_], writes=[kb_])
        p.dma("g", lambda h: h.dma_start(bout[:, :], bout_d), writes=[B_("bout")])
        p.dma("g", lambda h: h.dma_start(
            A(wib, 0, [[8 * 1024, 128], [1024, 8], [1, 1024]]),
            A(winT_d.tensor, 0, [[1024, 128], [128 * 1024, 8], [1, 1024]])),
            writes=[B_("wib")])
        p.dma("g", lambda h: h.dma_start(
            A(wob, 0, [[8 * 1024, 128], [1024, 8], [1, 1024]]),
            A(woutT_d.tensor, 0, [[1024, 128], [128 * 1024, 8], [1, 1024]])),
            writes=[B_("wob")])
        for b in range(B):
            for tt in range(4):
                jt = b * 4 + tt
                k = tt % 2
                ci = k * 8
                cb = B_(f"colL{k}")
                ln_stats(x_tc[:, jt, :], B_("x_tc"), ci, cb)
                p.op("s", lambda h, jt=jt, ci=ci: h.activation(
                    xltb2[:, :], x_tc[:, jt, :], AF.Identity,
                    bias=cols[:, ci + 1:ci + 2], scale=cols[:, ci:ci + 1]),
                    reads=[B_("x_tc"), cb], writes=[B_("xltb2")])
                for half in range(2):
                    bki = 4 + (tt + half) % 2
                    bk = bank[bki]
                    for kc in range(4):
                        kg = half * 4 + kc
                        p.op("t", lambda h, kg=kg, kc=kc, bki=bki: h.matmul(
                            pv(bki, kc * 128, (kc + 1) * 128),
                            xltb2[:, kg * 128:(kg + 1) * 128], ident[:, :],
                            start=True, stop=True),
                            reads=[B_("xltb2"), B_("ident")], writes=[bk])
                    p.op("s", lambda h, tt=tt, half=half, bki=bki: h.copy(
                        A(xctb, half * 4 * 512 + tt * 128, [[8 * 512, 128], [512, 4], [1, 128]]),
                        pv(bki).rearrange("p (a c) -> p a c", a=4)),
                        reads=[bk], writes=[B_("xctb")])
            for et in range(8):
                bki = et % 2
                bk = bank[bki]
                for kc in range(8):
                    p.op("t", lambda h, et=et, kc=kc, bki=bki: h.matmul(
                        pv(bki), wib[:, kc, et * 128:(et + 1) * 128],
                        xctb[:, kc, :], start=(kc == 0), stop=(kc == 7)),
                        reads=[B_("wib"), B_("xctb")], writes=[bk])
                p.op("s", lambda h, et=et, b=b, bki=bki: h.activation(
                    ub[:, et, b * 512:(b + 1) * 512], pv(bki), AF.Silu,
                    bias=binc[:, et:et + 1]), reads=[bk, B_("binc")], writes=[B_("ub")])
        for b in range(B):
            for et in range(8):
                if b == 0:
                    p.op("v", lambda h, et=et: h.tensor_scalar(
                        vbb[:, et, :], ub[:, et, 0:512], kv(et, KSL[0]), None, ALU.mult),
                        reads=[B_("ub"), kb_], writes=[B_("vbb")])
                else:
                    p.op("v", lambda h, et=et, b=b: h.tensor_scalar(
                        cvt0[:, :], ub[:, et, 0:512], kv(et, KSL[b]), None, ALU.mult),
                        reads=[B_("ub"), kb_], writes=[B_("cvt0")])
                    for s_ in range(1, b):
                        p.op("v", lambda h, et=et, b=b, s_=s_: h.scalar_tensor_tensor(
                            cvt0[:, :], ub[:, et, s_ * 512:(s_ + 1) * 512],
                            kv(et, KSL[b - s_]), cvt0[:, :], ALU.mult, ALU.add),
                            reads=[B_("ub"), kb_, B_("cvt0")], writes=[B_("cvt0")])
                    p.op("v", lambda h, et=et, b=b: h.scalar_tensor_tensor(
                        vbb[:, et, :], ub[:, et, b * 512:(b + 1) * 512], kv(et, KSL[0]),
                        cvt0[:, :], ALU.mult, ALU.add),
                        reads=[B_("ub"), kb_, B_("cvt0")], writes=[B_("vbb")])
            for jt4 in range(4):
                for cc in range(2):
                    bki = (jt4 * 2 + cc) % 2
                    bk = bank[bki]
                    for dc in range(8):
                        p.op("t", lambda h, jt4=jt4, cc=cc, dc=dc, bki=bki: h.matmul(
                            pv(bki), vbb[:, dc, jt4 * 128:(jt4 + 1) * 128],
                            wob[:, dc, cc * 512:(cc + 1) * 512],
                            start=(dc == 0), stop=False),
                            reads=[B_("vbb"), B_("wob")], writes=[bk])
                    p.op("t", lambda h, cc=cc, bki=bki: h.matmul(
                        pv(bki), ones_row[0:1, 0:128], bout[0:1, cc * 512:(cc + 1) * 512],
                        start=False, stop=True),
                        reads=[B_("ones_row"), B_("bout")], writes=[bk])
                    ttb_ = tts[bki]
                    ttbb = B_(f"tts{bki}")
                    p.op("s", lambda h, ttb_=ttb_, bki=bki: h.activation(
                        ttb_[:, :], pv(bki), AF.Tanh), reads=[bk], writes=[ttbb])
                    j = b * 4 + jt4
                    p.op("v", lambda h, j=j, cc=cc, ttb_=ttb_: h.tensor_tensor(
                        x_tc[:, j, cc * 512:(cc + 1) * 512],
                        x_tc[:, j, cc * 512:(cc + 1) * 512], ttb_[:, :], ALU.add),
                        reads=[ttbb, B_("x_tc")], writes=[B_("x_tc")])

    import os as _os2
    KBAR = _os2.environ.get("KBAR", "1") == "1"
    KDMA = _os2.environ.get("KDMA", "1") == "1"
    if "wq" in stages:
        plan_wquant()
    if "attn" in stages:
        plan_attn()
    if KBAR:
        p.barrier(dma=KDMA)
    if "ffn1" in stages:
        plan_ffn(0)
    if KBAR:
        p.barrier(dma=KDMA)
    if "attnb" in stages:
        plan_attnb()
    if KBAR:
        p.barrier(dma=KDMA)
    if "ffn2" in stages:
        plan_ffn(1)

    p.dma("y", lambda h: h.dma_start(
        A(out_d.tensor, 0, [[1024, 128], [128 * 1024, 16], [1, 1024]]),
        A(x_tc, 0, [[16 * 1024, 128], [1024, 16], [1, 1024]])),
        reads=[B_("x_tc")], writes=[B_("out")])

    def final_waits(h):
        for name, val in p.dma_cnt.items():
            h.wait_ge(p.sems[name], val)
        return h.nop()
    p.q["y"].append(final_waits)

    p.assign_values()
    globals()["LAST_P"] = p
    with nc.Block() as block:
        @block.tensor
        def _(h):
            for f in p.q["t"]:
                f(h)

        @block.vector
        def _(h):
            for f in p.q["v"]:
                f(h)

        @block.scalar
        def _(h):
            for f in p.q["s"]:
                f(h)

        @block.sync
        def _(h):
            for f in p.q["y"]:
                f(h)

        @block.gpsimd
        def _(h):
            for f in p.q["g"]:
                f(h)

    stack.close()
    return nc


def _stage_inputs(kw):
    X = np.ascontiguousarray(np.asarray(kw["L0_attn_a_X"], np.float32))
    wq = np.ascontiguousarray(np.asarray(kw["L0_attn_a_Wq"], np.float32))
    wk = np.ascontiguousarray(np.asarray(kw["L0_attn_a_Wk"], np.float32))
    wv = np.ascontiguousarray(np.asarray(kw["L0_attn_a_Wv"], np.float32))
    wr = np.asarray(kw["L0_attn_a_Wr"], np.float32)
    wrp = np.zeros((E, 512), np.float32)
    for hh in range(H):
        wrp[:, 32 * hh:32 * hh + 10] = wr[:, 10 * hh:10 * hh + 10]
    wpj = np.ascontiguousarray(np.asarray(kw["L0_attn_a_Wproj"], np.float32))
    bq = np.asarray(kw["L0_attn_a_bq"], np.float32)
    bqc = np.ascontiguousarray(bq.reshape(8, 128).T)
    br = np.asarray(kw["L0_attn_a_br"], np.float32)
    brp = np.zeros((512,), np.float32)
    for hh in range(H):
        brp[32 * hh:32 * hh + 10] = br[10 * hh:10 * hh + 10]
    brc = np.ascontiguousarray(brp.reshape(4, 128).T)
    bpj = np.ascontiguousarray(np.asarray(kw["L0_attn_a_bproj"], np.float32).reshape(1, E))
    bndr = np.ascontiguousarray(np.asarray(kw["L0_attn_a_b_nd"], np.float32)[:, ::-1])
    winT = np.ascontiguousarray(np.asarray(kw["L0_attn_b_W_in"], np.float32).T)
    woutT = np.ascontiguousarray(np.asarray(kw["L0_attn_b_W_out"], np.float32).T)
    binc = np.ascontiguousarray(np.asarray(kw["L0_attn_b_b_in"], np.float32).reshape(8, 128).T)
    bout = np.ascontiguousarray(np.asarray(kw["L0_attn_b_b_out"], np.float32).reshape(1, E))
    gwc = np.ascontiguousarray(np.asarray(kw["L0_attn_b_weight"], np.float32).reshape(8, 128).T)
    wgT = np.ascontiguousarray(np.asarray(kw["L0_ffn_w_gate"], np.float32).T)
    wuT = np.ascontiguousarray(np.asarray(kw["L0_ffn_w_up"], np.float32).T)
    wdT = np.ascontiguousarray(np.asarray(kw["L0_ffn_w_down"], np.float32).T)
    idm = np.eye(128, dtype=bfd)

    maps = []
    for c in range(NCORE):
        lo = 512 * c - 1024
        hi = 512 * c + 512
        xs = np.zeros((B, SPAN, E), np.float32)
        s0 = max(0, lo)
        xs[:, s0 - lo:, :] = X[:, s0:hi, :]
        edge = np.zeros((1, 1024), np.float32)
        npad = max(0, -lo)
        edge[0, :npad] = NEG
        maps.append(dict(
            xsp=np.ascontiguousarray(xs), wq=wq, wk=wk, wv=wv, wrp=wrp, wpj=wpj,
            bqc=bqc, brc=brc, bpj=bpj, bndr=bndr, edge=np.ascontiguousarray(edge),
            winT=winT, woutT=woutT, binc=binc, bout=bout, gwc=gwc,
            wgT=wgT, wuT=wuT, wdT=wdT, idm=idm,
        ))
    return maps


def kernel(**kw):
    maps = _stage_inputs(kw)
    nc = build_program()
    res = run_bass_kernel_spmd(nc, maps, core_ids=list(range(NCORE)))
    out = np.empty((B, T, E), np.float32)
    for c in range(NCORE):
        o = np.asarray(res.results[c]["out"]).reshape(16, 128, E)
        for b in range(B):
            out[b, 512 * c:512 * (c + 1), :] = o[4 * b:4 * b + 4].reshape(512, E)
    return out


# version 05
# speedup vs baseline: 1.0107x; 1.0107x over previous
import numpy as np
import ml_dtypes
import contextlib
import concourse.bass as bass
import concourse.mybir as mybir
from concourse.bass_utils import run_bass_kernel_spmd

F32 = mybir.dt.float32
BF16 = mybir.dt.bfloat16
I32 = mybir.dt.int32
AF = mybir.ActivationFunctionType
ALU = mybir.AluOpType
AX = mybir.AxisListType
bfd = ml_dtypes.bfloat16

NCORE = 8
B = 4
T = 4096
E = 1024
H = 16
NB = 10
SPAN = 1536
NKB = 12
NEG = -1.0e9
INNER = 4096
CROW = 2048
LPAD = 512
OWN0 = 1024

ENGS = ("t", "v", "s", "y", "g")


class Cell:
    __slots__ = ("eng", "idx", "inc", "val")
    def __init__(self, eng, idx):
        self.eng = eng; self.idx = idx; self.inc = False; self.val = None


class DmaTok:
    __slots__ = ("sem", "val")
    def __init__(self, sem, val):
        self.sem = sem; self.val = val


class Buf:
    __slots__ = ("name", "w", "readers")
    def __init__(self, name):
        self.name = name; self.w = None; self.readers = []


class P:
    def __init__(self, nc):
        self.nc = nc
        self.log = {}
        self.q = {e: [] for e in ENGS}
        self.opidx = {e: 0 for e in ENGS}
        self.lastdrain = {e: -1 for e in ENGS}
        self.wm = {}
        self.dma_wm = {}
        self.cells = {e: [] for e in ENGS}
        self.sems = {}
        self.dma_cnt = {}
        self.dma_names = {"y": ["dy0", "dy1"], "g": ["dg0", "dg1"], "s": ["ds0", "ds1"]}
        self.dma_rr = {"y": 0, "g": 0, "s": 0}
        self.bufs = {}

    def B(self, name):
        if name not in self.bufs:
            self.bufs[name] = Buf(name)
        return self.bufs[name]

    def _await(self, eng, tk, raw=True):
        if isinstance(tk, DmaTok):
            key = (eng, tk.sem)
            if self.dma_wm.get(key, 0) >= tk.val:
                return
            self.dma_wm[key] = tk.val
            sem_name, val = tk.sem, tk.val
            self.log.setdefault(eng, []).append(("waitD", sem_name, val))
            self.q[eng].append(lambda h: h.wait_ge(self.sems[sem_name], val))
        elif tk.eng == eng:
            if eng == "t" or not raw:
                return
            if tk.idx > self.lastdrain[eng]:
                self.log.setdefault(eng, []).append(("drain",))
                self.q[eng].append(lambda h: h.drain())
                self.lastdrain[eng] = self.opidx[eng] - 1
        else:
            tk.inc = True
            key = (eng, tk.eng)
            if self.wm.get(key, -1) >= tk.idx:
                return
            self.wm[key] = tk.idx
            peng = tk.eng
            self.log.setdefault(eng, []).append(("waitE", peng, tk))
            self.q[eng].append(lambda h, tk=tk, peng=peng: h.wait_ge(self.sems[peng], tk.val))

    def _deps(self, eng, reads, writes):
        for tk in getattr(self, "fence", ()):
            self._await(eng, tk)
        for b in reads:
            if b.w is not None:
                self._await(eng, b.w)
        for b in writes:
            if b.w is not None:
                self._await(eng, b.w, raw=False)
            for tk in b.readers:
                self._await(eng, tk, raw=False)

    def op(self, eng, emitfn, reads=(), writes=()):
        self._deps(eng, reads, writes)
        cell = Cell(eng, self.opidx[eng])
        def run(h, cell=cell, emitfn=emitfn, self_=self):
            ins = emitfn(h)
            if cell.inc:
                ins.then_inc(self_.sems[cell.eng], 1)
        self.q[eng].append(run)
        self.cells[eng].append(cell)
        import traceback as _tb
        fr = _tb.extract_stack()[-2]
        self.log.setdefault(eng, []).append(("op", cell, f"{fr.name}:{fr.lineno}",
                                             [b.name for b in reads], [b.name for b in writes]))
        self.opidx[eng] += 1
        for b in writes:
            b.w = cell; b.readers = []
        for b in reads:
            b.readers.append(cell)
        return cell

    def dma(self, qeng, emitfn, reads=(), writes=()):
        self._deps(qeng, reads, writes)
        name = self.dma_names[qeng][self.dma_rr[qeng]]
        self.dma_rr[qeng] ^= 1
        val = self.dma_cnt.get(name, 0) + 16
        self.dma_cnt[name] = val
        tok = DmaTok(name, val)
        def run(h, emitfn=emitfn, name=name, self_=self):
            ins = emitfn(h)
            ins.then_inc(self_.sems[name], 16)
        self.q[qeng].append(run)
        self.log.setdefault(qeng, []).append(("dma", name, val,
                                              [b.name for b in reads], [b.name for b in writes]))
        self.opidx[qeng] += 1
        for b in writes:
            b.w = tok; b.readers = []
        for b in reads:
            b.readers.append(tok)
        return tok

    def barrier(self, dma=True):
        best = {}
        for b in self.bufs.values():
            toks = list(b.readers)
            if b.w is not None:
                toks.append(b.w)
            for tk in toks:
                if isinstance(tk, DmaTok):
                    key = ("d", tk.sem)
                    cur = best.get(key)
                    if cur is None or tk.val > cur.val:
                        best[key] = tk
                else:
                    key = ("e", tk.eng)
                    cur = best.get(key)
                    if cur is None or tk.idx > cur.idx:
                        best[key] = tk
        for sem, val in self.dma_cnt.items():
            key = ("d", sem)
            cur = best.get(key)
            if cur is None or val > cur.val:
                best[key] = DmaTok(sem, val)
        self.fence = list(best.values())

    def assign_values(self):
        for e in ENGS:
            v = 0
            for c in self.cells[e]:
                if c.inc:
                    v += 1
                    c.val = v


def build_program(stages=("wq", "attn", "ffn1", "attnb", "ffn2")):
    nc = bass.Bass("TRN2", target_bir_lowering=False, detect_race_conditions=False,
                   dynamic_dma_scratch_size=16384)

    xsp = nc.dram_tensor("xsp", [B, SPAN, E], F32, kind="ExternalInput").ap()
    wq_d = nc.dram_tensor("wq", [E, E], F32, kind="ExternalInput").ap()
    wk_d = nc.dram_tensor("wk", [E, E], F32, kind="ExternalInput").ap()
    wv_d = nc.dram_tensor("wv", [E, E], F32, kind="ExternalInput").ap()
    wrp_d = nc.dram_tensor("wrp", [E, 512], F32, kind="ExternalInput").ap()
    wpj_d = nc.dram_tensor("wpj", [E, E], F32, kind="ExternalInput").ap()
    bqc_d = nc.dram_tensor("bqc", [128, 8], F32, kind="ExternalInput").ap()
    brc_d = nc.dram_tensor("brc", [128, 4], F32, kind="ExternalInput").ap()
    bpj_d = nc.dram_tensor("bpj", [1, E], F32, kind="ExternalInput").ap()
    bndr_d = nc.dram_tensor("bndr", [NB, 1024], F32, kind="ExternalInput").ap()
    edge_d = nc.dram_tensor("edge", [1, 1024], F32, kind="ExternalInput").ap()
    winT_d = nc.dram_tensor("winT", [E, E], F32, kind="ExternalInput").ap()
    woutT_d = nc.dram_tensor("woutT", [E, E], F32, kind="ExternalInput").ap()
    binc_d = nc.dram_tensor("binc", [128, 8], F32, kind="ExternalInput").ap()
    bout_d = nc.dram_tensor("bout", [1, E], F32, kind="ExternalInput").ap()
    gwc_d = nc.dram_tensor("gwc", [128, 8], F32, kind="ExternalInput").ap()
    wgT_d = nc.dram_tensor("wgT", [E, INNER], F32, kind="ExternalInput").ap()
    wuT_d = nc.dram_tensor("wuT", [E, INNER], F32, kind="ExternalInput").ap()
    wdT_d = nc.dram_tensor("wdT", [INNER, E], F32, kind="ExternalInput").ap()
    id_d = nc.dram_tensor("idm", [128, 128], BF16, kind="ExternalInput").ap()
    out_d = nc.dram_tensor("out", [2048, E], F32, kind="ExternalOutput").ap()
    import os as _osd
    KDBG = int(_osd.environ.get("KDBG", "0"))
    if KDBG == 1:
        dbgE = nc.dram_tensor("dbgE", [128, 4 * 1152], BF16, kind="ExternalOutput").ap()
        dbgW = nc.dram_tensor("dbgW", [128, 4 * 512], BF16, kind="ExternalOutput").ap()
        dbgA = nc.dram_tensor("dbgA", [128, 4 * 65], F32, kind="ExternalOutput").ap()
        dbgS = nc.dram_tensor("dbgS", [128, 4 * 1024], BF16, kind="ExternalOutput").ap()
    if KDBG == 2:
        dbgR = nc.dram_tensor("dbgR", [128, 4 * 1024], F32, kind="ExternalOutput").ap()
        dbgT = nc.dram_tensor("dbgT", [128, 8 * 512], BF16, kind="ExternalOutput").ap()
        dbgO = nc.dram_tensor("dbgO", [128, 4 * 1024], F32, kind="ExternalOutput").ap()

    cdr = nc.dram_tensor("cdr", [2 * 512 * CROW], BF16, kind="Internal").ap()
    wgQ = nc.dram_tensor("wgQ", [E * INNER], BF16, kind="Internal").ap()
    wuQ = nc.dram_tensor("wuQ", [E * INNER], BF16, kind="Internal").ap()
    wdQ = nc.dram_tensor("wdQ", [INNER * E], BF16, kind="Internal").ap()
    scl_d = nc.dram_tensor("scl", [1, 4], F32, kind="Internal").ap()

    p = P(nc)
    B_ = p.B
    A = bass.AP

    PB = ((nc.sbuf_base + 63) // 64) * 64

    def at(name, shape, dt, off):
        return nc.alloc_sbuf_tensor_at(name, shape, dt, offset=PB + off)

    x_tc = at("x_tc", [128, 16, 1024], F32, 0)
    junk = at("junk", [128, 1024], BF16, 65536)
    ident = at("ident", [128, 128], BF16, 67584)
    zero128 = at("zero128", [128, 128], BF16, 67840)
    ones_row = at("ones_row", [1, 512], BF16, 68096)
    cols = at("cols", [128, 64], F32, 69120)
    bqc = at("bqc_s", [128, 8], F32, 69376)
    brc = at("brc_s", [128, 4], F32, 69408)
    binc = at("binc_s", [128, 8], F32, 69440)
    gwc = at("gwc_s", [128, 8], F32, 69472)
    kcv = at("kcv", [128, 32], F32, 69504)
    wscal = at("wscal", [128, 8], F32, 69632)

    OV = 69696
    qwf = [at(f"qwf{i}", [128, 512], F32, OV + i * 4096) for i in range(2)]
    qwq = [at(f"qwq{i}", [128, 512], BF16, OV + 8192 + i * 1024) for i in range(2)]
    qwt = [at(f"qwt{i}", [128, 512], BF16, OV + 10240 + i * 1024) for i in range(2)]

    xt0 = at("xt0", [128, 1024], F32, OV + 0)
    cst2 = at("cst2", [128, 1024], BF16, OV + 0)
    xt1 = at("xt1", [128, 1024], F32, OV + 4096)
    xlt = at("xlt", [128, 1024], BF16, OV + 8192)
    qpe = at("qpe", [128, 512], BF16, OV + 8192)
    qpo = at("qpo", [128, 512], BF16, OV + 9216)
    xct = at("xct", [128, 2 * 8 * 512], BF16, OV + 10240)
    wslot0 = at("wslot0", [128, 8, 1024], BF16, OV + 26624)
    wslot1 = at("wslot1", [128, 8, 512], BF16, OV + 43008)
    kp = at("kp", [128, 8, 1536], BF16, OV + 51200)
    vt = at("vt", [128, 12, 1040], BF16, OV + 75776)
    q_sb = at("q_sb", [128, 8, 512], BF16, OV + 100736)
    rr = at("rr", [128, 4, 512], BF16, OV + 108928)
    rst = at("rst", [128, 512], BF16, OV + 113024)
    wsb = at("wsb", [128, 4, 512], BF16, OV + 114048)
    esb = at("esb", [128, 4, 1152], BF16, OV + 118144)
    act_ = at("act_", [128, 8, 512], BF16, OV + 118144)
    asb = at("asb", [128, 4, 1024], BF16, OV + 127360)
    brev = at("brev", [128, 1024], BF16, OV + 135552)
    edge_b = at("edge_b", [1, 1024], BF16, OV + 137600)
    bpj = at("bpj_s", [1, 1024], BF16, OV + 139648)

    hb = at("hb", [128, 8, 4096], BF16, OV + 0)
    xqc = at("xqc", [128, 8 * 1024], BF16, OV + 65536)
    xql = [at(f"xql{i}", [128, 1024], F32, OV + 81920 + i * 4096) for i in range(2)]
    xqi = [at(f"xqi{i}", [128, 1024], BF16, OV + 90112 + i * 2048) for i in range(2)]
    rti = [at(f"rti{i}", [128, 512], I32, OV + 94208 + i * 2048) for i in range(2)]
    hot = at("hot", [128, 128, 128], BF16, OV + 65536)
    rtf = [at(f"rtf{i}", [128, 512], F32, OV + 98304 + i * 2048) for i in range(2)]
    rtg = [at(f"rtg{i}", [128, 512], BF16, OV + 102400 + i * 1024) for i in range(2)]
    gbuf = [at(f"gbf{i}", [128, 512], F32, OV + 104448 + i * 2048) for i in range(3)]
    wfs = [at(f"wfs{i}", [128, 8, 512], BF16, OV + 110592 + i * 8192) for i in range(3)]
    wdfs = [at(f"wdfs{i}", [128, 512], BF16, OV + 135168 + i * 1024) for i in range(2)]

    ub = at("ub", [128, 8, 2048], BF16, OV + 0)
    vbb = at("vbb", [128, 8, 512], BF16, OV + 32768)
    xctb = at("xctb", [128, 8, 512], BF16, OV + 40960)
    xltb2 = at("xltb2", [128, 1024], BF16, OV + 49152)
    wib = at("wib", [128, 8, 1024], BF16, OV + 51200)
    wob = at("wob", [128, 8, 1024], BF16, OV + 67584)
    cvt0 = at("cvt0", [128, 512], F32, OV + 83968)
    tts = [at(f"tts{i}", [128, 512], F32, OV + 86016 + i * 2048) for i in range(2)]
    bout = at("bout_s", [1, 1024], BF16, OV + 90112)

    stack = contextlib.ExitStack()
    pbig = [stack.enter_context(nc.psum_tensor(f"pbig{i}", [128, 1024], F32)) for i in range(4)]
    def pv(i, a=0, b2=512):
        return pbig[i // 2][:, (i % 2) * 512 + a:(i % 2) * 512 + b2]
    bank = [B_(f"bank{i}") for i in range(8)]

    for s_ in ("t", "v", "s", "y", "g", "dy0", "dy1", "dg0", "dg1", "ds0", "ds1"):
        p.sems[s_] = stack.enter_context(nc.semaphore("sem_" + s_))

    import os as _os
    KI = _os.environ.get("KI", "xin,smalls,pads").split(",")
    for _b in range(B if "xin" in KI else 0):
        p.dma("y", lambda h, _b=_b: h.dma_start(
            A(x_tc, _b * 4 * 1024, [[16 * 1024, 128], [1024, 4], [1, 1024]]),
            A(xsp.tensor, _b * SPAN * 1024 + OWN0 * 1024,
              [[1024, 128], [128 * 1024, 4], [1, 1024]])),
            writes=[B_("x_tc")])
    p.dma("y", lambda h: h.dma_start(ident[:, :], id_d), writes=[B_("ident")])
    if "smalls" in KI:
        p.dma("g", lambda h: h.dma_start(brev[0:NB, :], bndr_d), writes=[B_("brev")])
        p.dma("g", lambda h: h.dma_start(edge_b[:, :], edge_d), writes=[B_("edge_b")])
        p.dma("g", lambda h: h.dma_start(bpj[:, :], bpj_d), writes=[B_("bpj")])
        p.dma("y", lambda h: h.dma_start(bqc[:, :], bqc_d), writes=[B_("bqc")])
        p.dma("y", lambda h: h.dma_start(brc[:, :], brc_d), writes=[B_("brc")])
        p.dma("y", lambda h: h.dma_start(binc[:, :], binc_d), writes=[B_("binc")])
        p.dma("y", lambda h: h.dma_start(gwc[:, :], gwc_d), writes=[B_("gwc")])
    p.op("v", lambda h: h.memset(ones_row[:, :], 1.0), writes=[B_("ones_row")])
    p.op("v", lambda h: h.memset(zero128[:, :], 0.0), writes=[B_("zero128")])
    if "smalls" in KI:
        p.op("v", lambda h: h.tensor_scalar(bqc[:, :], bqc[:, :], 0.125, None, ALU.mult),
             reads=[B_("bqc")], writes=[B_("bqc")])
    p.op("v", lambda h: h.memset(junk[:, 0:512], NEG), writes=[B_("junk")])
    for par in range(2 if "pads" in KI else 0):
        for side in range(2):
            for rk in range(4):
                off = par * 512 * CROW + rk * 128 * CROW + side * 1536
                p.dma("g", lambda h, off=off: h.dma_start(
                    A(cdr.tensor, off, [[CROW, 128], [1, 512]]), junk[:, 0:512]),
                    reads=[B_("junk")], writes=[B_(f"cpad{par}")])

    def ln_stats(srcap, srcbuf, ci, colbuf):
        p.op("v", lambda h: h.reduce_sum(cols[:, ci + 2:ci + 3], srcap, axis=AX.X),
             reads=[srcbuf], writes=[colbuf])
        p.op("s", lambda h: h.activation(junk[:, :], srcap, AF.Square,
                                         accum_out=cols[:, ci + 3:ci + 4]),
             reads=[srcbuf], writes=[B_("junk"), colbuf])
        p.op("v", lambda h: h.tensor_scalar(cols[:, ci + 2:ci + 3], cols[:, ci + 2:ci + 3],
             1.0 / 1024, None, ALU.mult), reads=[colbuf], writes=[colbuf])
        p.op("v", lambda h: h.tensor_scalar(cols[:, ci + 3:ci + 4], cols[:, ci + 3:ci + 4],
             1.0 / 1024, None, ALU.mult), reads=[colbuf], writes=[colbuf])
        p.op("v", lambda h: h.tensor_tensor(cols[:, ci + 4:ci + 5], cols[:, ci + 2:ci + 3],
             cols[:, ci + 2:ci + 3], ALU.mult), reads=[colbuf], writes=[colbuf])
        p.op("v", lambda h: h.tensor_tensor(cols[:, ci + 3:ci + 4], cols[:, ci + 3:ci + 4],
             cols[:, ci + 4:ci + 5], ALU.subtract), reads=[colbuf], writes=[colbuf])
        p.op("v", lambda h: h.tensor_scalar(cols[:, ci + 3:ci + 4], cols[:, ci + 3:ci + 4],
             1e-5, None, ALU.add), reads=[colbuf], writes=[colbuf])
        p.op("s", lambda h: h.activation(cols[:, ci + 4:ci + 5], cols[:, ci + 3:ci + 4],
             AF.Ln), reads=[colbuf], writes=[colbuf])
        p.op("s", lambda h: h.activation(cols[:, ci:ci + 1], cols[:, ci + 4:ci + 5],
             AF.Exp, scale=-0.5), reads=[colbuf], writes=[colbuf])
        p.op("v", lambda h: h.tensor_tensor(cols[:, ci + 1:ci + 2], cols[:, ci + 2:ci + 3],
             cols[:, ci:ci + 1], ALU.mult), reads=[colbuf], writes=[colbuf])
        p.op("v", lambda h: h.tensor_scalar(cols[:, ci + 1:ci + 2], cols[:, ci + 1:ci + 2],
             -1.0, None, ALU.mult), reads=[colbuf], writes=[colbuf])

    def rnd_int8(srcap, srcbufs, scol, colbuf, dstap, dstbuf, ch):
        k = ch % 2
        ib, fb2, gb_ = rti[k], rtf[k], rtg[k]
        ibb, fbb, gbb = B_(f"rti{k}"), B_(f"rtf{k}"), B_(f"rtg{k}")
        p.op("v", lambda h: h.tensor_scalar(ib[:, :], srcap, scol, 384.0, ALU.mult, ALU.add),
             reads=list(srcbufs) + [colbuf], writes=[ibb])
        p.op("v", lambda h: h.scalar_tensor_tensor(fb2[:, :], srcap, scol, ib[:, :],
             ALU.mult, ALU.subtract), reads=list(srcbufs) + [colbuf, ibb], writes=[fbb])
        p.op("v", lambda h: h.tensor_single_scalar(gb_[:, :], fb2[:, :], -383.5, ALU.is_gt),
             reads=[fbb], writes=[gbb])
        p.op("v", lambda h: h.scalar_tensor_tensor(dstap, ib[:, :], -384.0, gb_[:, :],
             ALU.add, ALU.add), reads=[ibb, gbb], writes=[dstbuf])

    def plan_wquant():
        cw = B_("colW")
        mats = [(wgT_d, wgQ, 0), (wuT_d, wuQ, 1), (wdT_d, wdQ, 2)]
        csz = 128 * 512
        nch = (E * INNER) // csz
        p.op("v", lambda h: h.memset(kcv[:, 30:31], 1.0), writes=[B_("kcvo")])
        for (src, dst, mi) in mats:
            acc = 36 + mi
            p.op("v", lambda h, acc=acc: h.memset(cols[:, acc:acc + 1], 0.0), writes=[cw])
            for ch in range(nch):
                k = ch % 2
                fb = B_(("xt0", "xt1")[k])
                p.dma("g", lambda h, src=src, ch=ch, k=k: h.dma_start(
                    qwf[k][:, :], A(src.tensor, ch * csz, [[512, 128], [1, 512]])),
                    writes=[fb])
                p.op("v", lambda h, k=k: h.reduce_sum(cols[:, 39:40], qwf[k][:, :],
                     axis=AX.X, apply_absolute_value=True), reads=[fb], writes=[cw])
                p.op("v", lambda h, acc=acc: h.tensor_tensor(cols[:, acc:acc + 1],
                     cols[:, acc:acc + 1], cols[:, 39:40], ALU.add), reads=[cw], writes=[cw])
            bk = bank[7]
            p.op("t", lambda h, acc=acc: h.matmul(pv(7, 0, 1)[0:1, :], cols[:, acc:acc + 1],
                 kcv[:, 30:31], start=True, stop=True), reads=[cw, B_("kcvo")], writes=[bk])
            p.op("s", lambda h, acc=acc: h.copy(cols[0:1, 39:40], pv(7, 0, 1)[0:1, :]),
                 reads=[bk], writes=[cw])
            p.op("v", lambda h, acc=acc: h.tensor_scalar(cols[0:1, acc:acc + 1],
                 cols[0:1, 39:40], 1.0 / (E * INNER), None, ALU.mult),
                 reads=[cw], writes=[cw])
            p.op("v", lambda h, acc=acc: h.tensor_scalar(cols[0:1, acc:acc + 1],
                 cols[0:1, acc:acc + 1], 1e-5, None, ALU.max), reads=[cw], writes=[cw])
        p.dma("y", lambda h: h.dma_start(A(scl_d.tensor, 0, [[1, 1], [1, 3]]),
              cols[0:1, 36:39]), reads=[cw], writes=[B_("scl")])
        p.dma("y", lambda h: h.dma_start(A(wscal, 0, [[8, 128], [1, 3]]),
              A(scl_d.tensor, 0, [[0, 128], [1, 3]])),
              reads=[B_("scl")], writes=[B_("wscal")])
        wsc = B_("wscal")
        for mi in range(3):
            p.op("v", lambda h, mi=mi: h.reciprocal(wscal[:, 3 + mi:4 + mi],
                 wscal[:, mi:mi + 1]), reads=[wsc], writes=[wsc])
        p.op("v", lambda h: h.tensor_scalar(wscal[:, 6:7], wscal[:, 3:4], -1.0, None,
             ALU.mult), reads=[wsc], writes=[wsc])
        p.op("v", lambda h: h.tensor_scalar(wscal[:, 7:8], wscal[:, 4:5], -1.0, None,
             ALU.mult), reads=[wsc], writes=[wsc])
        p.op("v", lambda h: h.tensor_scalar(kcv[:, 29:30], wscal[:, 5:6], -1.0, None,
             ALU.mult), reads=[wsc], writes=[B_("kcvq")])
        for (src, dst, mi) in mats:
            scol = wscal[:, 3 + mi:4 + mi]
            nscol = kcv[:, 29:30] if mi == 2 else wscal[:, 6 + mi:7 + mi]
            nsb = B_("kcvq") if mi == 2 else wsc
            for ch in range(nch):
                k = ch % 2
                fb, qb, tb = B_(("xt0", "xt1")[k]), B_("qpxl"), B_("xct0")
                p.dma("g", lambda h, src=src, ch=ch, k=k: h.dma_start(
                    qwf[k][:, :], A(src.tensor, ch * csz, [[512, 128], [1, 512]])),
                    writes=[fb])
                p.op("v", lambda h, k=k, scol=scol: h.tensor_scalar(
                    qwq[k][:, :], qwf[k][:, :], scol, 0.5, ALU.mult, ALU.is_gt),
                    reads=[fb, wsc], writes=[qb])
                p.op("v", lambda h, k=k, nscol=nscol, nsb=nsb: h.tensor_scalar(
                    qwt[k][:, :], qwf[k][:, :], nscol, 0.5, ALU.mult, ALU.is_gt),
                    reads=[fb, nsb], writes=[tb])
                p.op("v", lambda h, k=k: h.tensor_tensor(
                    qwq[k][:, :], qwq[k][:, :], qwt[k][:, :], ALU.subtract),
                    reads=[qb, tb], writes=[qb])
                p.dma("g", lambda h, dst=dst, ch=ch, k=k: h.dma_start(
                    A(dst.tensor, ch * csz, [[512, 128], [1, 512]]), qwq[k][:, :]),
                    reads=[qb], writes=[B_(f"wQd{mi}")])

    def plan_attn():
        import os
        KB_ = int(os.environ.get("KBATCH", str(B)))
        KH_ = int(os.environ.get("KHEAD", str(H)))
        KPH = os.environ.get("KPH", "p1,p2,p3").split(",")
        KP1 = os.environ.get("KP1", "w,x,lns,lna,tr,xc,k,v,vm,qr").split(",")
        for b in range(KB_):
            if "w" in KP1:
                p.dma("g", lambda h: h.dma_start(
                    A(wslot0, 0, [[8 * 1024, 128], [1024, 8], [1, 1024]]),
                    A(wk_d.tensor, 0, [[1024, 128], [128 * 1024, 8], [1, 1024]])),
                    reads=[B_("wslot0")], writes=[B_("wslot0")])
            for nn in range(3):
                cx = nn % 2
                xcb = B_(f"xct{cx}")
                for stl in range(4):
                    st = nn * 4 + stl
                    k = st % 2
                    xt = (xt0, xt1)[k]
                    xtb = B_(f"xt{k}")
                    ci = k * 8
                    cb = B_(f"colL{k}")
                    p.dma("y", lambda h, b=b, st=st, xt=xt: h.dma_start(
                        xt[:, :], xsp[b, st * 128:(st + 1) * 128, :]), writes=[xtb])
                    if "lns" not in KP1:
                        continue
                    ln_stats(xt[:, :], xtb, ci, cb)
                    if "lna" not in KP1:
                        continue
                    p.op("s", lambda h, xt=xt, ci=ci: h.activation(
                        xlt[:, :], xt[:, :], AF.Identity,
                        bias=cols[:, ci + 1:ci + 2], scale=cols[:, ci:ci + 1]),
                        reads=[xtb, cb], writes=[B_("qpxl")])
                    if "tr" not in KP1:
                        continue
                    for half in range(2):
                        bki = 4 + (stl + half) % 2
                        bk = bank[bki]
                        for kc in range(4):
                            kg = half * 4 + kc
                            p.op("t", lambda h, kg=kg, kc=kc, bki=bki: h.matmul(
                                pv(bki, kc * 128, (kc + 1) * 128),
                                xlt[:, kg * 128:(kg + 1) * 128], ident[:, :],
                                start=True, stop=True),
                                reads=[B_("qpxl"), B_("ident")], writes=[bk])
                        p.op("s", lambda h, cx=cx, stl=stl, half=half, bki=bki: h.copy(
                            A(xct, cx * 8 * 512 + half * 4 * 512 + stl * 128,
                              [[2 * 8 * 512, 128], [512, 4], [1, 128]]),
                            pv(bki).rearrange("p (a c) -> p a c", a=4)),
                            reads=[bk], writes=[xcb])
                for ft in range(8 if "k" in KP1 else 0):
                    bki = ft % 2
                    bk = bank[bki]
                    for kc in range(8):
                        p.op("t", lambda h, ft=ft, kc=kc, bki=bki, cx=cx: h.matmul(
                            pv(bki), wslot0[:, kc, ft * 128:(ft + 1) * 128],
                            A(xct, cx * 8 * 512 + kc * 512, [[8192, 128], [1, 512]]),
                            start=(kc == 0), stop=(kc == 7)),
                            reads=[B_("wslot0"), xcb], writes=[bk])
                    p.op("s", lambda h, ft=ft, nn=nn, bki=bki: h.copy(
                        kp[:, ft, nn * 512:(nn + 1) * 512], pv(bki)),
                        reads=[bk], writes=[B_("kp")])
                for nn2 in range(2 if "v" in KP1 else 0):
                    p.dma("g", lambda h, nn2=nn2: h.dma_start(
                        A(wslot1, 0, [[8 * 512, 128], [512, 8], [1, 512]]),
                        A(wv_d.tensor, nn2 * 512, [[1024, 128], [128 * 1024, 8], [1, 512]])),
                        reads=[B_("wslot1")], writes=[B_("wslot1")])
                    for mtl in range(4):
                        mt = nn * 4 + mtl
                        bki = 2 + mtl % 2
                        bk = bank[bki]
                        for kc in range(8):
                            p.op("t", lambda h, mtl=mtl, kc=kc, bki=bki, cx=cx: h.matmul(
                                pv(bki),
                                A(xct, cx * 8 * 512 + kc * 512 + mtl * 128, [[8192, 128], [1, 128]]),
                                wslot1[:, kc, :], start=(kc == 0), stop=(kc == 7)),
                                reads=[B_("wslot1"), xcb], writes=[bk])
                        p.op("s", lambda h, mt=mt, nn2=nn2, bki=bki: h.copy(
                            A(vt, mt * 1040 + nn2 * 8 * 65, [[12 * 1040, 128], [65, 8], [1, 64]]),
                            pv(bki).rearrange("p (a c) -> p a c", a=8)),
                            reads=[bk], writes=[B_("vt")])
            if "vm" in KP1:
                p.op("v", lambda h: h.memset(
                    A(vt, 64, [[12 * 1040, 128], [1040, 12], [65, 16]]), 1.0),
                    writes=[B_("vt")])
            if "qr" not in KP1:
                continue
            p.dma("g", lambda h: h.dma_start(
                A(wslot0, 0, [[8 * 1024, 128], [1024, 8], [1, 1024]]),
                A(wq_d.tensor, 0, [[1024, 128], [128 * 1024, 8], [1, 1024]])),
                reads=[B_("wslot0")], writes=[B_("wslot0")])
            xcb2 = B_("xct0")
            for ft in range(8):
                bki = ft % 2
                bk = bank[bki]
                for kc in range(8):
                    p.op("t", lambda h, ft=ft, kc=kc, bki=bki: h.matmul(
                        pv(bki), wslot0[:, kc, ft * 128:(ft + 1) * 128],
                        A(xct, kc * 512, [[8192, 128], [1, 512]]),
                        start=(kc == 0), stop=(kc == 7)),
                        reads=[B_("wslot0"), xcb2], writes=[bk])
                p.op("s", lambda h, ft=ft, bki=bki: h.activation(
                    q_sb[:, ft, :], pv(bki), AF.Identity,
                    bias=bqc[:, ft:ft + 1], scale=0.125),
                    reads=[bk, B_("bqc")], writes=[B_("q_sb")])
            p.dma("g", lambda h: h.dma_start(
                A(wslot1, 0, [[8 * 512, 128], [512, 8], [1, 512]]),
                A(wrp_d.tensor, 0, [[512, 128], [128 * 512, 8], [1, 512]])),
                reads=[B_("wslot1")], writes=[B_("wslot1")])
            for rt in range(4):
                bki = 2 + rt % 2
                bk = bank[bki]
                for kc in range(8):
                    p.op("t", lambda h, rt=rt, kc=kc, bki=bki: h.matmul(
                        pv(bki), wslot1[:, kc, rt * 128:(rt + 1) * 128],
                        A(xct, kc * 512, [[8192, 128], [1, 512]]),
                        start=(kc == 0), stop=(kc == 7)),
                        reads=[B_("wslot1"), xcb2], writes=[bk])
                p.op("s", lambda h, rt=rt, bki=bki: h.activation(
                    rr[:, rt, :], pv(bki), AF.Identity, bias=brc[:, rt:rt + 1]),
                    reads=[bk, B_("brc")], writes=[B_("rr")])

            if "p2" not in KPH:
                continue
            p.op("v", lambda h: h.memset(qpe[64:128, :], 0.0), writes=[B_("qpxl")])
            p.op("v", lambda h: h.memset(qpo[0:64, :], 0.0), writes=[B_("qpxl")])
            for hh in range(KH_):
                par = hh % 2
                qpad = (qpe, qpo)[par]
                p.dma("g", lambda h, hh=hh: h.dma_start(
                    rst[0:10, :], rr[32 * (hh % 4):32 * (hh % 4) + 10, hh // 4, :]),
                    reads=[B_("rr")], writes=[B_("rst")])
                a0 = 64 * (hh % 2)
                p.op("v", lambda h, hh=hh, qpad=qpad, a0=a0: h.tensor_copy(
                    qpad[a0:a0 + 64, :], q_sb[a0:a0 + 64, hh // 2, :]),
                    reads=[B_("q_sb")], writes=[B_("qpxl")])
                cdb = B_(f"cdram{par}")
                for ct in range(4):
                    cstt = (junk, cst2)[ct % 2]
                    cbf = (B_("junk"), B_("xt0"))[ct % 2]
                    for cc in range(2):
                        bki = 4 + cc
                        bk = bank[bki]
                        p.op("t", lambda h, ct=ct, cc=cc, bki=bki: h.matmul(
                            pv(bki), rst[0:10, ct * 128:(ct + 1) * 128],
                            brev[0:10, cc * 512:(cc + 1) * 512], start=True, stop=True),
                            reads=[B_("rst"), B_("brev")], writes=[bk])
                        p.op("s", lambda h, cstt=cstt, cc=cc, bki=bki: h.copy(
                            cstt[:, cc * 512:(cc + 1) * 512], pv(bki)),
                            reads=[bk], writes=[cbf])
                    p.dma("y", lambda h, par=par, ct=ct, cstt=cstt: h.dma_start(
                        A(cdr.tensor, par * 512 * CROW + ct * 128 * CROW + LPAD,
                          [[CROW, 128], [1, 1024]]), cstt[:, :]),
                        reads=[cbf], writes=[cdb])
                esbb = B_("esb")
                for qt in range(4):
                    off = par * 512 * CROW + qt * 128 * CROW + 511
                    p.dma("y", lambda h, qt=qt, off=off: h.dma_start(
                        esb[:, qt, :], A(cdr.tensor, off, [[CROW - 1, 128], [1, 1152]])),
                        reads=[cdb, B_(f"cpad{par}")], writes=[esbb])
                if KDBG == 1 and b == 0 and hh == 0:
                    p.dma("y", lambda h: h.dma_start(dbgE, esb[:, :, :]),
                          reads=[esbb], writes=[B_("dbgE")])
                p.op("t", lambda h: h.matmul(pbig[3][:, 0:260], zero128[:, :],
                     vt[:, 0, 0:260], start=True, stop=False),
                     reads=[B_("zero128"), B_("vt")], writes=[bank[6], bank[7]])
                wsbb = B_("wsb")

                def emit_qke(kb, hh, qpad):
                    sj = (kb // 2) % 2
                    bki2 = 2 * sj + (kb % 2)
                    bk = bank[bki2]
                    qts = [qt for qt in range(4) if 0 <= kb - qt <= 8]
                    p.op("t", lambda h, hh=hh, kb=kb, bki2=bki2, qpad=qpad: h.matmul(
                        pv(bki2), kp[:, hh // 2, kb * 128:(kb + 1) * 128],
                        qpad[:, :], start=True, stop=False),
                        reads=[B_("kp"), B_("qpxl")], writes=[bk])
                    if kb < 8:
                        p.op("t", lambda h, kb=kb, bki2=bki2: h.matmul(
                            pv(bki2), edge_b[0:1, kb * 128:(kb + 1) * 128],
                            ones_row[0:1, :], start=False, stop=False),
                            reads=[B_("edge_b"), B_("ones_row")], writes=[bk])
                    for i, qt in enumerate(qts):
                        p.op("t", lambda h, qt=qt, kb=kb, bki2=bki2,
                             last=(i == len(qts) - 1): h.matmul(
                            pv(bki2, qt * 128, (qt + 1) * 128),
                            esb[:, qt, (kb - qt) * 128:(kb - qt) * 128 + 128], ident[:, :],
                            start=False, stop=last),
                            reads=[esbb, B_("ident")], writes=[bk])

                def emit_exp(kb2):
                    sj = kb2 % 2
                    p.op("s", lambda h, kb2=kb2, sj=sj: h.activation(
                        A(wsb, (kb2 * 2 % 4) * 512, [[4 * 512, 128], [1, 1024]]),
                        pbig[sj][:, :], AF.Exp),
                        reads=[bank[2 * sj], bank[2 * sj + 1]], writes=[wsbb])

                def emit_av(kb, hh):
                    qts = [qt for qt in range(4) if 0 <= kb - qt <= 8]
                    for i, qt in enumerate(qts):
                        p.op("t", lambda h, hh=hh, kb=kb, qt=qt,
                             last=(kb == NKB - 1 and qt == 3): h.matmul(
                            pbig[3][:, qt * 65:qt * 65 + 65],
                            wsb[:, kb % 4, qt * 128:(qt + 1) * 128],
                            vt[:, kb, 65 * hh:65 * hh + 65],
                            start=False, stop=last),
                            reads=[wsbb, B_("vt")], writes=[bank[6], bank[7]])

                for kb2 in range(6):
                    emit_qke(2 * kb2, hh, qpad)
                    emit_qke(2 * kb2 + 1, hh, qpad)
                    emit_exp(kb2)
                    if kb2 >= 1:
                        emit_av(2 * (kb2 - 1), hh)
                        emit_av(2 * (kb2 - 1) + 1, hh)
                emit_av(10, hh)
                emit_av(11, hh)
                if KDBG == 1 and b == 0 and hh == 0:
                    p.dma("y", lambda h: h.dma_start(dbgW, wsb[:, :, :]),
                          reads=[wsbb], writes=[B_("dbgW")])
                if KDBG == 1 and b == 0 and hh == H - 1:
                    pass
                    p.op("s", lambda h: h.copy(junk[:, 0:260].bitcast(F32) if False else xt0[:, 0:260], pbig[3][:, 0:260]),
                         reads=[bank[6], bank[7]], writes=[B_("xt0")])
                    p.dma("y", lambda h: h.dma_start(dbgA, xt0[:, 0:260]),
                          reads=[B_("xt0")], writes=[B_("dbgA")])
                for qt in range(4):
                    p.op("s", lambda h, qt=qt: h.copy(
                        cols[:, 44 + qt:45 + qt], pbig[3][:, qt * 65 + 64:qt * 65 + 65]),
                        reads=[bank[6], bank[7]], writes=[B_("colD")])
                p.op("v", lambda h: h.reciprocal(cols[:, 32:36], cols[:, 44:48]),
                     reads=[B_("colD")], writes=[B_("colD")])
                for qt in range(4):
                    p.op("s", lambda h, qt=qt, hh=hh: h.activation(
                        asb[:, qt, 64 * hh:64 * hh + 64], pbig[3][:, qt * 65:qt * 65 + 64],
                        AF.Identity, scale=cols[:, 32 + qt:33 + qt]),
                        reads=[bank[6], bank[7], B_("colD")], writes=[B_("asb")])

            if KDBG == 1 and b == 0:
                p.dma("y", lambda h: h.dma_start(dbgS, asb[:, :, :]),
                      reads=[B_("asb")], writes=[B_("dbgS")])
            if KDBG == 2 and b == 0:
                p.dma("y", lambda h: h.dma_start(dbgR, x_tc[:, 0:4, :]),
                      reads=[B_("x_tc")], writes=[B_("dbgR")])
            if "p3" not in KPH:
                continue
            p.dma("g", lambda h: h.dma_start(
                A(wslot0, 0, [[8 * 1024, 128], [1024, 8], [1, 1024]]),
                A(wpj_d.tensor, 0, [[1024, 128], [128 * 1024, 8], [1, 1024]])),
                reads=[B_("wslot0")], writes=[B_("wslot0")])
            for hc in range(8):
                bki = 4 + hc % 2
                bk = bank[bki]
                for qt in range(4):
                    p.op("t", lambda h, hc=hc, qt=qt, bki=bki: h.matmul(
                        pv(bki, qt * 128, (qt + 1) * 128),
                        asb[:, qt, hc * 128:(hc + 1) * 128], ident[:, :],
                        start=True, stop=True),
                        reads=[B_("asb"), B_("ident")], writes=[bk])
                p.op("s", lambda h, hc=hc, bki=bki: h.copy(
                    act_[:, hc, :], pv(bki)), reads=[bk], writes=[B_("esb")])
            if KDBG == 2 and b == 0:
                p.dma("y", lambda h: h.dma_start(dbgT, act_[:, :, :]),
                      reads=[B_("esb")], writes=[B_("dbgT")])
            for jt in range(4):
                for cc in range(2):
                    bki = (jt * 2 + cc) % 2
                    bk = bank[bki]
                    for hc in range(8):
                        p.op("t", lambda h, jt=jt, cc=cc, hc=hc, bki=bki: h.matmul(
                            pv(bki), act_[:, hc, jt * 128:(jt + 1) * 128],
                            wslot0[:, hc, cc * 512:(cc + 1) * 512],
                            start=(hc == 0), stop=False),
                            reads=[B_("esb"), B_("wslot0")], writes=[bk])
                    p.op("t", lambda h, cc=cc, bki=bki: h.matmul(
                        pv(bki), ones_row[0:1, 0:128], bpj[0:1, cc * 512:(cc + 1) * 512],
                        start=False, stop=True),
                        reads=[B_("ones_row"), B_("bpj")], writes=[bk])
                    j = b * 4 + jt
                    p.op("s", lambda h, cc=cc, bki=bki: h.copy(
                        xt1[:, bki * 512:(bki + 1) * 512], pv(bki)),
                        reads=[bk], writes=[B_("xt1")])
                    p.op("v", lambda h, j=j, cc=cc, bki=bki: h.tensor_tensor(
                        x_tc[:, j, cc * 512:(cc + 1) * 512],
                        x_tc[:, j, cc * 512:(cc + 1) * 512],
                        xt1[:, bki * 512:(bki + 1) * 512], ALU.add),
                        reads=[B_("xt1"), B_("x_tc")], writes=[B_("x_tc")])
            if KDBG == 2 and b == 0:
                p.dma("y", lambda h: h.dma_start(dbgO, x_tc[:, 0:4, :]),
                      reads=[B_("x_tc")], writes=[B_("dbgO")])

    def plan_ffn(call):
        wsc = B_("wscal")
        cs = B_("colS")
        cu = B_("colU")
        for g in range(2):
            tiles = list(range(8 * g, 8 * g + 8))
            for ii, jt in enumerate(tiles):
                k = jt % 2
                xq_, xi_ = xql[k], xqi[k]
                xqlb, xqib = B_(f"xql{k}"), B_(f"xqi{k}")
                ci = k * 8
                cb = B_(f"colL{k}")
                ln_stats(x_tc[:, jt, :], B_("x_tc"), ci, cb)
                p.op("s", lambda h, jt=jt, xq_=xq_, ci=ci: h.activation(
                    xq_[:, :], x_tc[:, jt, :], AF.Identity,
                    bias=cols[:, ci + 1:ci + 2], scale=cols[:, ci:ci + 1]),
                    reads=[B_("x_tc"), cb], writes=[xqlb])
                p.op("v", lambda h, xq_=xq_, ci=ci: h.reduce_max(
                    cols[:, ci + 5:ci + 6], xq_[:, :], axis=AX.X,
                    apply_absolute_value=True), reads=[xqlb], writes=[cb])
                p.op("v", lambda h, ci=ci: h.tensor_scalar(cols[:, ci + 5:ci + 6],
                     cols[:, ci + 5:ci + 6], 1e-5, None, ALU.max), reads=[cb], writes=[cb])
                p.op("v", lambda h, ci=ci, ii=ii: h.tensor_scalar(cols[:, 40 + ii:41 + ii],
                     cols[:, ci + 5:ci + 6], 1.0 / 127.0, None, ALU.mult),
                     reads=[cb], writes=[cs])
                p.op("v", lambda h, ci=ci: h.reciprocal(cols[:, ci + 6:ci + 7],
                     cols[:, ci + 5:ci + 6]), reads=[cb], writes=[cb])
                p.op("v", lambda h, ci=ci: h.tensor_scalar(cols[:, ci + 6:ci + 7],
                     cols[:, ci + 6:ci + 7], 127.0, None, ALU.mult), reads=[cb], writes=[cb])
                for ch in range(2):
                    sl = slice(ch * 512, (ch + 1) * 512)
                    rnd_int8(xq_[:, sl], [xqlb], cols[:, ci + 6:ci + 7], cb,
                             xi_[:, sl], xqib, ch)
                for half in range(2):
                    bki = 4 + (ii + half) % 2
                    bk = bank[bki]
                    for kc in range(4):
                        kg = half * 4 + kc
                        p.op("t", lambda h, xi_=xi_, kg=kg, kc=kc, bki=bki: h.matmul(
                            pv(bki, kc * 128, (kc + 1) * 128),
                            xi_[:, kg * 128:(kg + 1) * 128], ident[:, :],
                            start=True, stop=True),
                            reads=[xqib, B_("ident")], writes=[bk])
                    p.op("s", lambda h, ii=ii, half=half, bki=bki: h.copy(
                        A(xqc, ii * 1024 + half * 512, [[8192, 128], [1, 512]]),
                        pv(bki)), reads=[bk], writes=[B_("xqc")])
                p.op("v", lambda h, ii=ii: h.tensor_tensor(cols[:, 48 + ii:49 + ii],
                     cols[:, 40 + ii:41 + ii], wscal[:, 0:1], ALU.mult),
                     reads=[cs, wsc], writes=[cs])
                p.op("v", lambda h, ii=ii: h.tensor_tensor(cols[:, 24 + ii:25 + ii],
                     cols[:, 40 + ii:41 + ii], wscal[:, 1:2], ALU.mult),
                     reads=[cs, wsc], writes=[cu])
            for oc in range(8):
                k = oc % 2
                wg_s = wfs[k]
                wu_s = wfs[2]
                wgb, wub = B_(f"wfs{k}"), B_("wfs2")
                p.dma("g", lambda h, oc=oc, wg_s=wg_s: h.dma_start(
                    A(wg_s, 0, [[8 * 512, 128], [512, 8], [1, 512]]),
                    A(wgQ.tensor, oc * 512, [[INNER, 128], [128 * INNER, 8], [1, 512]])),
                    reads=[wgb], writes=[wgb])
                p.dma("g", lambda h, oc=oc, wu_s=wu_s: h.dma_start(
                    A(wu_s, 0, [[8 * 512, 128], [512, 8], [1, 512]]),
                    A(wuQ.tensor, oc * 512, [[INNER, 128], [128 * INNER, 8], [1, 512]])),
                    reads=[wub], writes=[wub])
                for ii in range(8):
                    bg = ii % 2
                    bu = 2 + ii % 2
                    for kc in range(8):
                        p.op("t", lambda h, ii=ii, kc=kc, wg_s=wg_s, bg=bg: h.matmul(
                            pv(bg), A(xqc, ii * 1024 + kc * 128, [[8192, 128], [1, 128]]),
                            wg_s[:, kc, :], start=(kc == 0), stop=(kc == 7)),
                            reads=[B_("xqc"), wgb], writes=[bank[bg]])
                    for kc in range(8):
                        p.op("t", lambda h, ii=ii, kc=kc, wu_s=wu_s, bu=bu: h.matmul(
                            pv(bu), A(xqc, ii * 1024 + kc * 128, [[8192, 128], [1, 128]]),
                            wu_s[:, kc, :], start=(kc == 0), stop=(kc == 7)),
                            reads=[B_("xqc"), wub], writes=[bank[bu]])
                    gb_, ub_ = gbuf[ii % 2], gbuf[2]
                    gbb, ubb = B_(f"gbf{ii % 2}"), B_("gbf2")
                    p.op("s", lambda h, gb_=gb_, ii=ii, bg=bg: h.activation(
                        gb_[:, :], pv(bg), AF.Silu, scale=cols[:, 48 + ii:49 + ii]),
                        reads=[bank[bg], cs], writes=[gbb])
                    p.op("s", lambda h, ub_=ub_, ii=ii, bu=bu: h.activation(
                        ub_[:, :], pv(bu), AF.Identity, scale=cols[:, 24 + ii:25 + ii]),
                        reads=[bank[bu], cu], writes=[ubb])
                    p.op("v", lambda h, gb_=gb_, ub_=ub_, ii=ii, oc=oc: h.tensor_tensor(
                        hb[:, ii, oc * 512:(oc + 1) * 512], gb_[:, :], ub_[:, :], ALU.mult),
                        reads=[gbb, ubb], writes=[B_("hb")])
            for ii in range(8):
                hm = 16 + (ii % 2) * 4
                cb = B_(f"colH{ii % 2}")
                p.op("v", lambda h, ii=ii, hm=hm: h.reduce_max(cols[:, hm:hm + 1],
                     hb[:, ii, :], axis=AX.X, apply_absolute_value=True),
                     reads=[B_("hb")], writes=[cb])
                p.op("v", lambda h, hm=hm: h.tensor_scalar(cols[:, hm:hm + 1],
                     cols[:, hm:hm + 1], 1e-5, None, ALU.max), reads=[cb], writes=[cb])
                p.op("v", lambda h, hm=hm, ii=ii: h.tensor_scalar(cols[:, 56 + ii:57 + ii],
                     cols[:, hm:hm + 1], 1.0 / 127.0, None, ALU.mult), reads=[cb], writes=[cs])
                p.op("v", lambda h, ii=ii: h.tensor_tensor(cols[:, 56 + ii:57 + ii],
                     cols[:, 56 + ii:57 + ii], wscal[:, 2:3], ALU.mult),
                     reads=[cs, wsc], writes=[cs])
                p.op("v", lambda h, hm=hm: h.reciprocal(cols[:, hm + 1:hm + 2],
                     cols[:, hm:hm + 1]), reads=[cb], writes=[cb])
                p.op("v", lambda h, hm=hm: h.tensor_scalar(cols[:, hm + 1:hm + 2],
                     cols[:, hm + 1:hm + 2], 127.0, None, ALU.mult), reads=[cb], writes=[cb])
                for ch in range(8):
                    sl = slice(ch * 512, (ch + 1) * 512)
                    rnd_int8(hb[:, ii, sl], [B_("hb")], cols[:, hm + 1:hm + 2], cb,
                             hb[:, ii, sl], B_("hb"), ch)
            for half in range(2):
                p.barrier(dma=KDMA)
                for ii in range(4):
                    ti = half * 4 + ii
                    for qtr in range(8):
                        bki = 4 + (ii * 8 + qtr) % 2
                        bk = bank[bki]
                        for kc in range(4):
                            oc32 = qtr * 4 + kc
                            p.op("t", lambda h, ti=ti, oc32=oc32, kc=kc, bki=bki: h.matmul(
                                pv(bki, kc * 128, (kc + 1) * 128),
                                hb[:, ti, oc32 * 128:(oc32 + 1) * 128], ident[:, :],
                                start=True, stop=True),
                                reads=[B_("hb"), B_("ident")], writes=[bk])
                        p.op("s", lambda h, ii=ii, qtr=qtr, bki=bki: h.copy(
                            A(hot, (ii * 32 + qtr * 4) * 128, [[16384, 128], [1, 512]]),
                            pv(bki)), reads=[bk], writes=[B_("hot")])
                for cc in range(2):
                    for oc32 in range(32):
                        k = oc32 % 2
                        wd_s = wdfs[k]
                        wdb = B_(f"wdfs{k}")
                        p.dma("g", lambda h, cc=cc, oc32=oc32, wd_s=wd_s: h.dma_start(
                            wd_s[:, :],
                            A(wdQ.tensor, oc32 * 128 * E + cc * 512, [[E, 128], [1, 512]])),
                            reads=[wdb], writes=[wdb])
                        for ii in range(4):
                            p.op("t", lambda h, ii=ii, oc32=oc32, wd_s=wd_s: h.matmul(
                                pv(ii), A(hot, (ii * 32 + oc32) * 128, [[16384, 128], [1, 128]]),
                                wd_s[:, :], start=(oc32 == 0), stop=(oc32 == 31)),
                                reads=[B_("hot"), wdb], writes=[bank[ii]])
                    for ii in range(4):
                        ti = half * 4 + ii
                        jt = tiles[ti]
                        gb_ = gbuf[ii % 2]
                        gbb = B_(f"gbf{ii % 2}")
                        p.op("s", lambda h, ti=ti, ii=ii, gb_=gb_: h.activation(
                            gb_[:, :], pv(ii), AF.Identity, scale=cols[:, 56 + ti:57 + ti]),
                            reads=[bank[ii], cs], writes=[gbb])
                        p.op("v", lambda h, jt=jt, cc=cc, gb_=gb_: h.tensor_tensor(
                            x_tc[:, jt, cc * 512:(cc + 1) * 512],
                            x_tc[:, jt, cc * 512:(cc + 1) * 512], gb_[:, :], ALU.add),
                            reads=[gbb, B_("x_tc")], writes=[B_("x_tc")])
                p.barrier(dma=KDMA)

    def plan_attnb():
        KSL = [0, 2, 3, 1]
        kb_ = B_("kcv")
        def kv(et, s):
            return A(kcv, et * 4 + s, [[32, 128], [1, 1]])
        kall = lambda s: A(kcv, s, [[32, 128], [4, 8], [1, 1]])
        p.op("s", lambda h: h.activation(kall(0), gwc[:, :], AF.Sigmoid),
             reads=[B_("gwc")], writes=[kb_])
        p.op("v", lambda h: h.tensor_scalar(kall(1), kall(0), -1.0, 1.0, ALU.mult, ALU.add),
             reads=[kb_], writes=[kb_])
        p.op("v", lambda h: h.tensor_tensor(kall(2), kall(0), kall(1), ALU.mult),
             reads=[kb_], writes=[kb_])
        p.op("v", lambda h: h.tensor_tensor(kall(3), kall(2), kall(1), ALU.mult),
             reads=[kb_], writes=[kb_])
        p.op("v", lambda h: h.tensor_tensor(kall(1), kall(3), kall(1), ALU.mult),
             reads=[kb_], writes=[kb_])
        p.dma("g", lambda h: h.dma_start(bout[:, :], bout_d), writes=[B_("bout")])
        p.dma("g", lambda h: h.dma_start(
            A(wib, 0, [[8 * 1024, 128], [1024, 8], [1, 1024]]),
            A(winT_d.tensor, 0, [[1024, 128], [128 * 1024, 8], [1, 1024]])),
            writes=[B_("wib")])
        p.dma("g", lambda h: h.dma_start(
            A(wob, 0, [[8 * 1024, 128], [1024, 8], [1, 1024]]),
            A(woutT_d.tensor, 0, [[1024, 128], [128 * 1024, 8], [1, 1024]])),
            writes=[B_("wob")])
        for b in range(B):
            for tt in range(4):
                jt = b * 4 + tt
                k = tt % 2
                ci = k * 8
                cb = B_(f"colL{k}")
                ln_stats(x_tc[:, jt, :], B_("x_tc"), ci, cb)
                p.op("s", lambda h, jt=jt, ci=ci: h.activation(
                    xltb2[:, :], x_tc[:, jt, :], AF.Identity,
                    bias=cols[:, ci + 1:ci + 2], scale=cols[:, ci:ci + 1]),
                    reads=[B_("x_tc"), cb], writes=[B_("xltb2")])
                for half in range(2):
                    bki = 4 + (tt + half) % 2
                    bk = bank[bki]
                    for kc in range(4):
                        kg = half * 4 + kc
                        p.op("t", lambda h, kg=kg, kc=kc, bki=bki: h.matmul(
                            pv(bki, kc * 128, (kc + 1) * 128),
                            xltb2[:, kg * 128:(kg + 1) * 128], ident[:, :],
                            start=True, stop=True),
                            reads=[B_("xltb2"), B_("ident")], writes=[bk])
                    p.op("s", lambda h, tt=tt, half=half, bki=bki: h.copy(
                        A(xctb, half * 4 * 512 + tt * 128, [[8 * 512, 128], [512, 4], [1, 128]]),
                        pv(bki).rearrange("p (a c) -> p a c", a=4)),
                        reads=[bk], writes=[B_("xctb")])
            for et in range(8):
                bki = et % 2
                bk = bank[bki]
                for kc in range(8):
                    p.op("t", lambda h, et=et, kc=kc, bki=bki: h.matmul(
                        pv(bki), wib[:, kc, et * 128:(et + 1) * 128],
                        xctb[:, kc, :], start=(kc == 0), stop=(kc == 7)),
                        reads=[B_("wib"), B_("xctb")], writes=[bk])
                p.op("s", lambda h, et=et, b=b, bki=bki: h.activation(
                    ub[:, et, b * 512:(b + 1) * 512], pv(bki), AF.Silu,
                    bias=binc[:, et:et + 1]), reads=[bk, B_("binc")], writes=[B_("ub")])
        for b in range(B):
            for et in range(8):
                if b == 0:
                    p.op("v", lambda h, et=et: h.tensor_scalar(
                        vbb[:, et, :], ub[:, et, 0:512], kv(et, KSL[0]), None, ALU.mult),
                        reads=[B_("ub"), kb_], writes=[B_("vbb")])
                else:
                    p.op("v", lambda h, et=et, b=b: h.tensor_scalar(
                        cvt0[:, :], ub[:, et, 0:512], kv(et, KSL[b]), None, ALU.mult),
                        reads=[B_("ub"), kb_], writes=[B_("cvt0")])
                    for s_ in range(1, b):
                        p.op("v", lambda h, et=et, b=b, s_=s_: h.scalar_tensor_tensor(
                            cvt0[:, :], ub[:, et, s_ * 512:(s_ + 1) * 512],
                            kv(et, KSL[b - s_]), cvt0[:, :], ALU.mult, ALU.add),
                            reads=[B_("ub"), kb_, B_("cvt0")], writes=[B_("cvt0")])
                    p.op("v", lambda h, et=et, b=b: h.scalar_tensor_tensor(
                        vbb[:, et, :], ub[:, et, b * 512:(b + 1) * 512], kv(et, KSL[0]),
                        cvt0[:, :], ALU.mult, ALU.add),
                        reads=[B_("ub"), kb_, B_("cvt0")], writes=[B_("vbb")])
            for jt4 in range(4):
                for cc in range(2):
                    bki = (jt4 * 2 + cc) % 2
                    bk = bank[bki]
                    for dc in range(8):
                        p.op("t", lambda h, jt4=jt4, cc=cc, dc=dc, bki=bki: h.matmul(
                            pv(bki), vbb[:, dc, jt4 * 128:(jt4 + 1) * 128],
                            wob[:, dc, cc * 512:(cc + 1) * 512],
                            start=(dc == 0), stop=False),
                            reads=[B_("vbb"), B_("wob")], writes=[bk])
                    p.op("t", lambda h, cc=cc, bki=bki: h.matmul(
                        pv(bki), ones_row[0:1, 0:128], bout[0:1, cc * 512:(cc + 1) * 512],
                        start=False, stop=True),
                        reads=[B_("ones_row"), B_("bout")], writes=[bk])
                    ttb_ = tts[bki]
                    ttbb = B_(f"tts{bki}")
                    p.op("s", lambda h, ttb_=ttb_, bki=bki: h.activation(
                        ttb_[:, :], pv(bki), AF.Tanh), reads=[bk], writes=[ttbb])
                    j = b * 4 + jt4
                    p.op("v", lambda h, j=j, cc=cc, ttb_=ttb_: h.tensor_tensor(
                        x_tc[:, j, cc * 512:(cc + 1) * 512],
                        x_tc[:, j, cc * 512:(cc + 1) * 512], ttb_[:, :], ALU.add),
                        reads=[ttbb, B_("x_tc")], writes=[B_("x_tc")])

    import os as _os2
    KBAR = _os2.environ.get("KBAR", "1") == "1"
    KDMA = _os2.environ.get("KDMA", "1") == "1"
    if "wq" in stages:
        plan_wquant()
    if "attn" in stages:
        plan_attn()
    if KBAR:
        p.barrier(dma=KDMA)
    if "ffn1" in stages:
        plan_ffn(0)
    if KBAR:
        p.barrier(dma=KDMA)
    if "attnb" in stages:
        plan_attnb()
    if KBAR:
        p.barrier(dma=KDMA)
    if "ffn2" in stages:
        plan_ffn(1)

    p.dma("y", lambda h: h.dma_start(
        A(out_d.tensor, 0, [[1024, 128], [128 * 1024, 16], [1, 1024]]),
        A(x_tc, 0, [[16 * 1024, 128], [1024, 16], [1, 1024]])),
        reads=[B_("x_tc")], writes=[B_("out")])

    def final_waits(h):
        for name, val in p.dma_cnt.items():
            h.wait_ge(p.sems[name], val)
        return h.nop()
    p.q["y"].append(final_waits)

    p.assign_values()
    globals()["LAST_P"] = p
    with nc.Block() as block:
        @block.tensor
        def _(h):
            for f in p.q["t"]:
                f(h)

        @block.vector
        def _(h):
            for f in p.q["v"]:
                f(h)

        @block.scalar
        def _(h):
            for f in p.q["s"]:
                f(h)

        @block.sync
        def _(h):
            for f in p.q["y"]:
                f(h)

        @block.gpsimd
        def _(h):
            for f in p.q["g"]:
                f(h)

    stack.close()
    return nc


def _stage_inputs(kw):
    X = np.ascontiguousarray(np.asarray(kw["L0_attn_a_X"], np.float32))
    wq = np.ascontiguousarray(np.asarray(kw["L0_attn_a_Wq"], np.float32))
    wk = np.ascontiguousarray(np.asarray(kw["L0_attn_a_Wk"], np.float32))
    wv = np.ascontiguousarray(np.asarray(kw["L0_attn_a_Wv"], np.float32))
    wr = np.asarray(kw["L0_attn_a_Wr"], np.float32)
    wrp = np.zeros((E, 512), np.float32)
    for hh in range(H):
        wrp[:, 32 * hh:32 * hh + 10] = wr[:, 10 * hh:10 * hh + 10]
    wpj = np.ascontiguousarray(np.asarray(kw["L0_attn_a_Wproj"], np.float32))
    bq = np.asarray(kw["L0_attn_a_bq"], np.float32)
    bqc = np.ascontiguousarray(bq.reshape(8, 128).T)
    br = np.asarray(kw["L0_attn_a_br"], np.float32)
    brp = np.zeros((512,), np.float32)
    for hh in range(H):
        brp[32 * hh:32 * hh + 10] = br[10 * hh:10 * hh + 10]
    brc = np.ascontiguousarray(brp.reshape(4, 128).T)
    bpj = np.ascontiguousarray(np.asarray(kw["L0_attn_a_bproj"], np.float32).reshape(1, E))
    bndr = np.ascontiguousarray(np.asarray(kw["L0_attn_a_b_nd"], np.float32)[:, ::-1])
    winT = np.ascontiguousarray(np.asarray(kw["L0_attn_b_W_in"], np.float32).T)
    woutT = np.ascontiguousarray(np.asarray(kw["L0_attn_b_W_out"], np.float32).T)
    binc = np.ascontiguousarray(np.asarray(kw["L0_attn_b_b_in"], np.float32).reshape(8, 128).T)
    bout = np.ascontiguousarray(np.asarray(kw["L0_attn_b_b_out"], np.float32).reshape(1, E))
    gwc = np.ascontiguousarray(np.asarray(kw["L0_attn_b_weight"], np.float32).reshape(8, 128).T)
    wgT = np.ascontiguousarray(np.asarray(kw["L0_ffn_w_gate"], np.float32).T)
    wuT = np.ascontiguousarray(np.asarray(kw["L0_ffn_w_up"], np.float32).T)
    wdT = np.ascontiguousarray(np.asarray(kw["L0_ffn_w_down"], np.float32).T)
    idm = np.eye(128, dtype=bfd)

    maps = []
    for c in range(NCORE):
        lo = 512 * c - 1024
        hi = 512 * c + 512
        xs = np.zeros((B, SPAN, E), np.float32)
        s0 = max(0, lo)
        xs[:, s0 - lo:, :] = X[:, s0:hi, :]
        edge = np.zeros((1, 1024), np.float32)
        npad = max(0, -lo)
        edge[0, :npad] = NEG
        maps.append(dict(
            xsp=np.ascontiguousarray(xs), wq=wq, wk=wk, wv=wv, wrp=wrp, wpj=wpj,
            bqc=bqc, brc=brc, bpj=bpj, bndr=bndr, edge=np.ascontiguousarray(edge),
            winT=winT, woutT=woutT, binc=binc, bout=bout, gwc=gwc,
            wgT=wgT, wuT=wuT, wdT=wdT, idm=idm,
        ))
    return maps


def kernel(**kw):
    maps = _stage_inputs(kw)
    nc = build_program()
    res = run_bass_kernel_spmd(nc, maps, core_ids=list(range(NCORE)))
    out = np.empty((B, T, E), np.float32)
    for c in range(NCORE):
        o = np.asarray(res.results[c]["out"]).reshape(16, 128, E)
        for b in range(B):
            out[b, 512 * c:512 * (c + 1), :] = o[4 * b:4 * b + 4].reshape(512, E)
    return out


# version 06
# speedup vs baseline: 1.0200x; 1.0092x over previous
import numpy as np
import ml_dtypes
import contextlib
import concourse.bass as bass
import concourse.mybir as mybir
from concourse.bass_utils import run_bass_kernel_spmd

F32 = mybir.dt.float32
BF16 = mybir.dt.bfloat16
I32 = mybir.dt.int32
AF = mybir.ActivationFunctionType
ALU = mybir.AluOpType
AX = mybir.AxisListType
bfd = ml_dtypes.bfloat16

NCORE = 8
B = 4
T = 4096
E = 1024
H = 16
NB = 10
SPAN = 1536
NKB = 12
NEG = -1.0e9
INNER = 4096
CROW = 2048
LPAD = 512
OWN0 = 1024

ENGS = ("t", "v", "s", "y", "g")


class Cell:
    __slots__ = ("eng", "idx", "inc", "val")
    def __init__(self, eng, idx):
        self.eng = eng; self.idx = idx; self.inc = False; self.val = None


class DmaTok:
    __slots__ = ("sem", "val")
    def __init__(self, sem, val):
        self.sem = sem; self.val = val


class Buf:
    __slots__ = ("name", "w", "readers")
    def __init__(self, name):
        self.name = name; self.w = None; self.readers = []


class P:
    def __init__(self, nc):
        self.nc = nc
        self.log = {}
        self.q = {e: [] for e in ENGS}
        self.opidx = {e: 0 for e in ENGS}
        self.lastdrain = {e: -1 for e in ENGS}
        self.wm = {}
        self.dma_wm = {}
        self.cells = {e: [] for e in ENGS}
        self.sems = {}
        self.dma_cnt = {}
        self.dma_names = {"y": ["dy0", "dy1"], "g": ["dg0", "dg1"], "s": ["ds0", "ds1"]}
        self.dma_rr = {"y": 0, "g": 0, "s": 0}
        self.bufs = {}

    def B(self, name):
        if name not in self.bufs:
            self.bufs[name] = Buf(name)
        return self.bufs[name]

    def _await(self, eng, tk, raw=True):
        if isinstance(tk, DmaTok):
            key = (eng, tk.sem)
            if self.dma_wm.get(key, 0) >= tk.val:
                return
            self.dma_wm[key] = tk.val
            sem_name, val = tk.sem, tk.val
            self.log.setdefault(eng, []).append(("waitD", sem_name, val))
            self.q[eng].append(lambda h: h.wait_ge(self.sems[sem_name], val))
        elif tk.eng == eng:
            if eng == "t" or not raw:
                return
            if tk.idx > self.lastdrain[eng]:
                self.log.setdefault(eng, []).append(("drain",))
                self.q[eng].append(lambda h: h.drain())
                self.lastdrain[eng] = self.opidx[eng] - 1
        else:
            tk.inc = True
            key = (eng, tk.eng)
            if self.wm.get(key, -1) >= tk.idx:
                return
            self.wm[key] = tk.idx
            peng = tk.eng
            self.log.setdefault(eng, []).append(("waitE", peng, tk))
            self.q[eng].append(lambda h, tk=tk, peng=peng: h.wait_ge(self.sems[peng], tk.val))

    def _deps(self, eng, reads, writes):
        for tk in getattr(self, "fence", ()):
            self._await(eng, tk)
        for b in reads:
            if b.w is not None:
                self._await(eng, b.w)
        for b in writes:
            if b.w is not None:
                self._await(eng, b.w, raw=False)
            for tk in b.readers:
                self._await(eng, tk)

    def op(self, eng, emitfn, reads=(), writes=()):
        self._deps(eng, reads, writes)
        cell = Cell(eng, self.opidx[eng])
        def run(h, cell=cell, emitfn=emitfn, self_=self):
            ins = emitfn(h)
            if cell.inc:
                ins.then_inc(self_.sems[cell.eng], 1)
        self.q[eng].append(run)
        self.cells[eng].append(cell)
        import traceback as _tb
        fr = _tb.extract_stack()[-2]
        self.log.setdefault(eng, []).append(("op", cell, f"{fr.name}:{fr.lineno}",
                                             [b.name for b in reads], [b.name for b in writes]))
        self.opidx[eng] += 1
        for b in writes:
            b.w = cell; b.readers = []
        for b in reads:
            b.readers.append(cell)
        return cell

    def dma(self, qeng, emitfn, reads=(), writes=()):
        self._deps(qeng, reads, writes)
        name = self.dma_names[qeng][self.dma_rr[qeng]]
        self.dma_rr[qeng] ^= 1
        val = self.dma_cnt.get(name, 0) + 16
        self.dma_cnt[name] = val
        tok = DmaTok(name, val)
        def run(h, emitfn=emitfn, name=name, self_=self):
            ins = emitfn(h)
            ins.then_inc(self_.sems[name], 16)
        self.q[qeng].append(run)
        self.log.setdefault(qeng, []).append(("dma", name, val,
                                              [b.name for b in reads], [b.name for b in writes]))
        self.opidx[qeng] += 1
        for b in writes:
            b.w = tok; b.readers = []
        for b in reads:
            b.readers.append(tok)
        return tok

    def barrier(self, dma=True):
        best = {}
        for b in self.bufs.values():
            toks = list(b.readers)
            if b.w is not None:
                toks.append(b.w)
            for tk in toks:
                if isinstance(tk, DmaTok):
                    key = ("d", tk.sem)
                    cur = best.get(key)
                    if cur is None or tk.val > cur.val:
                        best[key] = tk
                else:
                    key = ("e", tk.eng)
                    cur = best.get(key)
                    if cur is None or tk.idx > cur.idx:
                        best[key] = tk
        for sem, val in self.dma_cnt.items():
            key = ("d", sem)
            cur = best.get(key)
            if cur is None or val > cur.val:
                best[key] = DmaTok(sem, val)
        self.fence = list(best.values())

    def assign_values(self):
        for e in ENGS:
            v = 0
            for c in self.cells[e]:
                if c.inc:
                    v += 1
                    c.val = v


def build_program(stages=("wq", "attn", "ffn1", "attnb", "ffn2")):
    nc = bass.Bass("TRN2", target_bir_lowering=False, detect_race_conditions=False,
                   dynamic_dma_scratch_size=16384)

    xsp = nc.dram_tensor("xsp", [B, SPAN, E], F32, kind="ExternalInput").ap()
    wq_d = nc.dram_tensor("wq", [E, E], F32, kind="ExternalInput").ap()
    wk_d = nc.dram_tensor("wk", [E, E], F32, kind="ExternalInput").ap()
    wv_d = nc.dram_tensor("wv", [E, E], F32, kind="ExternalInput").ap()
    wrp_d = nc.dram_tensor("wrp", [E, 512], F32, kind="ExternalInput").ap()
    wpj_d = nc.dram_tensor("wpj", [E, E], F32, kind="ExternalInput").ap()
    bqc_d = nc.dram_tensor("bqc", [128, 8], F32, kind="ExternalInput").ap()
    brc_d = nc.dram_tensor("brc", [128, 4], F32, kind="ExternalInput").ap()
    bpj_d = nc.dram_tensor("bpj", [1, E], F32, kind="ExternalInput").ap()
    bndr_d = nc.dram_tensor("bndr", [NB, 1024], F32, kind="ExternalInput").ap()
    edge_d = nc.dram_tensor("edge", [1, 1024], F32, kind="ExternalInput").ap()
    winT_d = nc.dram_tensor("winT", [E, E], F32, kind="ExternalInput").ap()
    woutT_d = nc.dram_tensor("woutT", [E, E], F32, kind="ExternalInput").ap()
    binc_d = nc.dram_tensor("binc", [128, 8], F32, kind="ExternalInput").ap()
    bout_d = nc.dram_tensor("bout", [1, E], F32, kind="ExternalInput").ap()
    gwc_d = nc.dram_tensor("gwc", [128, 8], F32, kind="ExternalInput").ap()
    wgT_d = nc.dram_tensor("wgT", [E, INNER], F32, kind="ExternalInput").ap()
    wuT_d = nc.dram_tensor("wuT", [E, INNER], F32, kind="ExternalInput").ap()
    wdT_d = nc.dram_tensor("wdT", [INNER, E], F32, kind="ExternalInput").ap()
    id_d = nc.dram_tensor("idm", [128, 128], BF16, kind="ExternalInput").ap()
    out_d = nc.dram_tensor("out", [2048, E], F32, kind="ExternalOutput").ap()
    import os as _osd
    KDBG = int(_osd.environ.get("KDBG", "0"))
    if KDBG == 1:
        dbgE = nc.dram_tensor("dbgE", [128, 4 * 1152], BF16, kind="ExternalOutput").ap()
        dbgW = nc.dram_tensor("dbgW", [128, 4 * 512], BF16, kind="ExternalOutput").ap()
        dbgA = nc.dram_tensor("dbgA", [128, 4 * 65], F32, kind="ExternalOutput").ap()
        dbgS = nc.dram_tensor("dbgS", [128, 4 * 1024], BF16, kind="ExternalOutput").ap()
    if KDBG == 2:
        dbgR = nc.dram_tensor("dbgR", [128, 4 * 1024], F32, kind="ExternalOutput").ap()
        dbgT = nc.dram_tensor("dbgT", [128, 8 * 512], BF16, kind="ExternalOutput").ap()
        dbgO = nc.dram_tensor("dbgO", [128, 4 * 1024], F32, kind="ExternalOutput").ap()

    cdr = nc.dram_tensor("cdr", [2 * 512 * CROW], BF16, kind="Internal").ap()
    wgQ = nc.dram_tensor("wgQ", [E * INNER], BF16, kind="Internal").ap()
    wuQ = nc.dram_tensor("wuQ", [E * INNER], BF16, kind="Internal").ap()
    wdQ = nc.dram_tensor("wdQ", [INNER * E], BF16, kind="Internal").ap()
    scl_d = nc.dram_tensor("scl", [1, 4], F32, kind="Internal").ap()

    p = P(nc)
    B_ = p.B
    A = bass.AP

    PB = ((nc.sbuf_base + 63) // 64) * 64

    def at(name, shape, dt, off):
        return nc.alloc_sbuf_tensor_at(name, shape, dt, offset=PB + off)

    x_tc = at("x_tc", [128, 16, 1024], F32, 0)
    junk = at("junk", [128, 1024], BF16, 65536)
    ident = at("ident", [128, 128], BF16, 67584)
    zero128 = at("zero128", [128, 128], BF16, 67840)
    ones_row = at("ones_row", [1, 512], BF16, 68096)
    cols = at("cols", [128, 64], F32, 69120)
    bqc = at("bqc_s", [128, 8], F32, 69376)
    brc = at("brc_s", [128, 4], F32, 69408)
    binc = at("binc_s", [128, 8], F32, 69440)
    gwc = at("gwc_s", [128, 8], F32, 69472)
    kcv = at("kcv", [128, 32], F32, 69504)
    wscal = at("wscal", [128, 8], F32, 69632)

    OV = 69696
    qwf = [at(f"qwf{i}", [128, 512], F32, OV + i * 4096) for i in range(2)]
    qwq = [at(f"qwq{i}", [128, 512], BF16, OV + 8192 + i * 1024) for i in range(2)]
    qwt = [at(f"qwt{i}", [128, 512], BF16, OV + 10240 + i * 1024) for i in range(2)]

    xt0 = at("xt0", [128, 1024], F32, OV + 0)
    cst2 = at("cst2", [128, 1024], BF16, OV + 0)
    xt1 = at("xt1", [128, 1024], F32, OV + 4096)
    xlt = at("xlt", [128, 1024], BF16, OV + 8192)
    qpe = at("qpe", [128, 512], BF16, OV + 8192)
    qpo = at("qpo", [128, 512], BF16, OV + 9216)
    xct = at("xct", [128, 2 * 8 * 512], BF16, OV + 10240)
    wslot0 = at("wslot0", [128, 8, 1024], BF16, OV + 26624)
    wslot1 = at("wslot1", [128, 8, 512], BF16, OV + 43008)
    kp = at("kp", [128, 8, 1536], BF16, OV + 51200)
    vt = at("vt", [128, 12, 1040], BF16, OV + 75776)
    q_sb = at("q_sb", [128, 8, 512], BF16, OV + 100736)
    rr = at("rr", [128, 4, 512], BF16, OV + 108928)
    rst = at("rst", [128, 512], BF16, OV + 113024)
    wsb = at("wsb", [128, 4, 512], BF16, OV + 114048)
    esb = at("esb", [128, 4, 1152], BF16, OV + 118144)
    act_ = at("act_", [128, 8, 512], BF16, OV + 118144)
    asb = at("asb", [128, 4, 1024], BF16, OV + 127360)
    brev = at("brev", [128, 1024], BF16, OV + 135552)
    edge_b = at("edge_b", [1, 1024], BF16, OV + 137600)
    bpj = at("bpj_s", [1, 1024], BF16, OV + 139648)

    hb = at("hb", [128, 8, 4096], BF16, OV + 0)
    xqc = at("xqc", [128, 8 * 1024], BF16, OV + 65536)
    xql = [at(f"xql{i}", [128, 1024], F32, OV + 81920 + i * 4096) for i in range(2)]
    xqi = [at(f"xqi{i}", [128, 1024], BF16, OV + 90112 + i * 2048) for i in range(2)]
    rti = [at(f"rti{i}", [128, 512], I32, OV + 94208 + i * 2048) for i in range(2)]
    hot = at("hot", [128, 128, 128], BF16, OV + 65536)
    rtf = [at(f"rtf{i}", [128, 512], F32, OV + 98304 + i * 2048) for i in range(2)]
    rtg = [at(f"rtg{i}", [128, 512], BF16, OV + 102400 + i * 1024) for i in range(2)]
    gbuf = [at(f"gbf{i}", [128, 512], F32, OV + 104448 + i * 2048) for i in range(3)]
    wfs = [at(f"wfs{i}", [128, 8, 512], BF16, OV + 110592 + i * 8192) for i in range(3)]
    wdfs = [at(f"wdfs{i}", [128, 512], BF16, OV + 135168 + i * 1024) for i in range(2)]

    ub = at("ub", [128, 8, 2048], BF16, OV + 0)
    vbb = at("vbb", [128, 8, 512], BF16, OV + 32768)
    xctb = at("xctb", [128, 8, 512], BF16, OV + 40960)
    xltb2 = at("xltb2", [128, 1024], BF16, OV + 49152)
    wib = at("wib", [128, 8, 1024], BF16, OV + 51200)
    wob = at("wob", [128, 8, 1024], BF16, OV + 67584)
    cvt0 = at("cvt0", [128, 512], F32, OV + 83968)
    tts = [at(f"tts{i}", [128, 512], F32, OV + 86016 + i * 2048) for i in range(2)]
    bout = at("bout_s", [1, 1024], BF16, OV + 90112)

    stack = contextlib.ExitStack()
    pbig = [stack.enter_context(nc.psum_tensor(f"pbig{i}", [128, 1024], F32)) for i in range(4)]
    def pv(i, a=0, b2=512):
        return pbig[i // 2][:, (i % 2) * 512 + a:(i % 2) * 512 + b2]
    bank = [B_(f"bank{i}") for i in range(8)]

    for s_ in ("t", "v", "s", "y", "g", "dy0", "dy1", "dg0", "dg1", "ds0", "ds1"):
        p.sems[s_] = stack.enter_context(nc.semaphore("sem_" + s_))

    import os as _os
    KI = _os.environ.get("KI", "xin,smalls,pads").split(",")
    for _b in range(B if "xin" in KI else 0):
        p.dma("y", lambda h, _b=_b: h.dma_start(
            A(x_tc, _b * 4 * 1024, [[16 * 1024, 128], [1024, 4], [1, 1024]]),
            A(xsp.tensor, _b * SPAN * 1024 + OWN0 * 1024,
              [[1024, 128], [128 * 1024, 4], [1, 1024]])),
            writes=[B_("x_tc")])
    p.dma("y", lambda h: h.dma_start(ident[:, :], id_d), writes=[B_("ident")])
    if "smalls" in KI:
        p.dma("g", lambda h: h.dma_start(brev[0:NB, :], bndr_d), writes=[B_("brev")])
        p.dma("g", lambda h: h.dma_start(edge_b[:, :], edge_d), writes=[B_("edge_b")])
        p.dma("g", lambda h: h.dma_start(bpj[:, :], bpj_d), writes=[B_("bpj")])
        p.dma("y", lambda h: h.dma_start(bqc[:, :], bqc_d), writes=[B_("bqc")])
        p.dma("y", lambda h: h.dma_start(brc[:, :], brc_d), writes=[B_("brc")])
        p.dma("y", lambda h: h.dma_start(binc[:, :], binc_d), writes=[B_("binc")])
        p.dma("y", lambda h: h.dma_start(gwc[:, :], gwc_d), writes=[B_("gwc")])
    p.op("v", lambda h: h.memset(ones_row[:, :], 1.0), writes=[B_("ones_row")])
    p.op("v", lambda h: h.memset(zero128[:, :], 0.0), writes=[B_("zero128")])
    if "smalls" in KI:
        p.op("v", lambda h: h.tensor_scalar(bqc[:, :], bqc[:, :], 0.125, None, ALU.mult),
             reads=[B_("bqc")], writes=[B_("bqc")])
    p.op("v", lambda h: h.memset(junk[:, 0:512], NEG), writes=[B_("junk")])
    for par in range(2 if "pads" in KI else 0):
        for side in range(2):
            for rk in range(4):
                off = par * 512 * CROW + rk * 128 * CROW + side * 1536
                p.dma("g", lambda h, off=off: h.dma_start(
                    A(cdr.tensor, off, [[CROW, 128], [1, 512]]), junk[:, 0:512]),
                    reads=[B_("junk")], writes=[B_(f"cpad{par}")])

    def ln_stats(srcap, srcbuf, ci, colbuf):
        p.op("v", lambda h: h.reduce_sum(cols[:, ci + 2:ci + 3], srcap, axis=AX.X),
             reads=[srcbuf], writes=[colbuf])
        p.op("s", lambda h: h.activation(junk[:, :], srcap, AF.Square,
                                         accum_out=cols[:, ci + 3:ci + 4]),
             reads=[srcbuf], writes=[B_("junk"), colbuf])
        p.op("v", lambda h: h.tensor_scalar(cols[:, ci + 2:ci + 3], cols[:, ci + 2:ci + 3],
             1.0 / 1024, None, ALU.mult), reads=[colbuf], writes=[colbuf])
        p.op("v", lambda h: h.tensor_scalar(cols[:, ci + 3:ci + 4], cols[:, ci + 3:ci + 4],
             1.0 / 1024, None, ALU.mult), reads=[colbuf], writes=[colbuf])
        p.op("v", lambda h: h.tensor_tensor(cols[:, ci + 4:ci + 5], cols[:, ci + 2:ci + 3],
             cols[:, ci + 2:ci + 3], ALU.mult), reads=[colbuf], writes=[colbuf])
        p.op("v", lambda h: h.tensor_tensor(cols[:, ci + 3:ci + 4], cols[:, ci + 3:ci + 4],
             cols[:, ci + 4:ci + 5], ALU.subtract), reads=[colbuf], writes=[colbuf])
        p.op("v", lambda h: h.tensor_scalar(cols[:, ci + 3:ci + 4], cols[:, ci + 3:ci + 4],
             1e-5, None, ALU.add), reads=[colbuf], writes=[colbuf])
        p.op("s", lambda h: h.activation(cols[:, ci + 4:ci + 5], cols[:, ci + 3:ci + 4],
             AF.Ln), reads=[colbuf], writes=[colbuf])
        p.op("s", lambda h: h.activation(cols[:, ci:ci + 1], cols[:, ci + 4:ci + 5],
             AF.Exp, scale=-0.5), reads=[colbuf], writes=[colbuf])
        p.op("v", lambda h: h.tensor_tensor(cols[:, ci + 1:ci + 2], cols[:, ci + 2:ci + 3],
             cols[:, ci:ci + 1], ALU.mult), reads=[colbuf], writes=[colbuf])
        p.op("v", lambda h: h.tensor_scalar(cols[:, ci + 1:ci + 2], cols[:, ci + 1:ci + 2],
             -1.0, None, ALU.mult), reads=[colbuf], writes=[colbuf])

    def rnd_int8(srcap, srcbufs, scol, colbuf, dstap, dstbuf, ch):
        k = ch % 2
        ib, fb2, gb_ = rti[k], rtf[k], rtg[k]
        ibb, fbb, gbb = B_(f"rti{k}"), B_(f"rtf{k}"), B_(f"rtg{k}")
        p.op("v", lambda h: h.tensor_scalar(ib[:, :], srcap, scol, 384.0, ALU.mult, ALU.add),
             reads=list(srcbufs) + [colbuf], writes=[ibb])
        p.op("v", lambda h: h.scalar_tensor_tensor(fb2[:, :], srcap, scol, ib[:, :],
             ALU.mult, ALU.subtract), reads=list(srcbufs) + [colbuf, ibb], writes=[fbb])
        p.op("v", lambda h: h.tensor_single_scalar(gb_[:, :], fb2[:, :], -383.5, ALU.is_gt),
             reads=[fbb], writes=[gbb])
        p.op("v", lambda h: h.scalar_tensor_tensor(dstap, ib[:, :], -384.0, gb_[:, :],
             ALU.add, ALU.add), reads=[ibb, gbb], writes=[dstbuf])

    def plan_wquant():
        cw = B_("colW")
        mats = [(wgT_d, wgQ, 0), (wuT_d, wuQ, 1), (wdT_d, wdQ, 2)]
        csz = 128 * 512
        nch = (E * INNER) // csz
        p.op("v", lambda h: h.memset(kcv[:, 30:31], 1.0), writes=[B_("kcvo")])
        for (src, dst, mi) in mats:
            acc = 36 + mi
            p.op("v", lambda h, acc=acc: h.memset(cols[:, acc:acc + 1], 0.0), writes=[cw])
            for ch in range(nch):
                k = ch % 2
                fb = B_(("xt0", "xt1")[k])
                p.dma("g", lambda h, src=src, ch=ch, k=k: h.dma_start(
                    qwf[k][:, :], A(src.tensor, ch * csz, [[512, 128], [1, 512]])),
                    writes=[fb])
                p.op("v", lambda h, k=k: h.reduce_sum(cols[:, 39:40], qwf[k][:, :],
                     axis=AX.X, apply_absolute_value=True), reads=[fb], writes=[cw])
                p.op("v", lambda h, acc=acc: h.tensor_tensor(cols[:, acc:acc + 1],
                     cols[:, acc:acc + 1], cols[:, 39:40], ALU.add), reads=[cw], writes=[cw])
            bk = bank[7]
            p.op("t", lambda h, acc=acc: h.matmul(pv(7, 0, 1)[0:1, :], cols[:, acc:acc + 1],
                 kcv[:, 30:31], start=True, stop=True), reads=[cw, B_("kcvo")], writes=[bk])
            p.op("s", lambda h, acc=acc: h.copy(cols[0:1, 39:40], pv(7, 0, 1)[0:1, :]),
                 reads=[bk], writes=[cw])
            p.op("v", lambda h, acc=acc: h.tensor_scalar(cols[0:1, acc:acc + 1],
                 cols[0:1, 39:40], 1.0 / (E * INNER), None, ALU.mult),
                 reads=[cw], writes=[cw])
            p.op("v", lambda h, acc=acc: h.tensor_scalar(cols[0:1, acc:acc + 1],
                 cols[0:1, acc:acc + 1], 1e-5, None, ALU.max), reads=[cw], writes=[cw])
        p.dma("y", lambda h: h.dma_start(A(scl_d.tensor, 0, [[1, 1], [1, 3]]),
              cols[0:1, 36:39]), reads=[cw], writes=[B_("scl")])
        p.dma("y", lambda h: h.dma_start(A(wscal, 0, [[8, 128], [1, 3]]),
              A(scl_d.tensor, 0, [[0, 128], [1, 3]])),
              reads=[B_("scl")], writes=[B_("wscal")])
        wsc = B_("wscal")
        for mi in range(3):
            p.op("v", lambda h, mi=mi: h.reciprocal(wscal[:, 3 + mi:4 + mi],
                 wscal[:, mi:mi + 1]), reads=[wsc], writes=[wsc])
        p.op("v", lambda h: h.tensor_scalar(wscal[:, 6:7], wscal[:, 3:4], -1.0, None,
             ALU.mult), reads=[wsc], writes=[wsc])
        p.op("v", lambda h: h.tensor_scalar(wscal[:, 7:8], wscal[:, 4:5], -1.0, None,
             ALU.mult), reads=[wsc], writes=[wsc])
        p.op("v", lambda h: h.tensor_scalar(kcv[:, 29:30], wscal[:, 5:6], -1.0, None,
             ALU.mult), reads=[wsc], writes=[B_("kcvq")])
        for (src, dst, mi) in mats:
            scol = wscal[:, 3 + mi:4 + mi]
            nscol = kcv[:, 29:30] if mi == 2 else wscal[:, 6 + mi:7 + mi]
            nsb = B_("kcvq") if mi == 2 else wsc
            for ch in range(nch):
                k = ch % 2
                fb, qb, tb = B_(("xt0", "xt1")[k]), B_("qpxl"), B_("xct0")
                p.dma("g", lambda h, src=src, ch=ch, k=k: h.dma_start(
                    qwf[k][:, :], A(src.tensor, ch * csz, [[512, 128], [1, 512]])),
                    writes=[fb])
                p.op("v", lambda h, k=k, scol=scol: h.tensor_scalar(
                    qwq[k][:, :], qwf[k][:, :], scol, 0.5, ALU.mult, ALU.is_gt),
                    reads=[fb, wsc], writes=[qb])
                p.op("v", lambda h, k=k, nscol=nscol, nsb=nsb: h.tensor_scalar(
                    qwt[k][:, :], qwf[k][:, :], nscol, 0.5, ALU.mult, ALU.is_gt),
                    reads=[fb, nsb], writes=[tb])
                p.op("v", lambda h, k=k: h.tensor_tensor(
                    qwq[k][:, :], qwq[k][:, :], qwt[k][:, :], ALU.subtract),
                    reads=[qb, tb], writes=[qb])
                p.dma("g", lambda h, dst=dst, ch=ch, k=k: h.dma_start(
                    A(dst.tensor, ch * csz, [[512, 128], [1, 512]]), qwq[k][:, :]),
                    reads=[qb], writes=[B_(f"wQd{mi}")])

    def plan_attn():
        import os
        KB_ = int(os.environ.get("KBATCH", str(B)))
        KH_ = int(os.environ.get("KHEAD", str(H)))
        KPH = os.environ.get("KPH", "p1,p2,p3").split(",")
        KP1 = os.environ.get("KP1", "w,x,lns,lna,tr,xc,k,v,vm,qr").split(",")
        for b in range(KB_):
            if "w" in KP1:
                p.dma("g", lambda h: h.dma_start(
                    A(wslot0, 0, [[8 * 1024, 128], [1024, 8], [1, 1024]]),
                    A(wk_d.tensor, 0, [[1024, 128], [128 * 1024, 8], [1, 1024]])),
                    reads=[B_("wslot0")], writes=[B_("wslot0")])
            for nn in range(3):
                cx = nn % 2
                xcb = B_(f"xct{cx}")
                for stl in range(4):
                    st = nn * 4 + stl
                    k = st % 2
                    xt = (xt0, xt1)[k]
                    xtb = B_(f"xt{k}")
                    ci = k * 8
                    cb = B_(f"colL{k}")
                    p.dma("y", lambda h, b=b, st=st, xt=xt: h.dma_start(
                        xt[:, :], xsp[b, st * 128:(st + 1) * 128, :]), writes=[xtb])
                    if "lns" not in KP1:
                        continue
                    ln_stats(xt[:, :], xtb, ci, cb)
                    if "lna" not in KP1:
                        continue
                    p.op("s", lambda h, xt=xt, ci=ci: h.activation(
                        xlt[:, :], xt[:, :], AF.Identity,
                        bias=cols[:, ci + 1:ci + 2], scale=cols[:, ci:ci + 1]),
                        reads=[xtb, cb], writes=[B_("qpxl")])
                    if "tr" not in KP1:
                        continue
                    for half in range(2):
                        bki = 4 + (stl + half) % 2
                        bk = bank[bki]
                        for kc in range(4):
                            kg = half * 4 + kc
                            p.op("t", lambda h, kg=kg, kc=kc, bki=bki: h.matmul(
                                pv(bki, kc * 128, (kc + 1) * 128),
                                xlt[:, kg * 128:(kg + 1) * 128], ident[:, :],
                                start=True, stop=True),
                                reads=[B_("qpxl"), B_("ident")], writes=[bk])
                        p.op("s", lambda h, cx=cx, stl=stl, half=half, bki=bki: h.copy(
                            A(xct, cx * 8 * 512 + half * 4 * 512 + stl * 128,
                              [[2 * 8 * 512, 128], [512, 4], [1, 128]]),
                            pv(bki).rearrange("p (a c) -> p a c", a=4)),
                            reads=[bk], writes=[xcb])
                for ft in range(8 if "k" in KP1 else 0):
                    bki = ft % 2
                    bk = bank[bki]
                    for kc in range(8):
                        p.op("t", lambda h, ft=ft, kc=kc, bki=bki, cx=cx: h.matmul(
                            pv(bki), wslot0[:, kc, ft * 128:(ft + 1) * 128],
                            A(xct, cx * 8 * 512 + kc * 512, [[8192, 128], [1, 512]]),
                            start=(kc == 0), stop=(kc == 7)),
                            reads=[B_("wslot0"), xcb], writes=[bk])
                    p.op("s", lambda h, ft=ft, nn=nn, bki=bki: h.copy(
                        kp[:, ft, nn * 512:(nn + 1) * 512], pv(bki)),
                        reads=[bk], writes=[B_("kp")])
                for nn2 in range(2 if "v" in KP1 else 0):
                    p.dma("g", lambda h, nn2=nn2: h.dma_start(
                        A(wslot1, 0, [[8 * 512, 128], [512, 8], [1, 512]]),
                        A(wv_d.tensor, nn2 * 512, [[1024, 128], [128 * 1024, 8], [1, 512]])),
                        reads=[B_("wslot1")], writes=[B_("wslot1")])
                    for mtl in range(4):
                        mt = nn * 4 + mtl
                        bki = 2 + mtl % 2
                        bk = bank[bki]
                        for kc in range(8):
                            p.op("t", lambda h, mtl=mtl, kc=kc, bki=bki, cx=cx: h.matmul(
                                pv(bki),
                                A(xct, cx * 8 * 512 + kc * 512 + mtl * 128, [[8192, 128], [1, 128]]),
                                wslot1[:, kc, :], start=(kc == 0), stop=(kc == 7)),
                                reads=[B_("wslot1"), xcb], writes=[bk])
                        p.op("s", lambda h, mt=mt, nn2=nn2, bki=bki: h.copy(
                            A(vt, mt * 1040 + nn2 * 8 * 65, [[12 * 1040, 128], [65, 8], [1, 64]]),
                            pv(bki).rearrange("p (a c) -> p a c", a=8)),
                            reads=[bk], writes=[B_("vt")])
            if "vm" in KP1:
                p.op("v", lambda h: h.memset(
                    A(vt, 64, [[12 * 1040, 128], [1040, 12], [65, 16]]), 1.0),
                    writes=[B_("vt")])
            if "qr" not in KP1:
                continue
            p.dma("g", lambda h: h.dma_start(
                A(wslot0, 0, [[8 * 1024, 128], [1024, 8], [1, 1024]]),
                A(wq_d.tensor, 0, [[1024, 128], [128 * 1024, 8], [1, 1024]])),
                reads=[B_("wslot0")], writes=[B_("wslot0")])
            xcb2 = B_("xct0")
            for ft in range(8):
                bki = ft % 2
                bk = bank[bki]
                for kc in range(8):
                    p.op("t", lambda h, ft=ft, kc=kc, bki=bki: h.matmul(
                        pv(bki), wslot0[:, kc, ft * 128:(ft + 1) * 128],
                        A(xct, kc * 512, [[8192, 128], [1, 512]]),
                        start=(kc == 0), stop=(kc == 7)),
                        reads=[B_("wslot0"), xcb2], writes=[bk])
                p.op("s", lambda h, ft=ft, bki=bki: h.activation(
                    q_sb[:, ft, :], pv(bki), AF.Identity,
                    bias=bqc[:, ft:ft + 1], scale=0.125),
                    reads=[bk, B_("bqc")], writes=[B_("q_sb")])
            p.dma("g", lambda h: h.dma_start(
                A(wslot1, 0, [[8 * 512, 128], [512, 8], [1, 512]]),
                A(wrp_d.tensor, 0, [[512, 128], [128 * 512, 8], [1, 512]])),
                reads=[B_("wslot1")], writes=[B_("wslot1")])
            for rt in range(4):
                bki = 2 + rt % 2
                bk = bank[bki]
                for kc in range(8):
                    p.op("t", lambda h, rt=rt, kc=kc, bki=bki: h.matmul(
                        pv(bki), wslot1[:, kc, rt * 128:(rt + 1) * 128],
                        A(xct, kc * 512, [[8192, 128], [1, 512]]),
                        start=(kc == 0), stop=(kc == 7)),
                        reads=[B_("wslot1"), xcb2], writes=[bk])
                p.op("s", lambda h, rt=rt, bki=bki: h.activation(
                    rr[:, rt, :], pv(bki), AF.Identity, bias=brc[:, rt:rt + 1]),
                    reads=[bk, B_("brc")], writes=[B_("rr")])

            if "p2" not in KPH:
                continue
            p.op("v", lambda h: h.memset(qpe[64:128, :], 0.0), writes=[B_("qpxl")])
            p.op("v", lambda h: h.memset(qpo[0:64, :], 0.0), writes=[B_("qpxl")])
            for hh in range(KH_):
                par = hh % 2
                qpad = (qpe, qpo)[par]
                p.dma("g", lambda h, hh=hh: h.dma_start(
                    rst[0:10, :], rr[32 * (hh % 4):32 * (hh % 4) + 10, hh // 4, :]),
                    reads=[B_("rr")], writes=[B_("rst")])
                a0 = 64 * (hh % 2)
                p.op("v", lambda h, hh=hh, qpad=qpad, a0=a0: h.tensor_copy(
                    qpad[a0:a0 + 64, :], q_sb[a0:a0 + 64, hh // 2, :]),
                    reads=[B_("q_sb")], writes=[B_("qpxl")])
                cdb = B_(f"cdram{par}")
                for ct in range(4):
                    cstt = (junk, cst2)[ct % 2]
                    cbf = (B_("junk"), B_("xt0"))[ct % 2]
                    for cc in range(2):
                        bki = 4 + cc
                        bk = bank[bki]
                        p.op("t", lambda h, ct=ct, cc=cc, bki=bki: h.matmul(
                            pv(bki), rst[0:10, ct * 128:(ct + 1) * 128],
                            brev[0:10, cc * 512:(cc + 1) * 512], start=True, stop=True),
                            reads=[B_("rst"), B_("brev")], writes=[bk])
                        p.op("s", lambda h, cstt=cstt, cc=cc, bki=bki: h.copy(
                            cstt[:, cc * 512:(cc + 1) * 512], pv(bki)),
                            reads=[bk], writes=[cbf])
                    p.dma("y", lambda h, par=par, ct=ct, cstt=cstt: h.dma_start(
                        A(cdr.tensor, par * 512 * CROW + ct * 128 * CROW + LPAD,
                          [[CROW, 128], [1, 1024]]), cstt[:, :]),
                        reads=[cbf], writes=[cdb])
                esbb = B_("esb")
                for qt in range(4):
                    off = par * 512 * CROW + qt * 128 * CROW + 511
                    p.dma("y", lambda h, qt=qt, off=off: h.dma_start(
                        esb[:, qt, :], A(cdr.tensor, off, [[CROW - 1, 128], [1, 1152]])),
                        reads=[cdb, B_(f"cpad{par}")], writes=[esbb])
                if KDBG == 1 and b == 0 and hh == 0:
                    p.dma("y", lambda h: h.dma_start(dbgE, esb[:, :, :]),
                          reads=[esbb], writes=[B_("dbgE")])
                p.op("t", lambda h: h.matmul(pbig[3][:, 0:260], zero128[:, :],
                     vt[:, 0, 0:260], start=True, stop=False),
                     reads=[B_("zero128"), B_("vt")], writes=[bank[6], bank[7]])
                wsbb = B_("wsb")

                def emit_qke(kb, hh, qpad):
                    sj = (kb // 2) % 2
                    bki2 = 2 * sj + (kb % 2)
                    bk = bank[bki2]
                    qts = [qt for qt in range(4) if 0 <= kb - qt <= 8]
                    p.op("t", lambda h, hh=hh, kb=kb, bki2=bki2, qpad=qpad: h.matmul(
                        pv(bki2), kp[:, hh // 2, kb * 128:(kb + 1) * 128],
                        qpad[:, :], start=True, stop=False),
                        reads=[B_("kp"), B_("qpxl")], writes=[bk])
                    if kb < 8:
                        p.op("t", lambda h, kb=kb, bki2=bki2: h.matmul(
                            pv(bki2), edge_b[0:1, kb * 128:(kb + 1) * 128],
                            ones_row[0:1, :], start=False, stop=False),
                            reads=[B_("edge_b"), B_("ones_row")], writes=[bk])
                    for i, qt in enumerate(qts):
                        p.op("t", lambda h, qt=qt, kb=kb, bki2=bki2,
                             last=(i == len(qts) - 1): h.matmul(
                            pv(bki2, qt * 128, (qt + 1) * 128),
                            esb[:, qt, (kb - qt) * 128:(kb - qt) * 128 + 128], ident[:, :],
                            start=False, stop=last),
                            reads=[esbb, B_("ident")], writes=[bk])

                def emit_exp(kb2):
                    sj = kb2 % 2
                    p.op("s", lambda h, kb2=kb2, sj=sj: h.activation(
                        A(wsb, (kb2 * 2 % 4) * 512, [[4 * 512, 128], [1, 1024]]),
                        pbig[sj][:, :], AF.Exp),
                        reads=[bank[2 * sj], bank[2 * sj + 1]], writes=[wsbb])

                def emit_av(kb, hh):
                    qts = [qt for qt in range(4) if 0 <= kb - qt <= 8]
                    for i, qt in enumerate(qts):
                        p.op("t", lambda h, hh=hh, kb=kb, qt=qt,
                             last=(kb == NKB - 1 and qt == 3): h.matmul(
                            pbig[3][:, qt * 65:qt * 65 + 65],
                            wsb[:, kb % 4, qt * 128:(qt + 1) * 128],
                            vt[:, kb, 65 * hh:65 * hh + 65],
                            start=False, stop=last),
                            reads=[wsbb, B_("vt")], writes=[bank[6], bank[7]])

                for kb2 in range(6):
                    emit_qke(2 * kb2, hh, qpad)
                    emit_qke(2 * kb2 + 1, hh, qpad)
                    emit_exp(kb2)
                    if kb2 >= 1:
                        emit_av(2 * (kb2 - 1), hh)
                        emit_av(2 * (kb2 - 1) + 1, hh)
                emit_av(10, hh)
                emit_av(11, hh)
                if KDBG == 1 and b == 0 and hh == 0:
                    p.dma("y", lambda h: h.dma_start(dbgW, wsb[:, :, :]),
                          reads=[wsbb], writes=[B_("dbgW")])
                if KDBG == 1 and b == 0 and hh == H - 1:
                    pass
                    p.op("s", lambda h: h.copy(junk[:, 0:260].bitcast(F32) if False else xt0[:, 0:260], pbig[3][:, 0:260]),
                         reads=[bank[6], bank[7]], writes=[B_("xt0")])
                    p.dma("y", lambda h: h.dma_start(dbgA, xt0[:, 0:260]),
                          reads=[B_("xt0")], writes=[B_("dbgA")])
                for qt in range(4):
                    p.op("s", lambda h, qt=qt: h.copy(
                        cols[:, 44 + qt:45 + qt], pbig[3][:, qt * 65 + 64:qt * 65 + 65]),
                        reads=[bank[6], bank[7]], writes=[B_("colD")])
                p.op("v", lambda h: h.reciprocal(cols[:, 32:36], cols[:, 44:48]),
                     reads=[B_("colD")], writes=[B_("colD")])
                for qt in range(4):
                    p.op("s", lambda h, qt=qt, hh=hh: h.activation(
                        asb[:, qt, 64 * hh:64 * hh + 64], pbig[3][:, qt * 65:qt * 65 + 64],
                        AF.Identity, scale=cols[:, 32 + qt:33 + qt]),
                        reads=[bank[6], bank[7], B_("colD")], writes=[B_("asb")])

            if KDBG == 1 and b == 0:
                p.dma("y", lambda h: h.dma_start(dbgS, asb[:, :, :]),
                      reads=[B_("asb")], writes=[B_("dbgS")])
            if KDBG == 2 and b == 0:
                p.dma("y", lambda h: h.dma_start(dbgR, x_tc[:, 0:4, :]),
                      reads=[B_("x_tc")], writes=[B_("dbgR")])
            if "p3" not in KPH:
                continue
            p.dma("g", lambda h: h.dma_start(
                A(wslot0, 0, [[8 * 1024, 128], [1024, 8], [1, 1024]]),
                A(wpj_d.tensor, 0, [[1024, 128], [128 * 1024, 8], [1, 1024]])),
                reads=[B_("wslot0")], writes=[B_("wslot0")])
            for hc in range(8):
                bki = 4 + hc % 2
                bk = bank[bki]
                for qt in range(4):
                    p.op("t", lambda h, hc=hc, qt=qt, bki=bki: h.matmul(
                        pv(bki, qt * 128, (qt + 1) * 128),
                        asb[:, qt, hc * 128:(hc + 1) * 128], ident[:, :],
                        start=True, stop=True),
                        reads=[B_("asb"), B_("ident")], writes=[bk])
                p.op("s", lambda h, hc=hc, bki=bki: h.copy(
                    act_[:, hc, :], pv(bki)), reads=[bk], writes=[B_("esb")])
            if KDBG == 2 and b == 0:
                p.dma("y", lambda h: h.dma_start(dbgT, act_[:, :, :]),
                      reads=[B_("esb")], writes=[B_("dbgT")])
            for jt in range(4):
                for cc in range(2):
                    bki = (jt * 2 + cc) % 2
                    bk = bank[bki]
                    for hc in range(8):
                        p.op("t", lambda h, jt=jt, cc=cc, hc=hc, bki=bki: h.matmul(
                            pv(bki), act_[:, hc, jt * 128:(jt + 1) * 128],
                            wslot0[:, hc, cc * 512:(cc + 1) * 512],
                            start=(hc == 0), stop=False),
                            reads=[B_("esb"), B_("wslot0")], writes=[bk])
                    p.op("t", lambda h, cc=cc, bki=bki: h.matmul(
                        pv(bki), ones_row[0:1, 0:128], bpj[0:1, cc * 512:(cc + 1) * 512],
                        start=False, stop=True),
                        reads=[B_("ones_row"), B_("bpj")], writes=[bk])
                    j = b * 4 + jt
                    p.op("s", lambda h, cc=cc, bki=bki: h.copy(
                        xt1[:, bki * 512:(bki + 1) * 512], pv(bki)),
                        reads=[bk], writes=[B_("xt1")])
                    p.op("v", lambda h, j=j, cc=cc, bki=bki: h.tensor_tensor(
                        x_tc[:, j, cc * 512:(cc + 1) * 512],
                        x_tc[:, j, cc * 512:(cc + 1) * 512],
                        xt1[:, bki * 512:(bki + 1) * 512], ALU.add),
                        reads=[B_("xt1"), B_("x_tc")], writes=[B_("x_tc")])
            if KDBG == 2 and b == 0:
                p.dma("y", lambda h: h.dma_start(dbgO, x_tc[:, 0:4, :]),
                      reads=[B_("x_tc")], writes=[B_("dbgO")])

    def plan_ffn(call):
        wsc = B_("wscal")
        cs = B_("colS")
        cu = B_("colU")
        for g in range(2):
            tiles = list(range(8 * g, 8 * g + 8))
            for ii, jt in enumerate(tiles):
                k = jt % 2
                xq_, xi_ = xql[k], xqi[k]
                xqlb, xqib = B_(f"xql{k}"), B_(f"xqi{k}")
                ci = k * 8
                cb = B_(f"colL{k}")
                ln_stats(x_tc[:, jt, :], B_("x_tc"), ci, cb)
                p.op("s", lambda h, jt=jt, xq_=xq_, ci=ci: h.activation(
                    xq_[:, :], x_tc[:, jt, :], AF.Identity,
                    bias=cols[:, ci + 1:ci + 2], scale=cols[:, ci:ci + 1]),
                    reads=[B_("x_tc"), cb], writes=[xqlb])
                p.op("v", lambda h, xq_=xq_, ci=ci: h.reduce_max(
                    cols[:, ci + 5:ci + 6], xq_[:, :], axis=AX.X,
                    apply_absolute_value=True), reads=[xqlb], writes=[cb])
                p.op("v", lambda h, ci=ci: h.tensor_scalar(cols[:, ci + 5:ci + 6],
                     cols[:, ci + 5:ci + 6], 1e-5, None, ALU.max), reads=[cb], writes=[cb])
                p.op("v", lambda h, ci=ci, ii=ii: h.tensor_scalar(cols[:, 40 + ii:41 + ii],
                     cols[:, ci + 5:ci + 6], 1.0 / 127.0, None, ALU.mult),
                     reads=[cb], writes=[cs])
                p.op("v", lambda h, ci=ci: h.reciprocal(cols[:, ci + 6:ci + 7],
                     cols[:, ci + 5:ci + 6]), reads=[cb], writes=[cb])
                p.op("v", lambda h, ci=ci: h.tensor_scalar(cols[:, ci + 6:ci + 7],
                     cols[:, ci + 6:ci + 7], 127.0, None, ALU.mult), reads=[cb], writes=[cb])
                for ch in range(2):
                    sl = slice(ch * 512, (ch + 1) * 512)
                    rnd_int8(xq_[:, sl], [xqlb], cols[:, ci + 6:ci + 7], cb,
                             xi_[:, sl], xqib, ch)
                for half in range(2):
                    bki = 4 + (ii + half) % 2
                    bk = bank[bki]
                    for kc in range(4):
                        kg = half * 4 + kc
                        p.op("t", lambda h, xi_=xi_, kg=kg, kc=kc, bki=bki: h.matmul(
                            pv(bki, kc * 128, (kc + 1) * 128),
                            xi_[:, kg * 128:(kg + 1) * 128], ident[:, :],
                            start=True, stop=True),
                            reads=[xqib, B_("ident")], writes=[bk])
                    p.op("s", lambda h, ii=ii, half=half, bki=bki: h.copy(
                        A(xqc, ii * 1024 + half * 512, [[8192, 128], [1, 512]]),
                        pv(bki)), reads=[bk], writes=[B_("xqc")])
                p.op("v", lambda h, ii=ii: h.tensor_tensor(cols[:, 48 + ii:49 + ii],
                     cols[:, 40 + ii:41 + ii], wscal[:, 0:1], ALU.mult),
                     reads=[cs, wsc], writes=[cs])
                p.op("v", lambda h, ii=ii: h.tensor_tensor(cols[:, 24 + ii:25 + ii],
                     cols[:, 40 + ii:41 + ii], wscal[:, 1:2], ALU.mult),
                     reads=[cs, wsc], writes=[cu])
            for oc in range(8):
                k = oc % 2
                wg_s = wfs[k]
                wu_s = wfs[2]
                wgb, wub = B_(f"wfs{k}"), B_("wfs2")
                p.dma("g", lambda h, oc=oc, wg_s=wg_s: h.dma_start(
                    A(wg_s, 0, [[8 * 512, 128], [512, 8], [1, 512]]),
                    A(wgQ.tensor, oc * 512, [[INNER, 128], [128 * INNER, 8], [1, 512]])),
                    reads=[wgb], writes=[wgb])
                p.dma("g", lambda h, oc=oc, wu_s=wu_s: h.dma_start(
                    A(wu_s, 0, [[8 * 512, 128], [512, 8], [1, 512]]),
                    A(wuQ.tensor, oc * 512, [[INNER, 128], [128 * INNER, 8], [1, 512]])),
                    reads=[wub], writes=[wub])
                for ii in range(8):
                    bg = ii % 2
                    bu = 2 + ii % 2
                    for kc in range(8):
                        p.op("t", lambda h, ii=ii, kc=kc, wg_s=wg_s, bg=bg: h.matmul(
                            pv(bg), A(xqc, ii * 1024 + kc * 128, [[8192, 128], [1, 128]]),
                            wg_s[:, kc, :], start=(kc == 0), stop=(kc == 7)),
                            reads=[B_("xqc"), wgb], writes=[bank[bg]])
                    for kc in range(8):
                        p.op("t", lambda h, ii=ii, kc=kc, wu_s=wu_s, bu=bu: h.matmul(
                            pv(bu), A(xqc, ii * 1024 + kc * 128, [[8192, 128], [1, 128]]),
                            wu_s[:, kc, :], start=(kc == 0), stop=(kc == 7)),
                            reads=[B_("xqc"), wub], writes=[bank[bu]])
                    gb_, ub_ = gbuf[ii % 2], gbuf[2]
                    gbb, ubb = B_(f"gbf{ii % 2}"), B_("gbf2")
                    p.op("s", lambda h, gb_=gb_, ii=ii, bg=bg: h.activation(
                        gb_[:, :], pv(bg), AF.Silu, scale=cols[:, 48 + ii:49 + ii]),
                        reads=[bank[bg], cs], writes=[gbb])
                    p.op("s", lambda h, ub_=ub_, ii=ii, bu=bu: h.activation(
                        ub_[:, :], pv(bu), AF.Identity, scale=cols[:, 24 + ii:25 + ii]),
                        reads=[bank[bu], cu], writes=[ubb])
                    p.op("v", lambda h, gb_=gb_, ub_=ub_, ii=ii, oc=oc: h.tensor_tensor(
                        hb[:, ii, oc * 512:(oc + 1) * 512], gb_[:, :], ub_[:, :], ALU.mult),
                        reads=[gbb, ubb], writes=[B_("hb")])
            for ii in range(8):
                hm = 16 + (ii % 2) * 4
                cb = B_(f"colH{ii % 2}")
                p.op("v", lambda h, ii=ii, hm=hm: h.reduce_max(cols[:, hm:hm + 1],
                     hb[:, ii, :], axis=AX.X, apply_absolute_value=True),
                     reads=[B_("hb")], writes=[cb])
                p.op("v", lambda h, hm=hm: h.tensor_scalar(cols[:, hm:hm + 1],
                     cols[:, hm:hm + 1], 1e-5, None, ALU.max), reads=[cb], writes=[cb])
                p.op("v", lambda h, hm=hm, ii=ii: h.tensor_scalar(cols[:, 56 + ii:57 + ii],
                     cols[:, hm:hm + 1], 1.0 / 127.0, None, ALU.mult), reads=[cb], writes=[cs])
                p.op("v", lambda h, ii=ii: h.tensor_tensor(cols[:, 56 + ii:57 + ii],
                     cols[:, 56 + ii:57 + ii], wscal[:, 2:3], ALU.mult),
                     reads=[cs, wsc], writes=[cs])
                p.op("v", lambda h, hm=hm: h.reciprocal(cols[:, hm + 1:hm + 2],
                     cols[:, hm:hm + 1]), reads=[cb], writes=[cb])
                p.op("v", lambda h, hm=hm: h.tensor_scalar(cols[:, hm + 1:hm + 2],
                     cols[:, hm + 1:hm + 2], 127.0, None, ALU.mult), reads=[cb], writes=[cb])
                for ch in range(8):
                    sl = slice(ch * 512, (ch + 1) * 512)
                    rnd_int8(hb[:, ii, sl], [B_("hb")], cols[:, hm + 1:hm + 2], cb,
                             hb[:, ii, sl], B_("hb"), ch)
            for half in range(2):
                p.barrier(dma=KDMA)
                for ii in range(4):
                    ti = half * 4 + ii
                    for qtr in range(8):
                        bki = 4 + (ii * 8 + qtr) % 2
                        bk = bank[bki]
                        for kc in range(4):
                            oc32 = qtr * 4 + kc
                            p.op("t", lambda h, ti=ti, oc32=oc32, kc=kc, bki=bki: h.matmul(
                                pv(bki, kc * 128, (kc + 1) * 128),
                                hb[:, ti, oc32 * 128:(oc32 + 1) * 128], ident[:, :],
                                start=True, stop=True),
                                reads=[B_("hb"), B_("ident")], writes=[bk])
                        p.op("s", lambda h, ii=ii, qtr=qtr, bki=bki: h.copy(
                            A(hot, (ii * 32 + qtr * 4) * 128, [[16384, 128], [1, 512]]),
                            pv(bki)), reads=[bk], writes=[B_("hot")])
                for cc in range(2):
                    for oc32 in range(32):
                        k = oc32 % 2
                        wd_s = wdfs[k]
                        wdb = B_(f"wdfs{k}")
                        p.dma("g", lambda h, cc=cc, oc32=oc32, wd_s=wd_s: h.dma_start(
                            wd_s[:, :],
                            A(wdQ.tensor, oc32 * 128 * E + cc * 512, [[E, 128], [1, 512]])),
                            reads=[wdb], writes=[wdb])
                        for ii in range(4):
                            p.op("t", lambda h, ii=ii, oc32=oc32, wd_s=wd_s: h.matmul(
                                pv(ii), A(hot, (ii * 32 + oc32) * 128, [[16384, 128], [1, 128]]),
                                wd_s[:, :], start=(oc32 == 0), stop=(oc32 == 31)),
                                reads=[B_("hot"), wdb], writes=[bank[ii]])
                    for ii in range(4):
                        ti = half * 4 + ii
                        jt = tiles[ti]
                        gb_ = gbuf[ii % 2]
                        gbb = B_(f"gbf{ii % 2}")
                        p.op("s", lambda h, ti=ti, ii=ii, gb_=gb_: h.activation(
                            gb_[:, :], pv(ii), AF.Identity, scale=cols[:, 56 + ti:57 + ti]),
                            reads=[bank[ii], cs], writes=[gbb])
                        p.op("v", lambda h, jt=jt, cc=cc, gb_=gb_: h.tensor_tensor(
                            x_tc[:, jt, cc * 512:(cc + 1) * 512],
                            x_tc[:, jt, cc * 512:(cc + 1) * 512], gb_[:, :], ALU.add),
                            reads=[gbb, B_("x_tc")], writes=[B_("x_tc")])
                p.barrier(dma=KDMA)

    def plan_attnb():
        KSL = [0, 2, 3, 1]
        kb_ = B_("kcv")
        def kv(et, s):
            return A(kcv, et * 4 + s, [[32, 128], [1, 1]])
        kall = lambda s: A(kcv, s, [[32, 128], [4, 8], [1, 1]])
        p.op("s", lambda h: h.activation(kall(0), gwc[:, :], AF.Sigmoid),
             reads=[B_("gwc")], writes=[kb_])
        p.op("v", lambda h: h.tensor_scalar(kall(1), kall(0), -1.0, 1.0, ALU.mult, ALU.add),
             reads=[kb_], writes=[kb_])
        p.op("v", lambda h: h.tensor_tensor(kall(2), kall(0), kall(1), ALU.mult),
             reads=[kb_], writes=[kb_])
        p.op("v", lambda h: h.tensor_tensor(kall(3), kall(2), kall(1), ALU.mult),
             reads=[kb_], writes=[kb_])
        p.op("v", lambda h: h.tensor_tensor(kall(1), kall(3), kall(1), ALU.mult),
             reads=[kb_], writes=[kb_])
        p.dma("g", lambda h: h.dma_start(bout[:, :], bout_d), writes=[B_("bout")])
        p.dma("g", lambda h: h.dma_start(
            A(wib, 0, [[8 * 1024, 128], [1024, 8], [1, 1024]]),
            A(winT_d.tensor, 0, [[1024, 128], [128 * 1024, 8], [1, 1024]])),
            writes=[B_("wib")])
        p.dma("g", lambda h: h.dma_start(
            A(wob, 0, [[8 * 1024, 128], [1024, 8], [1, 1024]]),
            A(woutT_d.tensor, 0, [[1024, 128], [128 * 1024, 8], [1, 1024]])),
            writes=[B_("wob")])
        for b in range(B):
            for tt in range(4):
                jt = b * 4 + tt
                k = tt % 2
                ci = k * 8
                cb = B_(f"colL{k}")
                ln_stats(x_tc[:, jt, :], B_("x_tc"), ci, cb)
                p.op("s", lambda h, jt=jt, ci=ci: h.activation(
                    xltb2[:, :], x_tc[:, jt, :], AF.Identity,
                    bias=cols[:, ci + 1:ci + 2], scale=cols[:, ci:ci + 1]),
                    reads=[B_("x_tc"), cb], writes=[B_("xltb2")])
                for half in range(2):
                    bki = 4 + (tt + half) % 2
                    bk = bank[bki]
                    for kc in range(4):
                        kg = half * 4 + kc
                        p.op("t", lambda h, kg=kg, kc=kc, bki=bki: h.matmul(
                            pv(bki, kc * 128, (kc + 1) * 128),
                            xltb2[:, kg * 128:(kg + 1) * 128], ident[:, :],
                            start=True, stop=True),
                            reads=[B_("xltb2"), B_("ident")], writes=[bk])
                    p.op("s", lambda h, tt=tt, half=half, bki=bki: h.copy(
                        A(xctb, half * 4 * 512 + tt * 128, [[8 * 512, 128], [512, 4], [1, 128]]),
                        pv(bki).rearrange("p (a c) -> p a c", a=4)),
                        reads=[bk], writes=[B_("xctb")])
            for et in range(8):
                bki = et % 2
                bk = bank[bki]
                for kc in range(8):
                    p.op("t", lambda h, et=et, kc=kc, bki=bki: h.matmul(
                        pv(bki), wib[:, kc, et * 128:(et + 1) * 128],
                        xctb[:, kc, :], start=(kc == 0), stop=(kc == 7)),
                        reads=[B_("wib"), B_("xctb")], writes=[bk])
                p.op("s", lambda h, et=et, b=b, bki=bki: h.activation(
                    ub[:, et, b * 512:(b + 1) * 512], pv(bki), AF.Silu,
                    bias=binc[:, et:et + 1]), reads=[bk, B_("binc")], writes=[B_("ub")])
        for b in range(B):
            for et in range(8):
                if b == 0:
                    p.op("v", lambda h, et=et: h.tensor_scalar(
                        vbb[:, et, :], ub[:, et, 0:512], kv(et, KSL[0]), None, ALU.mult),
                        reads=[B_("ub"), kb_], writes=[B_("vbb")])
                else:
                    p.op("v", lambda h, et=et, b=b: h.tensor_scalar(
                        cvt0[:, :], ub[:, et, 0:512], kv(et, KSL[b]), None, ALU.mult),
                        reads=[B_("ub"), kb_], writes=[B_("cvt0")])
                    for s_ in range(1, b):
                        p.op("v", lambda h, et=et, b=b, s_=s_: h.scalar_tensor_tensor(
                            cvt0[:, :], ub[:, et, s_ * 512:(s_ + 1) * 512],
                            kv(et, KSL[b - s_]), cvt0[:, :], ALU.mult, ALU.add),
                            reads=[B_("ub"), kb_, B_("cvt0")], writes=[B_("cvt0")])
                    p.op("v", lambda h, et=et, b=b: h.scalar_tensor_tensor(
                        vbb[:, et, :], ub[:, et, b * 512:(b + 1) * 512], kv(et, KSL[0]),
                        cvt0[:, :], ALU.mult, ALU.add),
                        reads=[B_("ub"), kb_, B_("cvt0")], writes=[B_("vbb")])
            for jt4 in range(4):
                for cc in range(2):
                    bki = (jt4 * 2 + cc) % 2
                    bk = bank[bki]
                    for dc in range(8):
                        p.op("t", lambda h, jt4=jt4, cc=cc, dc=dc, bki=bki: h.matmul(
                            pv(bki), vbb[:, dc, jt4 * 128:(jt4 + 1) * 128],
                            wob[:, dc, cc * 512:(cc + 1) * 512],
                            start=(dc == 0), stop=False),
                            reads=[B_("vbb"), B_("wob")], writes=[bk])
                    p.op("t", lambda h, cc=cc, bki=bki: h.matmul(
                        pv(bki), ones_row[0:1, 0:128], bout[0:1, cc * 512:(cc + 1) * 512],
                        start=False, stop=True),
                        reads=[B_("ones_row"), B_("bout")], writes=[bk])
                    ttb_ = tts[bki]
                    ttbb = B_(f"tts{bki}")
                    p.op("s", lambda h, ttb_=ttb_, bki=bki: h.activation(
                        ttb_[:, :], pv(bki), AF.Tanh), reads=[bk], writes=[ttbb])
                    j = b * 4 + jt4
                    p.op("v", lambda h, j=j, cc=cc, ttb_=ttb_: h.tensor_tensor(
                        x_tc[:, j, cc * 512:(cc + 1) * 512],
                        x_tc[:, j, cc * 512:(cc + 1) * 512], ttb_[:, :], ALU.add),
                        reads=[ttbb, B_("x_tc")], writes=[B_("x_tc")])

    import os as _os2
    KBAR = _os2.environ.get("KBAR", "1") == "1"
    KDMA = _os2.environ.get("KDMA", "1") == "1"
    if "wq" in stages:
        plan_wquant()
    if "attn" in stages:
        plan_attn()
    if KBAR:
        p.barrier(dma=KDMA)
    if "ffn1" in stages:
        plan_ffn(0)
    if KBAR:
        p.barrier(dma=KDMA)
    if "attnb" in stages:
        plan_attnb()
    if KBAR:
        p.barrier(dma=KDMA)
    if "ffn2" in stages:
        plan_ffn(1)

    p.dma("y", lambda h: h.dma_start(
        A(out_d.tensor, 0, [[1024, 128], [128 * 1024, 16], [1, 1024]]),
        A(x_tc, 0, [[16 * 1024, 128], [1024, 16], [1, 1024]])),
        reads=[B_("x_tc")], writes=[B_("out")])

    def final_waits(h):
        for name, val in p.dma_cnt.items():
            h.wait_ge(p.sems[name], val)
        return h.nop()
    p.q["y"].append(final_waits)

    p.assign_values()
    globals()["LAST_P"] = p
    with nc.Block() as block:
        @block.tensor
        def _(h):
            for f in p.q["t"]:
                f(h)

        @block.vector
        def _(h):
            for f in p.q["v"]:
                f(h)

        @block.scalar
        def _(h):
            for f in p.q["s"]:
                f(h)

        @block.sync
        def _(h):
            for f in p.q["y"]:
                f(h)

        @block.gpsimd
        def _(h):
            for f in p.q["g"]:
                f(h)

    stack.close()
    return nc


def _stage_inputs(kw):
    X = np.ascontiguousarray(np.asarray(kw["L0_attn_a_X"], np.float32))
    wq = np.ascontiguousarray(np.asarray(kw["L0_attn_a_Wq"], np.float32))
    wk = np.ascontiguousarray(np.asarray(kw["L0_attn_a_Wk"], np.float32))
    wv = np.ascontiguousarray(np.asarray(kw["L0_attn_a_Wv"], np.float32))
    wr = np.asarray(kw["L0_attn_a_Wr"], np.float32)
    wrp = np.zeros((E, 512), np.float32)
    for hh in range(H):
        wrp[:, 32 * hh:32 * hh + 10] = wr[:, 10 * hh:10 * hh + 10]
    wpj = np.ascontiguousarray(np.asarray(kw["L0_attn_a_Wproj"], np.float32))
    bq = np.asarray(kw["L0_attn_a_bq"], np.float32)
    bqc = np.ascontiguousarray(bq.reshape(8, 128).T)
    br = np.asarray(kw["L0_attn_a_br"], np.float32)
    brp = np.zeros((512,), np.float32)
    for hh in range(H):
        brp[32 * hh:32 * hh + 10] = br[10 * hh:10 * hh + 10]
    brc = np.ascontiguousarray(brp.reshape(4, 128).T)
    bpj = np.ascontiguousarray(np.asarray(kw["L0_attn_a_bproj"], np.float32).reshape(1, E))
    bndr = np.ascontiguousarray(np.asarray(kw["L0_attn_a_b_nd"], np.float32)[:, ::-1])
    winT = np.ascontiguousarray(np.asarray(kw["L0_attn_b_W_in"], np.float32).T)
    woutT = np.ascontiguousarray(np.asarray(kw["L0_attn_b_W_out"], np.float32).T)
    binc = np.ascontiguousarray(np.asarray(kw["L0_attn_b_b_in"], np.float32).reshape(8, 128).T)
    bout = np.ascontiguousarray(np.asarray(kw["L0_attn_b_b_out"], np.float32).reshape(1, E))
    gwc = np.ascontiguousarray(np.asarray(kw["L0_attn_b_weight"], np.float32).reshape(8, 128).T)
    wgT = np.ascontiguousarray(np.asarray(kw["L0_ffn_w_gate"], np.float32).T)
    wuT = np.ascontiguousarray(np.asarray(kw["L0_ffn_w_up"], np.float32).T)
    wdT = np.ascontiguousarray(np.asarray(kw["L0_ffn_w_down"], np.float32).T)
    idm = np.eye(128, dtype=bfd)

    maps = []
    for c in range(NCORE):
        lo = 512 * c - 1024
        hi = 512 * c + 512
        xs = np.zeros((B, SPAN, E), np.float32)
        s0 = max(0, lo)
        xs[:, s0 - lo:, :] = X[:, s0:hi, :]
        edge = np.zeros((1, 1024), np.float32)
        npad = max(0, -lo)
        edge[0, :npad] = NEG
        maps.append(dict(
            xsp=np.ascontiguousarray(xs), wq=wq, wk=wk, wv=wv, wrp=wrp, wpj=wpj,
            bqc=bqc, brc=brc, bpj=bpj, bndr=bndr, edge=np.ascontiguousarray(edge),
            winT=winT, woutT=woutT, binc=binc, bout=bout, gwc=gwc,
            wgT=wgT, wuT=wuT, wdT=wdT, idm=idm,
        ))
    return maps


def kernel(**kw):
    maps = _stage_inputs(kw)
    nc = build_program()
    res = run_bass_kernel_spmd(nc, maps, core_ids=list(range(NCORE)))
    out = np.empty((B, T, E), np.float32)
    for c in range(NCORE):
        o = np.asarray(res.results[c]["out"]).reshape(16, 128, E)
        for b in range(B):
            out[b, 512 * c:512 * (c + 1), :] = o[4 * b:4 * b + 4].reshape(512, E)
    return out


# version 07 (final)
# speedup vs baseline: 1.1069x; 1.0852x over previous
import numpy as np
import ml_dtypes
import contextlib
import concourse.bass as bass
import concourse.mybir as mybir
from concourse.bass_utils import run_bass_kernel_spmd

F32 = mybir.dt.float32
BF16 = mybir.dt.bfloat16
I32 = mybir.dt.int32
AF = mybir.ActivationFunctionType
ALU = mybir.AluOpType
AX = mybir.AxisListType
bfd = ml_dtypes.bfloat16

NCORE = 8
B = 4
T = 4096
E = 1024
H = 16
NB = 10
SPAN = 1536
NKB = 12
NEG = -1.0e9
INNER = 4096
CROW = 2048
LPAD = 512
OWN0 = 1024

ENGS = ("t", "v", "s", "y", "g")


class Cell:
    __slots__ = ("eng", "idx", "inc", "val")
    def __init__(self, eng, idx):
        self.eng = eng; self.idx = idx; self.inc = False; self.val = None


class DmaTok:
    __slots__ = ("sem", "val")
    def __init__(self, sem, val):
        self.sem = sem; self.val = val


class Buf:
    __slots__ = ("name", "w", "readers")
    def __init__(self, name):
        self.name = name; self.w = None; self.readers = []


class P:
    def __init__(self, nc):
        self.nc = nc
        self.log = {}
        self.q = {e: [] for e in ENGS}
        self.opidx = {e: 0 for e in ENGS}
        self.lastdrain = {e: -1 for e in ENGS}
        self.wm = {}
        self.dma_wm = {}
        self.cells = {e: [] for e in ENGS}
        self.sems = {}
        self.dma_cnt = {}
        self.dma_names = {"y": ["dy0", "dy1"], "g": ["dg0", "dg1"], "s": ["ds0", "ds1"]}
        self.dma_rr = {"y": 0, "g": 0, "s": 0}
        self.bufs = {}

    def B(self, name):
        if name not in self.bufs:
            self.bufs[name] = Buf(name)
        return self.bufs[name]

    def _await(self, eng, tk, raw=True):
        if isinstance(tk, DmaTok):
            key = (eng, tk.sem)
            if self.dma_wm.get(key, 0) >= tk.val:
                return
            self.dma_wm[key] = tk.val
            sem_name, val = tk.sem, tk.val
            self.log.setdefault(eng, []).append(("waitD", sem_name, val))
            self.q[eng].append(lambda h: h.wait_ge(self.sems[sem_name], val))
        elif tk.eng == eng:
            if eng == "t" or not raw:
                return
            if tk.idx > self.lastdrain[eng]:
                self.log.setdefault(eng, []).append(("drain",))
                self.q[eng].append(lambda h: h.drain())
                self.lastdrain[eng] = self.opidx[eng] - 1
        else:
            tk.inc = True
            key = (eng, tk.eng)
            if self.wm.get(key, -1) >= tk.idx:
                return
            self.wm[key] = tk.idx
            peng = tk.eng
            self.log.setdefault(eng, []).append(("waitE", peng, tk))
            self.q[eng].append(lambda h, tk=tk, peng=peng: h.wait_ge(self.sems[peng], tk.val))

    def _deps(self, eng, reads, writes):
        for tk in getattr(self, "fence", ()):
            self._await(eng, tk)
        for b in reads:
            if b.w is not None:
                self._await(eng, b.w)
        for b in writes:
            if b.w is not None:
                self._await(eng, b.w, raw=False)
            for tk in b.readers:
                self._await(eng, tk)

    def op(self, eng, emitfn, reads=(), writes=()):
        self._deps(eng, reads, writes)
        cell = Cell(eng, self.opidx[eng])
        def run(h, cell=cell, emitfn=emitfn, self_=self):
            ins = emitfn(h)
            if cell.inc:
                ins.then_inc(self_.sems[cell.eng], 1)
        self.q[eng].append(run)
        self.cells[eng].append(cell)
        import traceback as _tb
        fr = _tb.extract_stack()[-2]
        self.log.setdefault(eng, []).append(("op", cell, f"{fr.name}:{fr.lineno}",
                                             [b.name for b in reads], [b.name for b in writes]))
        self.opidx[eng] += 1
        for b in writes:
            b.w = cell; b.readers = []
        for b in reads:
            b.readers.append(cell)
        return cell

    def dma(self, qeng, emitfn, reads=(), writes=()):
        self._deps(qeng, reads, writes)
        name = self.dma_names[qeng][self.dma_rr[qeng]]
        self.dma_rr[qeng] ^= 1
        val = self.dma_cnt.get(name, 0) + 16
        self.dma_cnt[name] = val
        tok = DmaTok(name, val)
        def run(h, emitfn=emitfn, name=name, self_=self):
            ins = emitfn(h)
            ins.then_inc(self_.sems[name], 16)
        self.q[qeng].append(run)
        self.log.setdefault(qeng, []).append(("dma", name, val,
                                              [b.name for b in reads], [b.name for b in writes]))
        self.opidx[qeng] += 1
        for b in writes:
            b.w = tok; b.readers = []
        for b in reads:
            b.readers.append(tok)
        return tok

    def barrier(self, dma=True):
        best = {}
        for b in self.bufs.values():
            toks = list(b.readers)
            if b.w is not None:
                toks.append(b.w)
            for tk in toks:
                if isinstance(tk, DmaTok):
                    key = ("d", tk.sem)
                    cur = best.get(key)
                    if cur is None or tk.val > cur.val:
                        best[key] = tk
                else:
                    key = ("e", tk.eng)
                    cur = best.get(key)
                    if cur is None or tk.idx > cur.idx:
                        best[key] = tk
        for sem, val in self.dma_cnt.items():
            key = ("d", sem)
            cur = best.get(key)
            if cur is None or val > cur.val:
                best[key] = DmaTok(sem, val)
        self.fence = list(best.values())

    def assign_values(self):
        for e in ENGS:
            v = 0
            for c in self.cells[e]:
                if c.inc:
                    v += 1
                    c.val = v


def build_program(stages=("wq", "attn", "ffn1", "attnb", "ffn2")):
    nc = bass.Bass("TRN2", target_bir_lowering=False, detect_race_conditions=False,
                   dynamic_dma_scratch_size=16384)

    xsp = nc.dram_tensor("xsp", [B, SPAN, E], F32, kind="ExternalInput").ap()
    wq_d = nc.dram_tensor("wq", [E, E], F32, kind="ExternalInput").ap()
    wk_d = nc.dram_tensor("wk", [E, E], F32, kind="ExternalInput").ap()
    wv_d = nc.dram_tensor("wv", [E, E], F32, kind="ExternalInput").ap()
    wrp_d = nc.dram_tensor("wrp", [E, 512], F32, kind="ExternalInput").ap()
    wpj_d = nc.dram_tensor("wpj", [E, E], F32, kind="ExternalInput").ap()
    bqc_d = nc.dram_tensor("bqc", [128, 8], F32, kind="ExternalInput").ap()
    brc_d = nc.dram_tensor("brc", [128, 4], F32, kind="ExternalInput").ap()
    bpj_d = nc.dram_tensor("bpj", [1, E], F32, kind="ExternalInput").ap()
    bndr_d = nc.dram_tensor("bndr", [NB, 1024], F32, kind="ExternalInput").ap()
    edge_d = nc.dram_tensor("edge", [1, 1024], F32, kind="ExternalInput").ap()
    winT_d = nc.dram_tensor("winT", [E, E], F32, kind="ExternalInput").ap()
    woutT_d = nc.dram_tensor("woutT", [E, E], F32, kind="ExternalInput").ap()
    binc_d = nc.dram_tensor("binc", [128, 8], F32, kind="ExternalInput").ap()
    bout_d = nc.dram_tensor("bout", [1, E], F32, kind="ExternalInput").ap()
    gwc_d = nc.dram_tensor("gwc", [128, 8], F32, kind="ExternalInput").ap()
    wgT_d = nc.dram_tensor("wgT", [E, INNER], F32, kind="ExternalInput").ap()
    wuT_d = nc.dram_tensor("wuT", [E, INNER], F32, kind="ExternalInput").ap()
    wdT_d = nc.dram_tensor("wdT", [INNER, E], F32, kind="ExternalInput").ap()
    id_d = nc.dram_tensor("idm", [128, 128], BF16, kind="ExternalInput").ap()
    out_d = nc.dram_tensor("out", [2048, E], F32, kind="ExternalOutput").ap()
    import os as _osd
    KDBG = int(_osd.environ.get("KDBG", "0"))
    if KDBG == 1:
        dbgE = nc.dram_tensor("dbgE", [128, 4 * 1152], BF16, kind="ExternalOutput").ap()
        dbgW = nc.dram_tensor("dbgW", [128, 4 * 512], BF16, kind="ExternalOutput").ap()
        dbgA = nc.dram_tensor("dbgA", [128, 4 * 65], F32, kind="ExternalOutput").ap()
        dbgS = nc.dram_tensor("dbgS", [128, 4 * 1024], BF16, kind="ExternalOutput").ap()
    if KDBG == 2:
        dbgR = nc.dram_tensor("dbgR", [128, 4 * 1024], F32, kind="ExternalOutput").ap()
        dbgT = nc.dram_tensor("dbgT", [128, 8 * 512], BF16, kind="ExternalOutput").ap()
        dbgO = nc.dram_tensor("dbgO", [128, 4 * 1024], F32, kind="ExternalOutput").ap()

    cdr = nc.dram_tensor("cdr", [2 * 512 * CROW], BF16, kind="Internal").ap()
    wgQ = nc.dram_tensor("wgQ", [E * INNER], BF16, kind="Internal").ap()
    wuQ = nc.dram_tensor("wuQ", [E * INNER], BF16, kind="Internal").ap()
    wdQ = nc.dram_tensor("wdQ", [INNER * E], BF16, kind="Internal").ap()
    scl_d = nc.dram_tensor("scl", [1, 4], F32, kind="Internal").ap()

    p = P(nc)
    B_ = p.B
    A = bass.AP

    PB = ((nc.sbuf_base + 63) // 64) * 64

    def at(name, shape, dt, off):
        return nc.alloc_sbuf_tensor_at(name, shape, dt, offset=PB + off)

    x_tc = at("x_tc", [128, 16, 1024], F32, 0)
    junk = at("junk", [128, 1024], BF16, 65536)
    ident = at("ident", [128, 128], BF16, 67584)
    zero128 = at("zero128", [128, 128], BF16, 67840)
    ones_row = at("ones_row", [1, 512], BF16, 68096)
    cols = at("cols", [128, 64], F32, 69120)
    bqc = at("bqc_s", [128, 8], F32, 69376)
    brc = at("brc_s", [128, 4], F32, 69408)
    binc = at("binc_s", [128, 8], F32, 69440)
    gwc = at("gwc_s", [128, 8], F32, 69472)
    kcv = at("kcv", [128, 32], F32, 69504)
    wscal = at("wscal", [128, 8], F32, 69632)

    OV = 69696
    qwf = [at(f"qwf{i}", [128, 1024], F32, OV + i * 4096) for i in range(2)]
    qwq = [at(f"qwq{i}", [128, 1024], BF16, OV + 8192 + i * 2048) for i in range(2)]
    qwt = [at(f"qwt{i}", [128, 1024], BF16, OV + 12288 + i * 2048) for i in range(2)]

    xt0 = at("xt0", [128, 1024], F32, OV + 0)
    cst2 = at("cst2", [128, 1024], BF16, OV + 0)
    xt1 = at("xt1", [128, 1024], F32, OV + 4096)
    xlt = at("xlt", [128, 1024], BF16, OV + 8192)
    qpe = at("qpe", [128, 512], BF16, OV + 8192)
    qpo = at("qpo", [128, 512], BF16, OV + 9216)
    xct = at("xct", [128, 2 * 8 * 512], BF16, OV + 10240)
    wslot0 = at("wslot0", [128, 8, 1024], BF16, OV + 26624)
    wslot1 = at("wslot1", [128, 8, 512], BF16, OV + 43008)
    kp = at("kp", [128, 8, 1536], BF16, OV + 51200)
    vt = at("vt", [128, 12, 1040], BF16, OV + 75776)
    q_sb = at("q_sb", [128, 8, 512], BF16, OV + 100736)
    rr = at("rr", [128, 4, 512], BF16, OV + 108928)
    rst = at("rst", [128, 512], BF16, OV + 113024)
    wsb = at("wsb", [128, 4, 512], BF16, OV + 114048)
    esb = at("esb", [128, 4, 1152], BF16, OV + 118144)
    act_ = at("act_", [128, 8, 512], BF16, OV + 118144)
    asb = at("asb", [128, 4, 1024], BF16, OV + 127360)
    brev = at("brev", [128, 1024], BF16, OV + 135552)
    edge_b = at("edge_b", [1, 1024], BF16, OV + 137600)
    bpj = at("bpj_s", [1, 1024], BF16, OV + 139648)

    hb = at("hb", [128, 8, 4096], BF16, OV + 0)
    xqc = at("xqc", [128, 8 * 1024], BF16, OV + 65536)
    xql = [at(f"xql{i}", [128, 1024], F32, OV + 81920 + i * 4096) for i in range(2)]
    xqi = [at(f"xqi{i}", [128, 1024], BF16, OV + 90112 + i * 2048) for i in range(2)]
    rti = [at(f"rti{i}", [128, 512], I32, OV + 94208 + i * 2048) for i in range(2)]
    hot = at("hot", [128, 128, 128], BF16, OV + 65536)
    rtf = [at(f"rtf{i}", [128, 512], F32, OV + 98304 + i * 2048) for i in range(2)]
    rtg = [at(f"rtg{i}", [128, 512], BF16, OV + 102400 + i * 1024) for i in range(2)]
    gbuf = [at(f"gbf{i}", [128, 512], F32, OV + 104448 + i * 2048) for i in range(3)]
    wfs = [at(f"wfs{i}", [128, 8, 512], BF16, OV + 110592 + i * 8192) for i in range(3)]
    wdfs = [at(f"wdfs{i}", [128, 512], BF16, OV + 135168 + i * 1024) for i in range(2)]

    ub = at("ub", [128, 8, 2048], BF16, OV + 0)
    vbb = at("vbb", [128, 8, 512], BF16, OV + 32768)
    xctb = at("xctb", [128, 8, 512], BF16, OV + 40960)
    xltb2 = at("xltb2", [128, 1024], BF16, OV + 49152)
    wib = at("wib", [128, 8, 1024], BF16, OV + 51200)
    wob = at("wob", [128, 8, 1024], BF16, OV + 67584)
    cvt0 = at("cvt0", [128, 512], F32, OV + 83968)
    tts = [at(f"tts{i}", [128, 512], F32, OV + 86016 + i * 2048) for i in range(2)]
    bout = at("bout_s", [1, 1024], BF16, OV + 90112)

    stack = contextlib.ExitStack()
    pbig = [stack.enter_context(nc.psum_tensor(f"pbig{i}", [128, 1024], F32)) for i in range(4)]
    def pv(i, a=0, b2=512):
        return pbig[i // 2][:, (i % 2) * 512 + a:(i % 2) * 512 + b2]
    bank = [B_(f"bank{i}") for i in range(8)]

    for s_ in ("t", "v", "s", "y", "g", "dy0", "dy1", "dg0", "dg1", "ds0", "ds1"):
        p.sems[s_] = stack.enter_context(nc.semaphore("sem_" + s_))

    import os as _os
    KI = _os.environ.get("KI", "xin,smalls,pads").split(",")
    for _b in range(B if "xin" in KI else 0):
        p.dma("y", lambda h, _b=_b: h.dma_start(
            A(x_tc, _b * 4 * 1024, [[16 * 1024, 128], [1024, 4], [1, 1024]]),
            A(xsp.tensor, _b * SPAN * 1024 + OWN0 * 1024,
              [[1024, 128], [128 * 1024, 4], [1, 1024]])),
            writes=[B_("x_tc")])
    p.dma("y", lambda h: h.dma_start(ident[:, :], id_d), writes=[B_("ident")])
    if "smalls" in KI:
        p.dma("g", lambda h: h.dma_start(brev[0:NB, :], bndr_d), writes=[B_("brev")])
        p.dma("g", lambda h: h.dma_start(edge_b[:, :], edge_d), writes=[B_("edge_b")])
        p.dma("g", lambda h: h.dma_start(bpj[:, :], bpj_d), writes=[B_("bpj")])
        p.dma("y", lambda h: h.dma_start(bqc[:, :], bqc_d), writes=[B_("bqc")])
        p.dma("y", lambda h: h.dma_start(brc[:, :], brc_d), writes=[B_("brc")])
        p.dma("y", lambda h: h.dma_start(binc[:, :], binc_d), writes=[B_("binc")])
        p.dma("y", lambda h: h.dma_start(gwc[:, :], gwc_d), writes=[B_("gwc")])
    p.op("v", lambda h: h.memset(ones_row[:, :], 1.0), writes=[B_("ones_row")])
    p.op("v", lambda h: h.memset(zero128[:, :], 0.0), writes=[B_("zero128")])
    if "smalls" in KI:
        p.op("v", lambda h: h.tensor_scalar(bqc[:, :], bqc[:, :], 0.125, None, ALU.mult),
             reads=[B_("bqc")], writes=[B_("bqc")])
    p.op("v", lambda h: h.memset(junk[:, 0:512], NEG), writes=[B_("junk")])
    for par in range(2 if "pads" in KI else 0):
        for side in range(2):
            for rk in range(4):
                off = par * 512 * CROW + rk * 128 * CROW + side * 1536
                p.dma("g", lambda h, off=off: h.dma_start(
                    A(cdr.tensor, off, [[CROW, 128], [1, 512]]), junk[:, 0:512]),
                    reads=[B_("junk")], writes=[B_(f"cpad{par}")])

    def ln_stats(srcap, srcbuf, ci, colbuf):
        p.op("v", lambda h: h.reduce_sum(cols[:, ci + 2:ci + 3], srcap, axis=AX.X),
             reads=[srcbuf], writes=[colbuf])
        p.op("s", lambda h: h.activation(junk[:, :], srcap, AF.Square,
                                         accum_out=cols[:, ci + 3:ci + 4]),
             reads=[srcbuf], writes=[B_("junk"), colbuf])
        p.op("v", lambda h: h.tensor_scalar(cols[:, ci + 2:ci + 3], cols[:, ci + 2:ci + 3],
             1.0 / 1024, None, ALU.mult), reads=[colbuf], writes=[colbuf])
        p.op("v", lambda h: h.tensor_scalar(cols[:, ci + 3:ci + 4], cols[:, ci + 3:ci + 4],
             1.0 / 1024, None, ALU.mult), reads=[colbuf], writes=[colbuf])
        p.op("v", lambda h: h.tensor_tensor(cols[:, ci + 4:ci + 5], cols[:, ci + 2:ci + 3],
             cols[:, ci + 2:ci + 3], ALU.mult), reads=[colbuf], writes=[colbuf])
        p.op("v", lambda h: h.tensor_tensor(cols[:, ci + 3:ci + 4], cols[:, ci + 3:ci + 4],
             cols[:, ci + 4:ci + 5], ALU.subtract), reads=[colbuf], writes=[colbuf])
        p.op("v", lambda h: h.tensor_scalar(cols[:, ci + 3:ci + 4], cols[:, ci + 3:ci + 4],
             1e-5, None, ALU.add), reads=[colbuf], writes=[colbuf])
        p.op("s", lambda h: h.activation(cols[:, ci + 4:ci + 5], cols[:, ci + 3:ci + 4],
             AF.Ln), reads=[colbuf], writes=[colbuf])
        p.op("s", lambda h: h.activation(cols[:, ci:ci + 1], cols[:, ci + 4:ci + 5],
             AF.Exp, scale=-0.5), reads=[colbuf], writes=[colbuf])
        p.op("v", lambda h: h.tensor_tensor(cols[:, ci + 1:ci + 2], cols[:, ci + 2:ci + 3],
             cols[:, ci:ci + 1], ALU.mult), reads=[colbuf], writes=[colbuf])
        p.op("v", lambda h: h.tensor_scalar(cols[:, ci + 1:ci + 2], cols[:, ci + 1:ci + 2],
             -1.0, None, ALU.mult), reads=[colbuf], writes=[colbuf])

    def rnd_int8(srcap, srcbufs, scol, colbuf, dstap, dstbuf, ch):
        k = ch % 2
        ib, fb2, gb_ = rti[k], rtf[k], rtg[k]
        ibb, fbb, gbb = B_(f"rti{k}"), B_(f"rtf{k}"), B_(f"rtg{k}")
        p.op("v", lambda h: h.tensor_scalar(ib[:, :], srcap, scol, 384.0, ALU.mult, ALU.add),
             reads=list(srcbufs) + [colbuf], writes=[ibb])
        p.op("v", lambda h: h.scalar_tensor_tensor(fb2[:, :], srcap, scol, ib[:, :],
             ALU.mult, ALU.subtract), reads=list(srcbufs) + [colbuf, ibb], writes=[fbb])
        p.op("v", lambda h: h.tensor_single_scalar(gb_[:, :], fb2[:, :], -383.5, ALU.is_gt),
             reads=[fbb], writes=[gbb])
        p.op("v", lambda h: h.scalar_tensor_tensor(dstap, ib[:, :], -384.0, gb_[:, :],
             ALU.add, ALU.add), reads=[ibb, gbb], writes=[dstbuf])

    def plan_wquant():
        cw = B_("colW")
        mats = [(wgT_d, wgQ, 0), (wuT_d, wuQ, 1), (wdT_d, wdQ, 2)]
        csz = 128 * 1024
        nch = (E * INNER) // csz
        p.op("v", lambda h: h.memset(kcv[:, 30:31], 1.0), writes=[B_("kcvo")])
        for (src, dst, mi) in mats:
            acc = 36 + mi
            p.op("v", lambda h, acc=acc: h.memset(cols[:, acc:acc + 1], 0.0), writes=[cw])
            for ch in range(nch):
                k = ch % 2
                fb = B_(("xt0", "xt1")[k])
                p.dma("g", lambda h, src=src, ch=ch, k=k: h.dma_start(
                    qwf[k][:, :], A(src.tensor, ch * csz, [[1024, 128], [1, 1024]])),
                    writes=[fb])
                p.op("v", lambda h, k=k: h.reduce_sum(cols[:, 39:40], qwf[k][:, :],
                     axis=AX.X, apply_absolute_value=True), reads=[fb], writes=[cw])
                p.op("v", lambda h, acc=acc: h.tensor_tensor(cols[:, acc:acc + 1],
                     cols[:, acc:acc + 1], cols[:, 39:40], ALU.add), reads=[cw], writes=[cw])
            bk = bank[7]
            p.op("t", lambda h, acc=acc: h.matmul(pv(7, 0, 1)[0:1, :], cols[:, acc:acc + 1],
                 kcv[:, 30:31], start=True, stop=True), reads=[cw, B_("kcvo")], writes=[bk])
            p.op("s", lambda h, acc=acc: h.copy(cols[0:1, 39:40], pv(7, 0, 1)[0:1, :]),
                 reads=[bk], writes=[cw])
            p.op("v", lambda h, acc=acc: h.tensor_scalar(cols[0:1, acc:acc + 1],
                 cols[0:1, 39:40], 1.0 / (E * INNER), None, ALU.mult),
                 reads=[cw], writes=[cw])
            p.op("v", lambda h, acc=acc: h.tensor_scalar(cols[0:1, acc:acc + 1],
                 cols[0:1, acc:acc + 1], 1e-5, None, ALU.max), reads=[cw], writes=[cw])
        p.dma("y", lambda h: h.dma_start(A(scl_d.tensor, 0, [[1, 1], [1, 3]]),
              cols[0:1, 36:39]), reads=[cw], writes=[B_("scl")])
        p.dma("y", lambda h: h.dma_start(A(wscal, 0, [[8, 128], [1, 3]]),
              A(scl_d.tensor, 0, [[0, 128], [1, 3]])),
              reads=[B_("scl")], writes=[B_("wscal")])
        wsc = B_("wscal")
        for mi in range(3):
            p.op("v", lambda h, mi=mi: h.reciprocal(wscal[:, 3 + mi:4 + mi],
                 wscal[:, mi:mi + 1]), reads=[wsc], writes=[wsc])
        p.op("v", lambda h: h.tensor_scalar(wscal[:, 6:7], wscal[:, 3:4], -1.0, None,
             ALU.mult), reads=[wsc], writes=[wsc])
        p.op("v", lambda h: h.tensor_scalar(wscal[:, 7:8], wscal[:, 4:5], -1.0, None,
             ALU.mult), reads=[wsc], writes=[wsc])
        p.op("v", lambda h: h.tensor_scalar(kcv[:, 29:30], wscal[:, 5:6], -1.0, None,
             ALU.mult), reads=[wsc], writes=[B_("kcvq")])
        for (src, dst, mi) in mats:
            scol = wscal[:, 3 + mi:4 + mi]
            nscol = kcv[:, 29:30] if mi == 2 else wscal[:, 6 + mi:7 + mi]
            nsb = B_("kcvq") if mi == 2 else wsc
            for ch in range(nch):
                k = ch % 2
                fb, qb, tb = B_(("xt0", "xt1")[k]), B_(("qpxl", "xct0")[k]), B_("xct0")
                p.dma("g", lambda h, src=src, ch=ch, k=k: h.dma_start(
                    qwf[k][:, :], A(src.tensor, ch * csz, [[1024, 128], [1, 1024]])),
                    writes=[fb])
                p.op("v", lambda h, k=k, scol=scol: h.tensor_scalar(
                    qwq[k][:, :], qwf[k][:, :], scol, 0.5, ALU.mult, ALU.is_gt),
                    reads=[fb, wsc], writes=[qb])
                p.op("v", lambda h, k=k, nscol=nscol, nsb=nsb: h.tensor_scalar(
                    qwt[k][:, :], qwf[k][:, :], nscol, 0.5, ALU.mult, ALU.is_gt),
                    reads=[fb, nsb], writes=[tb])
                p.op("v", lambda h, k=k: h.tensor_tensor(
                    qwq[k][:, :], qwq[k][:, :], qwt[k][:, :], ALU.subtract),
                    reads=[qb, tb], writes=[qb])
                p.dma("g", lambda h, dst=dst, ch=ch, k=k: h.dma_start(
                    A(dst.tensor, ch * csz, [[1024, 128], [1, 1024]]), qwq[k][:, :]),
                    reads=[qb], writes=[B_(f"wQd{mi}")])

    def plan_attn():
        import os
        KB_ = int(os.environ.get("KBATCH", str(B)))
        KH_ = int(os.environ.get("KHEAD", str(H)))
        KPH = os.environ.get("KPH", "p1,p2,p3").split(",")
        KP1 = os.environ.get("KP1", "w,x,lns,lna,tr,xc,k,v,vm,qr").split(",")
        for b in range(KB_):
            if "w" in KP1:
                p.dma("g", lambda h: h.dma_start(
                    A(wslot0, 0, [[8 * 1024, 128], [1024, 8], [1, 1024]]),
                    A(wk_d.tensor, 0, [[1024, 128], [128 * 1024, 8], [1, 1024]])),
                    reads=[B_("wslot0")], writes=[B_("wslot0")])
            for nn in range(3):
                cx = nn % 2
                xcb = B_(f"xct{cx}")
                for stl in range(4):
                    st = nn * 4 + stl
                    k = st % 2
                    xt = (xt0, xt1)[k]
                    xtb = B_(f"xt{k}")
                    ci = k * 8
                    cb = B_(f"colL{k}")
                    p.dma("y", lambda h, b=b, st=st, xt=xt: h.dma_start(
                        xt[:, :], xsp[b, st * 128:(st + 1) * 128, :]), writes=[xtb])
                    if "lns" not in KP1:
                        continue
                    ln_stats(xt[:, :], xtb, ci, cb)
                    if "lna" not in KP1:
                        continue
                    p.op("s", lambda h, xt=xt, ci=ci: h.activation(
                        xlt[:, :], xt[:, :], AF.Identity,
                        bias=cols[:, ci + 1:ci + 2], scale=cols[:, ci:ci + 1]),
                        reads=[xtb, cb], writes=[B_("qpxl")])
                    if "tr" not in KP1:
                        continue
                    for half in range(2):
                        bki = 4 + (stl + half) % 2
                        bk = bank[bki]
                        for kc in range(4):
                            kg = half * 4 + kc
                            p.op("t", lambda h, kg=kg, kc=kc, bki=bki: h.matmul(
                                pv(bki, kc * 128, (kc + 1) * 128),
                                xlt[:, kg * 128:(kg + 1) * 128], ident[:, :],
                                start=True, stop=True),
                                reads=[B_("qpxl"), B_("ident")], writes=[bk])
                        p.op("s", lambda h, cx=cx, stl=stl, half=half, bki=bki: h.copy(
                            A(xct, cx * 8 * 512 + half * 4 * 512 + stl * 128,
                              [[2 * 8 * 512, 128], [512, 4], [1, 128]]),
                            pv(bki).rearrange("p (a c) -> p a c", a=4)),
                            reads=[bk], writes=[xcb])
                for ft in range(8 if "k" in KP1 else 0):
                    bki = ft % 2
                    bk = bank[bki]
                    for kc in range(8):
                        p.op("t", lambda h, ft=ft, kc=kc, bki=bki, cx=cx: h.matmul(
                            pv(bki), wslot0[:, kc, ft * 128:(ft + 1) * 128],
                            A(xct, cx * 8 * 512 + kc * 512, [[8192, 128], [1, 512]]),
                            start=(kc == 0), stop=(kc == 7)),
                            reads=[B_("wslot0"), xcb], writes=[bk])
                    p.op("s", lambda h, ft=ft, nn=nn, bki=bki: h.copy(
                        kp[:, ft, nn * 512:(nn + 1) * 512], pv(bki)),
                        reads=[bk], writes=[B_("kp")])
                for nn2 in range(2 if "v" in KP1 else 0):
                    p.dma("g", lambda h, nn2=nn2: h.dma_start(
                        A(wslot1, 0, [[8 * 512, 128], [512, 8], [1, 512]]),
                        A(wv_d.tensor, nn2 * 512, [[1024, 128], [128 * 1024, 8], [1, 512]])),
                        reads=[B_("wslot1")], writes=[B_("wslot1")])
                    for mtl in range(4):
                        mt = nn * 4 + mtl
                        bki = 2 + mtl % 2
                        bk = bank[bki]
                        for kc in range(8):
                            p.op("t", lambda h, mtl=mtl, kc=kc, bki=bki, cx=cx: h.matmul(
                                pv(bki),
                                A(xct, cx * 8 * 512 + kc * 512 + mtl * 128, [[8192, 128], [1, 128]]),
                                wslot1[:, kc, :], start=(kc == 0), stop=(kc == 7)),
                                reads=[B_("wslot1"), xcb], writes=[bk])
                        p.op("s", lambda h, mt=mt, nn2=nn2, bki=bki: h.copy(
                            A(vt, mt * 1040 + nn2 * 8 * 65, [[12 * 1040, 128], [65, 8], [1, 64]]),
                            pv(bki).rearrange("p (a c) -> p a c", a=8)),
                            reads=[bk], writes=[B_("vt")])
            if "vm" in KP1:
                p.op("v", lambda h: h.memset(
                    A(vt, 64, [[12 * 1040, 128], [1040, 12], [65, 16]]), 1.0),
                    writes=[B_("vt")])
            if "qr" not in KP1:
                continue
            p.dma("g", lambda h: h.dma_start(
                A(wslot0, 0, [[8 * 1024, 128], [1024, 8], [1, 1024]]),
                A(wq_d.tensor, 0, [[1024, 128], [128 * 1024, 8], [1, 1024]])),
                reads=[B_("wslot0")], writes=[B_("wslot0")])
            xcb2 = B_("xct0")
            for ft in range(8):
                bki = ft % 2
                bk = bank[bki]
                for kc in range(8):
                    p.op("t", lambda h, ft=ft, kc=kc, bki=bki: h.matmul(
                        pv(bki), wslot0[:, kc, ft * 128:(ft + 1) * 128],
                        A(xct, kc * 512, [[8192, 128], [1, 512]]),
                        start=(kc == 0), stop=(kc == 7)),
                        reads=[B_("wslot0"), xcb2], writes=[bk])
                p.op("s", lambda h, ft=ft, bki=bki: h.activation(
                    q_sb[:, ft, :], pv(bki), AF.Identity,
                    bias=bqc[:, ft:ft + 1], scale=0.125),
                    reads=[bk, B_("bqc")], writes=[B_("q_sb")])
            p.dma("g", lambda h: h.dma_start(
                A(wslot1, 0, [[8 * 512, 128], [512, 8], [1, 512]]),
                A(wrp_d.tensor, 0, [[512, 128], [128 * 512, 8], [1, 512]])),
                reads=[B_("wslot1")], writes=[B_("wslot1")])
            for rt in range(4):
                bki = 2 + rt % 2
                bk = bank[bki]
                for kc in range(8):
                    p.op("t", lambda h, rt=rt, kc=kc, bki=bki: h.matmul(
                        pv(bki), wslot1[:, kc, rt * 128:(rt + 1) * 128],
                        A(xct, kc * 512, [[8192, 128], [1, 512]]),
                        start=(kc == 0), stop=(kc == 7)),
                        reads=[B_("wslot1"), xcb2], writes=[bk])
                p.op("s", lambda h, rt=rt, bki=bki: h.activation(
                    rr[:, rt, :], pv(bki), AF.Identity, bias=brc[:, rt:rt + 1]),
                    reads=[bk, B_("brc")], writes=[B_("rr")])

            if "p2" not in KPH:
                continue
            p.op("v", lambda h: h.memset(qpe[64:128, :], 0.0), writes=[B_("qpxl")])
            p.op("v", lambda h: h.memset(qpo[0:64, :], 0.0), writes=[B_("qpxl")])
            for hh in range(KH_):
                par = hh % 2
                qpad = (qpe, qpo)[par]
                p.dma("g", lambda h, hh=hh: h.dma_start(
                    rst[0:10, :], rr[32 * (hh % 4):32 * (hh % 4) + 10, hh // 4, :]),
                    reads=[B_("rr")], writes=[B_("rst")])
                a0 = 64 * (hh % 2)
                p.op("v", lambda h, hh=hh, qpad=qpad, a0=a0: h.tensor_copy(
                    qpad[a0:a0 + 64, :], q_sb[a0:a0 + 64, hh // 2, :]),
                    reads=[B_("q_sb")], writes=[B_("qpxl")])
                cdb = B_(f"cdram{par}")
                for ct in range(4):
                    cstt = (junk, cst2)[ct % 2]
                    cbf = (B_("junk"), B_("xt0"))[ct % 2]
                    for cc in range(2):
                        bki = 4 + cc
                        bk = bank[bki]
                        p.op("t", lambda h, ct=ct, cc=cc, bki=bki: h.matmul(
                            pv(bki), rst[0:10, ct * 128:(ct + 1) * 128],
                            brev[0:10, cc * 512:(cc + 1) * 512], start=True, stop=True),
                            reads=[B_("rst"), B_("brev")], writes=[bk])
                        p.op("s", lambda h, cstt=cstt, cc=cc, bki=bki: h.copy(
                            cstt[:, cc * 512:(cc + 1) * 512], pv(bki)),
                            reads=[bk], writes=[cbf])
                    p.dma("y", lambda h, par=par, ct=ct, cstt=cstt: h.dma_start(
                        A(cdr.tensor, par * 512 * CROW + ct * 128 * CROW + LPAD,
                          [[CROW, 128], [1, 1024]]), cstt[:, :]),
                        reads=[cbf], writes=[cdb])
                esbb = B_("esb")
                off = par * 512 * CROW + 511
                p.dma("y", lambda h, off=off: h.dma_start(
                    esb[:, :, :],
                    A(cdr.tensor, off, [[CROW - 1, 128], [128 * CROW, 4], [1, 1152]])),
                    reads=[cdb, B_(f"cpad{par}")], writes=[esbb])
                if KDBG == 1 and b == 0 and hh == 0:
                    p.dma("y", lambda h: h.dma_start(dbgE, esb[:, :, :]),
                          reads=[esbb], writes=[B_("dbgE")])
                p.op("t", lambda h: h.matmul(pbig[3][:, 0:260], zero128[:, :],
                     vt[:, 0, 0:260], start=True, stop=False),
                     reads=[B_("zero128"), B_("vt")], writes=[bank[6], bank[7]])
                wsbb = B_("wsb")

                def emit_qke(kb, hh, qpad):
                    sj = (kb // 2) % 2
                    bki2 = 2 * sj + (kb % 2)
                    bk = bank[bki2]
                    qts = [qt for qt in range(4) if 0 <= kb - qt <= 8]
                    p.op("t", lambda h, hh=hh, kb=kb, bki2=bki2, qpad=qpad: h.matmul(
                        pv(bki2), kp[:, hh // 2, kb * 128:(kb + 1) * 128],
                        qpad[:, :], start=True, stop=False),
                        reads=[B_("kp"), B_("qpxl")], writes=[bk])
                    if kb < 8:
                        p.op("t", lambda h, kb=kb, bki2=bki2: h.matmul(
                            pv(bki2), edge_b[0:1, kb * 128:(kb + 1) * 128],
                            ones_row[0:1, :], start=False, stop=False),
                            reads=[B_("edge_b"), B_("ones_row")], writes=[bk])
                    for i, qt in enumerate(qts):
                        p.op("t", lambda h, qt=qt, kb=kb, bki2=bki2,
                             last=(i == len(qts) - 1): h.matmul(
                            pv(bki2, qt * 128, (qt + 1) * 128),
                            esb[:, qt, (kb - qt) * 128:(kb - qt) * 128 + 128], ident[:, :],
                            start=False, stop=last),
                            reads=[esbb, B_("ident")], writes=[bk])

                def emit_exp(kb2):
                    sj = kb2 % 2
                    p.op("s", lambda h, kb2=kb2, sj=sj: h.activation(
                        A(wsb, (kb2 * 2 % 4) * 512, [[4 * 512, 128], [1, 1024]]),
                        pbig[sj][:, :], AF.Exp),
                        reads=[bank[2 * sj], bank[2 * sj + 1]], writes=[wsbb])

                def emit_av(kb, hh):
                    qts = [qt for qt in range(4) if 0 <= kb - qt <= 8]
                    for i, qt in enumerate(qts):
                        p.op("t", lambda h, hh=hh, kb=kb, qt=qt,
                             last=(kb == NKB - 1 and qt == 3): h.matmul(
                            pbig[3][:, qt * 65:qt * 65 + 65],
                            wsb[:, kb % 4, qt * 128:(qt + 1) * 128],
                            vt[:, kb, 65 * hh:65 * hh + 65],
                            start=False, stop=last),
                            reads=[wsbb, B_("vt")], writes=[bank[6], bank[7]])

                for kb2 in range(6):
                    emit_qke(2 * kb2, hh, qpad)
                    emit_qke(2 * kb2 + 1, hh, qpad)
                    emit_exp(kb2)
                    if kb2 >= 1:
                        emit_av(2 * (kb2 - 1), hh)
                        emit_av(2 * (kb2 - 1) + 1, hh)
                emit_av(10, hh)
                emit_av(11, hh)
                if KDBG == 1 and b == 0 and hh == 0:
                    p.dma("y", lambda h: h.dma_start(dbgW, wsb[:, :, :]),
                          reads=[wsbb], writes=[B_("dbgW")])
                if KDBG == 1 and b == 0 and hh == H - 1:
                    pass
                    p.op("s", lambda h: h.copy(junk[:, 0:260].bitcast(F32) if False else xt0[:, 0:260], pbig[3][:, 0:260]),
                         reads=[bank[6], bank[7]], writes=[B_("xt0")])
                    p.dma("y", lambda h: h.dma_start(dbgA, xt0[:, 0:260]),
                          reads=[B_("xt0")], writes=[B_("dbgA")])
                for qt in range(4):
                    p.op("s", lambda h, qt=qt: h.copy(
                        cols[:, 44 + qt:45 + qt], pbig[3][:, qt * 65 + 64:qt * 65 + 65]),
                        reads=[bank[6], bank[7]], writes=[B_("colD")])
                p.op("v", lambda h: h.reciprocal(cols[:, 32:36], cols[:, 44:48]),
                     reads=[B_("colD")], writes=[B_("colD")])
                for qt in range(4):
                    p.op("s", lambda h, qt=qt, hh=hh: h.activation(
                        asb[:, qt, 64 * hh:64 * hh + 64], pbig[3][:, qt * 65:qt * 65 + 64],
                        AF.Identity, scale=cols[:, 32 + qt:33 + qt]),
                        reads=[bank[6], bank[7], B_("colD")], writes=[B_("asb")])

            if KDBG == 1 and b == 0:
                p.dma("y", lambda h: h.dma_start(dbgS, asb[:, :, :]),
                      reads=[B_("asb")], writes=[B_("dbgS")])
            if KDBG == 2 and b == 0:
                p.dma("y", lambda h: h.dma_start(dbgR, x_tc[:, 0:4, :]),
                      reads=[B_("x_tc")], writes=[B_("dbgR")])
            if "p3" not in KPH:
                continue
            p.dma("g", lambda h: h.dma_start(
                A(wslot0, 0, [[8 * 1024, 128], [1024, 8], [1, 1024]]),
                A(wpj_d.tensor, 0, [[1024, 128], [128 * 1024, 8], [1, 1024]])),
                reads=[B_("wslot0")], writes=[B_("wslot0")])
            for hc in range(8):
                bki = 4 + hc % 2
                bk = bank[bki]
                for qt in range(4):
                    p.op("t", lambda h, hc=hc, qt=qt, bki=bki: h.matmul(
                        pv(bki, qt * 128, (qt + 1) * 128),
                        asb[:, qt, hc * 128:(hc + 1) * 128], ident[:, :],
                        start=True, stop=True),
                        reads=[B_("asb"), B_("ident")], writes=[bk])
                p.op("s", lambda h, hc=hc, bki=bki: h.copy(
                    act_[:, hc, :], pv(bki)), reads=[bk], writes=[B_("esb")])
            if KDBG == 2 and b == 0:
                p.dma("y", lambda h: h.dma_start(dbgT, act_[:, :, :]),
                      reads=[B_("esb")], writes=[B_("dbgT")])
            for jt in range(4):
                for cc in range(2):
                    bki = (jt * 2 + cc) % 2
                    bk = bank[bki]
                    for hc in range(8):
                        p.op("t", lambda h, jt=jt, cc=cc, hc=hc, bki=bki: h.matmul(
                            pv(bki), act_[:, hc, jt * 128:(jt + 1) * 128],
                            wslot0[:, hc, cc * 512:(cc + 1) * 512],
                            start=(hc == 0), stop=False),
                            reads=[B_("esb"), B_("wslot0")], writes=[bk])
                    p.op("t", lambda h, cc=cc, bki=bki: h.matmul(
                        pv(bki), ones_row[0:1, 0:128], bpj[0:1, cc * 512:(cc + 1) * 512],
                        start=False, stop=True),
                        reads=[B_("ones_row"), B_("bpj")], writes=[bk])
                    j = b * 4 + jt
                    p.op("s", lambda h, cc=cc, bki=bki: h.copy(
                        xt1[:, bki * 512:(bki + 1) * 512], pv(bki)),
                        reads=[bk], writes=[B_("xt1")])
                    p.op("v", lambda h, j=j, cc=cc, bki=bki: h.tensor_tensor(
                        x_tc[:, j, cc * 512:(cc + 1) * 512],
                        x_tc[:, j, cc * 512:(cc + 1) * 512],
                        xt1[:, bki * 512:(bki + 1) * 512], ALU.add),
                        reads=[B_("xt1"), B_("x_tc")], writes=[B_("x_tc")])
            if KDBG == 2 and b == 0:
                p.dma("y", lambda h: h.dma_start(dbgO, x_tc[:, 0:4, :]),
                      reads=[B_("x_tc")], writes=[B_("dbgO")])

    def plan_ffn(call):
        wsc = B_("wscal")
        cs = B_("colS")
        cu = B_("colU")
        for g in range(2):
            tiles = list(range(8 * g, 8 * g + 8))
            for ii, jt in enumerate(tiles):
                k = jt % 2
                xq_, xi_ = xql[k], xqi[k]
                xqlb, xqib = B_(f"xql{k}"), B_(f"xqi{k}")
                ci = k * 8
                cb = B_(f"colL{k}")
                ln_stats(x_tc[:, jt, :], B_("x_tc"), ci, cb)
                p.op("s", lambda h, jt=jt, xq_=xq_, ci=ci: h.activation(
                    xq_[:, :], x_tc[:, jt, :], AF.Identity,
                    bias=cols[:, ci + 1:ci + 2], scale=cols[:, ci:ci + 1]),
                    reads=[B_("x_tc"), cb], writes=[xqlb])
                p.op("v", lambda h, xq_=xq_, ci=ci: h.reduce_max(
                    cols[:, ci + 5:ci + 6], xq_[:, :], axis=AX.X,
                    apply_absolute_value=True), reads=[xqlb], writes=[cb])
                p.op("v", lambda h, ci=ci: h.tensor_scalar(cols[:, ci + 5:ci + 6],
                     cols[:, ci + 5:ci + 6], 1e-5, None, ALU.max), reads=[cb], writes=[cb])
                p.op("v", lambda h, ci=ci, ii=ii: h.tensor_scalar(cols[:, 40 + ii:41 + ii],
                     cols[:, ci + 5:ci + 6], 1.0 / 127.0, None, ALU.mult),
                     reads=[cb], writes=[cs])
                p.op("v", lambda h, ci=ci: h.reciprocal(cols[:, ci + 6:ci + 7],
                     cols[:, ci + 5:ci + 6]), reads=[cb], writes=[cb])
                p.op("v", lambda h, ci=ci: h.tensor_scalar(cols[:, ci + 6:ci + 7],
                     cols[:, ci + 6:ci + 7], 127.0, None, ALU.mult), reads=[cb], writes=[cb])
                for ch in range(2):
                    sl = slice(ch * 512, (ch + 1) * 512)
                    rnd_int8(xq_[:, sl], [xqlb], cols[:, ci + 6:ci + 7], cb,
                             xi_[:, sl], xqib, ch)
                for half in range(2):
                    bki = 4 + (ii + half) % 2
                    bk = bank[bki]
                    for kc in range(4):
                        kg = half * 4 + kc
                        p.op("t", lambda h, xi_=xi_, kg=kg, kc=kc, bki=bki: h.matmul(
                            pv(bki, kc * 128, (kc + 1) * 128),
                            xi_[:, kg * 128:(kg + 1) * 128], ident[:, :],
                            start=True, stop=True),
                            reads=[xqib, B_("ident")], writes=[bk])
                    p.op("s", lambda h, ii=ii, half=half, bki=bki: h.copy(
                        A(xqc, ii * 1024 + half * 512, [[8192, 128], [1, 512]]),
                        pv(bki)), reads=[bk], writes=[B_("xqc")])
                p.op("v", lambda h, ii=ii: h.tensor_tensor(cols[:, 48 + ii:49 + ii],
                     cols[:, 40 + ii:41 + ii], wscal[:, 0:1], ALU.mult),
                     reads=[cs, wsc], writes=[cs])
                p.op("v", lambda h, ii=ii: h.tensor_tensor(cols[:, 24 + ii:25 + ii],
                     cols[:, 40 + ii:41 + ii], wscal[:, 1:2], ALU.mult),
                     reads=[cs, wsc], writes=[cu])
            for oc in range(8):
                k = oc % 2
                wg_s = wfs[k]
                wu_s = wfs[2]
                wgb, wub = B_(f"wfs{k}"), B_("wfs2")
                p.dma("g", lambda h, oc=oc, wg_s=wg_s: h.dma_start(
                    A(wg_s, 0, [[8 * 512, 128], [512, 8], [1, 512]]),
                    A(wgQ.tensor, oc * 512, [[INNER, 128], [128 * INNER, 8], [1, 512]])),
                    reads=[wgb], writes=[wgb])
                p.dma("g", lambda h, oc=oc, wu_s=wu_s: h.dma_start(
                    A(wu_s, 0, [[8 * 512, 128], [512, 8], [1, 512]]),
                    A(wuQ.tensor, oc * 512, [[INNER, 128], [128 * INNER, 8], [1, 512]])),
                    reads=[wub], writes=[wub])
                for ii in range(8):
                    bg = ii % 2
                    bu = 2 + ii % 2
                    for kc in range(8):
                        p.op("t", lambda h, ii=ii, kc=kc, wg_s=wg_s, bg=bg: h.matmul(
                            pv(bg), A(xqc, ii * 1024 + kc * 128, [[8192, 128], [1, 128]]),
                            wg_s[:, kc, :], start=(kc == 0), stop=(kc == 7)),
                            reads=[B_("xqc"), wgb], writes=[bank[bg]])
                    for kc in range(8):
                        p.op("t", lambda h, ii=ii, kc=kc, wu_s=wu_s, bu=bu: h.matmul(
                            pv(bu), A(xqc, ii * 1024 + kc * 128, [[8192, 128], [1, 128]]),
                            wu_s[:, kc, :], start=(kc == 0), stop=(kc == 7)),
                            reads=[B_("xqc"), wub], writes=[bank[bu]])
                    gb_, ub_ = gbuf[ii % 2], gbuf[2]
                    gbb, ubb = B_(f"gbf{ii % 2}"), B_("gbf2")
                    p.op("s", lambda h, gb_=gb_, ii=ii, bg=bg: h.activation(
                        gb_[:, :], pv(bg), AF.Silu, scale=cols[:, 48 + ii:49 + ii]),
                        reads=[bank[bg], cs], writes=[gbb])
                    p.op("s", lambda h, ub_=ub_, ii=ii, bu=bu: h.activation(
                        ub_[:, :], pv(bu), AF.Identity, scale=cols[:, 24 + ii:25 + ii]),
                        reads=[bank[bu], cu], writes=[ubb])
                    p.op("v", lambda h, gb_=gb_, ub_=ub_, ii=ii, oc=oc: h.tensor_tensor(
                        hb[:, ii, oc * 512:(oc + 1) * 512], gb_[:, :], ub_[:, :], ALU.mult),
                        reads=[gbb, ubb], writes=[B_("hb")])
            for ii in range(8):
                hm = 16 + (ii % 2) * 4
                cb = B_(f"colH{ii % 2}")
                p.op("v", lambda h, ii=ii, hm=hm: h.reduce_max(cols[:, hm:hm + 1],
                     hb[:, ii, :], axis=AX.X, apply_absolute_value=True),
                     reads=[B_("hb")], writes=[cb])
                p.op("v", lambda h, hm=hm: h.tensor_scalar(cols[:, hm:hm + 1],
                     cols[:, hm:hm + 1], 1e-5, None, ALU.max), reads=[cb], writes=[cb])
                p.op("v", lambda h, hm=hm, ii=ii: h.tensor_scalar(cols[:, 56 + ii:57 + ii],
                     cols[:, hm:hm + 1], 1.0 / 127.0, None, ALU.mult), reads=[cb], writes=[cs])
                p.op("v", lambda h, ii=ii: h.tensor_tensor(cols[:, 56 + ii:57 + ii],
                     cols[:, 56 + ii:57 + ii], wscal[:, 2:3], ALU.mult),
                     reads=[cs, wsc], writes=[cs])
                p.op("v", lambda h, hm=hm: h.reciprocal(cols[:, hm + 1:hm + 2],
                     cols[:, hm:hm + 1]), reads=[cb], writes=[cb])
                p.op("v", lambda h, hm=hm: h.tensor_scalar(cols[:, hm + 1:hm + 2],
                     cols[:, hm + 1:hm + 2], 127.0, None, ALU.mult), reads=[cb], writes=[cb])
                for ch in range(8):
                    sl = slice(ch * 512, (ch + 1) * 512)
                    rnd_int8(hb[:, ii, sl], [B_("hb")], cols[:, hm + 1:hm + 2], cb,
                             hb[:, ii, sl], B_("hb"), ch)
            for half in range(2):
                p.barrier(dma=KDMA)
                for ii in range(4):
                    ti = half * 4 + ii
                    for qtr in range(8):
                        bki = 4 + (ii * 8 + qtr) % 2
                        bk = bank[bki]
                        for kc in range(4):
                            oc32 = qtr * 4 + kc
                            p.op("t", lambda h, ti=ti, oc32=oc32, kc=kc, bki=bki: h.matmul(
                                pv(bki, kc * 128, (kc + 1) * 128),
                                hb[:, ti, oc32 * 128:(oc32 + 1) * 128], ident[:, :],
                                start=True, stop=True),
                                reads=[B_("hb"), B_("ident")], writes=[bk])
                        p.op("s", lambda h, ii=ii, qtr=qtr, bki=bki: h.copy(
                            A(hot, (ii * 32 + qtr * 4) * 128, [[16384, 128], [1, 512]]),
                            pv(bki)), reads=[bk], writes=[B_("hot")])
                for cc in range(2):
                    for oc32 in range(32):
                        k = oc32 % 2
                        wd_s = wdfs[k]
                        wdb = B_(f"wdfs{k}")
                        p.dma("g", lambda h, cc=cc, oc32=oc32, wd_s=wd_s: h.dma_start(
                            wd_s[:, :],
                            A(wdQ.tensor, oc32 * 128 * E + cc * 512, [[E, 128], [1, 512]])),
                            reads=[wdb], writes=[wdb])
                        for ii in range(4):
                            p.op("t", lambda h, ii=ii, oc32=oc32, wd_s=wd_s: h.matmul(
                                pv(ii), A(hot, (ii * 32 + oc32) * 128, [[16384, 128], [1, 128]]),
                                wd_s[:, :], start=(oc32 == 0), stop=(oc32 == 31)),
                                reads=[B_("hot"), wdb], writes=[bank[ii]])
                    for ii in range(4):
                        ti = half * 4 + ii
                        jt = tiles[ti]
                        gb_ = gbuf[ii % 2]
                        gbb = B_(f"gbf{ii % 2}")
                        p.op("s", lambda h, ti=ti, ii=ii, gb_=gb_: h.activation(
                            gb_[:, :], pv(ii), AF.Identity, scale=cols[:, 56 + ti:57 + ti]),
                            reads=[bank[ii], cs], writes=[gbb])
                        p.op("v", lambda h, jt=jt, cc=cc, gb_=gb_: h.tensor_tensor(
                            x_tc[:, jt, cc * 512:(cc + 1) * 512],
                            x_tc[:, jt, cc * 512:(cc + 1) * 512], gb_[:, :], ALU.add),
                            reads=[gbb, B_("x_tc")], writes=[B_("x_tc")])
                p.barrier(dma=KDMA)

    def plan_attnb():
        KSL = [0, 2, 3, 1]
        kb_ = B_("kcv")
        def kv(et, s):
            return A(kcv, et * 4 + s, [[32, 128], [1, 1]])
        kall = lambda s: A(kcv, s, [[32, 128], [4, 8], [1, 1]])
        p.op("s", lambda h: h.activation(kall(0), gwc[:, :], AF.Sigmoid),
             reads=[B_("gwc")], writes=[kb_])
        p.op("v", lambda h: h.tensor_scalar(kall(1), kall(0), -1.0, 1.0, ALU.mult, ALU.add),
             reads=[kb_], writes=[kb_])
        p.op("v", lambda h: h.tensor_tensor(kall(2), kall(0), kall(1), ALU.mult),
             reads=[kb_], writes=[kb_])
        p.op("v", lambda h: h.tensor_tensor(kall(3), kall(2), kall(1), ALU.mult),
             reads=[kb_], writes=[kb_])
        p.op("v", lambda h: h.tensor_tensor(kall(1), kall(3), kall(1), ALU.mult),
             reads=[kb_], writes=[kb_])
        p.dma("g", lambda h: h.dma_start(bout[:, :], bout_d), writes=[B_("bout")])
        p.dma("g", lambda h: h.dma_start(
            A(wib, 0, [[8 * 1024, 128], [1024, 8], [1, 1024]]),
            A(winT_d.tensor, 0, [[1024, 128], [128 * 1024, 8], [1, 1024]])),
            writes=[B_("wib")])
        p.dma("g", lambda h: h.dma_start(
            A(wob, 0, [[8 * 1024, 128], [1024, 8], [1, 1024]]),
            A(woutT_d.tensor, 0, [[1024, 128], [128 * 1024, 8], [1, 1024]])),
            writes=[B_("wob")])
        for b in range(B):
            for tt in range(4):
                jt = b * 4 + tt
                k = tt % 2
                ci = k * 8
                cb = B_(f"colL{k}")
                ln_stats(x_tc[:, jt, :], B_("x_tc"), ci, cb)
                p.op("s", lambda h, jt=jt, ci=ci: h.activation(
                    xltb2[:, :], x_tc[:, jt, :], AF.Identity,
                    bias=cols[:, ci + 1:ci + 2], scale=cols[:, ci:ci + 1]),
                    reads=[B_("x_tc"), cb], writes=[B_("xltb2")])
                for half in range(2):
                    bki = 4 + (tt + half) % 2
                    bk = bank[bki]
                    for kc in range(4):
                        kg = half * 4 + kc
                        p.op("t", lambda h, kg=kg, kc=kc, bki=bki: h.matmul(
                            pv(bki, kc * 128, (kc + 1) * 128),
                            xltb2[:, kg * 128:(kg + 1) * 128], ident[:, :],
                            start=True, stop=True),
                            reads=[B_("xltb2"), B_("ident")], writes=[bk])
                    p.op("s", lambda h, tt=tt, half=half, bki=bki: h.copy(
                        A(xctb, half * 4 * 512 + tt * 128, [[8 * 512, 128], [512, 4], [1, 128]]),
                        pv(bki).rearrange("p (a c) -> p a c", a=4)),
                        reads=[bk], writes=[B_("xctb")])
            for et in range(8):
                bki = et % 2
                bk = bank[bki]
                for kc in range(8):
                    p.op("t", lambda h, et=et, kc=kc, bki=bki: h.matmul(
                        pv(bki), wib[:, kc, et * 128:(et + 1) * 128],
                        xctb[:, kc, :], start=(kc == 0), stop=(kc == 7)),
                        reads=[B_("wib"), B_("xctb")], writes=[bk])
                p.op("s", lambda h, et=et, b=b, bki=bki: h.activation(
                    ub[:, et, b * 512:(b + 1) * 512], pv(bki), AF.Silu,
                    bias=binc[:, et:et + 1]), reads=[bk, B_("binc")], writes=[B_("ub")])
        for b in range(B):
            for et in range(8):
                if b == 0:
                    p.op("v", lambda h, et=et: h.tensor_scalar(
                        vbb[:, et, :], ub[:, et, 0:512], kv(et, KSL[0]), None, ALU.mult),
                        reads=[B_("ub"), kb_], writes=[B_("vbb")])
                else:
                    p.op("v", lambda h, et=et, b=b: h.tensor_scalar(
                        cvt0[:, :], ub[:, et, 0:512], kv(et, KSL[b]), None, ALU.mult),
                        reads=[B_("ub"), kb_], writes=[B_("cvt0")])
                    for s_ in range(1, b):
                        p.op("v", lambda h, et=et, b=b, s_=s_: h.scalar_tensor_tensor(
                            cvt0[:, :], ub[:, et, s_ * 512:(s_ + 1) * 512],
                            kv(et, KSL[b - s_]), cvt0[:, :], ALU.mult, ALU.add),
                            reads=[B_("ub"), kb_, B_("cvt0")], writes=[B_("cvt0")])
                    p.op("v", lambda h, et=et, b=b: h.scalar_tensor_tensor(
                        vbb[:, et, :], ub[:, et, b * 512:(b + 1) * 512], kv(et, KSL[0]),
                        cvt0[:, :], ALU.mult, ALU.add),
                        reads=[B_("ub"), kb_, B_("cvt0")], writes=[B_("vbb")])
            for jt4 in range(4):
                for cc in range(2):
                    bki = (jt4 * 2 + cc) % 2
                    bk = bank[bki]
                    for dc in range(8):
                        p.op("t", lambda h, jt4=jt4, cc=cc, dc=dc, bki=bki: h.matmul(
                            pv(bki), vbb[:, dc, jt4 * 128:(jt4 + 1) * 128],
                            wob[:, dc, cc * 512:(cc + 1) * 512],
                            start=(dc == 0), stop=False),
                            reads=[B_("vbb"), B_("wob")], writes=[bk])
                    p.op("t", lambda h, cc=cc, bki=bki: h.matmul(
                        pv(bki), ones_row[0:1, 0:128], bout[0:1, cc * 512:(cc + 1) * 512],
                        start=False, stop=True),
                        reads=[B_("ones_row"), B_("bout")], writes=[bk])
                    ttb_ = tts[bki]
                    ttbb = B_(f"tts{bki}")
                    p.op("s", lambda h, ttb_=ttb_, bki=bki: h.activation(
                        ttb_[:, :], pv(bki), AF.Tanh), reads=[bk], writes=[ttbb])
                    j = b * 4 + jt4
                    p.op("v", lambda h, j=j, cc=cc, ttb_=ttb_: h.tensor_tensor(
                        x_tc[:, j, cc * 512:(cc + 1) * 512],
                        x_tc[:, j, cc * 512:(cc + 1) * 512], ttb_[:, :], ALU.add),
                        reads=[ttbb, B_("x_tc")], writes=[B_("x_tc")])

    import os as _os2
    KBAR = _os2.environ.get("KBAR", "1") == "1"
    KDMA = _os2.environ.get("KDMA", "1") == "1"
    if "wq" in stages:
        plan_wquant()
    if "attn" in stages:
        plan_attn()
    if KBAR:
        p.barrier(dma=KDMA)
    if "ffn1" in stages:
        plan_ffn(0)
    if KBAR:
        p.barrier(dma=KDMA)
    if "attnb" in stages:
        plan_attnb()
    if KBAR:
        p.barrier(dma=KDMA)
    if "ffn2" in stages:
        plan_ffn(1)

    p.dma("y", lambda h: h.dma_start(
        A(out_d.tensor, 0, [[1024, 128], [128 * 1024, 16], [1, 1024]]),
        A(x_tc, 0, [[16 * 1024, 128], [1024, 16], [1, 1024]])),
        reads=[B_("x_tc")], writes=[B_("out")])

    def final_waits(h):
        for name, val in p.dma_cnt.items():
            h.wait_ge(p.sems[name], val)
        return h.nop()
    p.q["y"].append(final_waits)

    p.assign_values()
    globals()["LAST_P"] = p
    with nc.Block() as block:
        @block.tensor
        def _(h):
            for f in p.q["t"]:
                f(h)

        @block.vector
        def _(h):
            for f in p.q["v"]:
                f(h)

        @block.scalar
        def _(h):
            for f in p.q["s"]:
                f(h)

        @block.sync
        def _(h):
            for f in p.q["y"]:
                f(h)

        @block.gpsimd
        def _(h):
            for f in p.q["g"]:
                f(h)

    stack.close()
    return nc


def _stage_inputs(kw):
    X = np.ascontiguousarray(np.asarray(kw["L0_attn_a_X"], np.float32))
    wq = np.ascontiguousarray(np.asarray(kw["L0_attn_a_Wq"], np.float32))
    wk = np.ascontiguousarray(np.asarray(kw["L0_attn_a_Wk"], np.float32))
    wv = np.ascontiguousarray(np.asarray(kw["L0_attn_a_Wv"], np.float32))
    wr = np.asarray(kw["L0_attn_a_Wr"], np.float32)
    wrp = np.zeros((E, 512), np.float32)
    for hh in range(H):
        wrp[:, 32 * hh:32 * hh + 10] = wr[:, 10 * hh:10 * hh + 10]
    wpj = np.ascontiguousarray(np.asarray(kw["L0_attn_a_Wproj"], np.float32))
    bq = np.asarray(kw["L0_attn_a_bq"], np.float32)
    bqc = np.ascontiguousarray(bq.reshape(8, 128).T)
    br = np.asarray(kw["L0_attn_a_br"], np.float32)
    brp = np.zeros((512,), np.float32)
    for hh in range(H):
        brp[32 * hh:32 * hh + 10] = br[10 * hh:10 * hh + 10]
    brc = np.ascontiguousarray(brp.reshape(4, 128).T)
    bpj = np.ascontiguousarray(np.asarray(kw["L0_attn_a_bproj"], np.float32).reshape(1, E))
    bndr = np.ascontiguousarray(np.asarray(kw["L0_attn_a_b_nd"], np.float32)[:, ::-1])
    winT = np.ascontiguousarray(np.asarray(kw["L0_attn_b_W_in"], np.float32).T)
    woutT = np.ascontiguousarray(np.asarray(kw["L0_attn_b_W_out"], np.float32).T)
    binc = np.ascontiguousarray(np.asarray(kw["L0_attn_b_b_in"], np.float32).reshape(8, 128).T)
    bout = np.ascontiguousarray(np.asarray(kw["L0_attn_b_b_out"], np.float32).reshape(1, E))
    gwc = np.ascontiguousarray(np.asarray(kw["L0_attn_b_weight"], np.float32).reshape(8, 128).T)
    wgT = np.ascontiguousarray(np.asarray(kw["L0_ffn_w_gate"], np.float32).T)
    wuT = np.ascontiguousarray(np.asarray(kw["L0_ffn_w_up"], np.float32).T)
    wdT = np.ascontiguousarray(np.asarray(kw["L0_ffn_w_down"], np.float32).T)
    idm = np.eye(128, dtype=bfd)

    maps = []
    for c in range(NCORE):
        lo = 512 * c - 1024
        hi = 512 * c + 512
        xs = np.zeros((B, SPAN, E), np.float32)
        s0 = max(0, lo)
        xs[:, s0 - lo:, :] = X[:, s0:hi, :]
        edge = np.zeros((1, 1024), np.float32)
        npad = max(0, -lo)
        edge[0, :npad] = NEG
        maps.append(dict(
            xsp=np.ascontiguousarray(xs), wq=wq, wk=wk, wv=wv, wrp=wrp, wpj=wpj,
            bqc=bqc, brc=brc, bpj=bpj, bndr=bndr, edge=np.ascontiguousarray(edge),
            winT=winT, woutT=woutT, binc=binc, bout=bout, gwc=gwc,
            wgT=wgT, wuT=wuT, wdT=wdT, idm=idm,
        ))
    return maps


def kernel(**kw):
    maps = _stage_inputs(kw)
    nc = build_program()
    res = run_bass_kernel_spmd(nc, maps, core_ids=list(range(NCORE)))
    out = np.empty((B, T, E), np.float32)
    for c in range(NCORE):
        o = np.asarray(res.results[c]["out"]).reshape(16, 128, E)
        for b in range(B):
            out[b, 512 * c:512 * (c + 1), :] = o[4 * b:4 * b + 4].reshape(512, E)
    return out


# version 08
# speedup vs baseline: 1.1305x; 1.0213x over previous
import numpy as np
import ml_dtypes
import contextlib
import concourse.bass as bass
import concourse.mybir as mybir
from concourse.bass_utils import run_bass_kernel_spmd

F32 = mybir.dt.float32
BF16 = mybir.dt.bfloat16
I32 = mybir.dt.int32
AF = mybir.ActivationFunctionType
ALU = mybir.AluOpType
AX = mybir.AxisListType
bfd = ml_dtypes.bfloat16

NCORE = 8
B = 4
T = 4096
E = 1024
H = 16
NB = 10
SPAN = 1536
NKB = 12
NEG = -1.0e9
INNER = 4096
CROW = 2048
LPAD = 512
OWN0 = 1024

ENGS = ("t", "v", "s", "y", "g")


class Cell:
    __slots__ = ("eng", "idx", "inc", "val")
    def __init__(self, eng, idx):
        self.eng = eng; self.idx = idx; self.inc = False; self.val = None


class DmaTok:
    __slots__ = ("sem", "val")
    def __init__(self, sem, val):
        self.sem = sem; self.val = val


class Buf:
    __slots__ = ("name", "w", "readers")
    def __init__(self, name):
        self.name = name; self.w = None; self.readers = []


class P:
    def __init__(self, nc):
        self.nc = nc
        self.log = {}
        self.q = {e: [] for e in ENGS}
        self.opidx = {e: 0 for e in ENGS}
        self.lastdrain = {e: -1 for e in ENGS}
        self.wm = {}
        self.dma_wm = {}
        self.cells = {e: [] for e in ENGS}
        self.sems = {}
        self.dma_cnt = {}
        self.dma_names = {"y": ["dy0", "dy1"], "g": ["dg0", "dg1"], "s": ["ds0", "ds1"]}
        self.dma_rr = {"y": 0, "g": 0, "s": 0}
        self.bufs = {}

    def B(self, name):
        if name not in self.bufs:
            self.bufs[name] = Buf(name)
        return self.bufs[name]

    def _await(self, eng, tk, raw=True):
        if isinstance(tk, DmaTok):
            key = (eng, tk.sem)
            if self.dma_wm.get(key, 0) >= tk.val:
                return
            self.dma_wm[key] = tk.val
            sem_name, val = tk.sem, tk.val
            self.log.setdefault(eng, []).append(("waitD", sem_name, val))
            self.q[eng].append(lambda h: h.wait_ge(self.sems[sem_name], val))
        elif tk.eng == eng:
            if eng == "t" or not raw:
                return
            if tk.idx > self.lastdrain[eng]:
                self.log.setdefault(eng, []).append(("drain",))
                self.q[eng].append(lambda h: h.drain())
                self.lastdrain[eng] = self.opidx[eng] - 1
        else:
            tk.inc = True
            key = (eng, tk.eng)
            if self.wm.get(key, -1) >= tk.idx:
                return
            self.wm[key] = tk.idx
            peng = tk.eng
            self.log.setdefault(eng, []).append(("waitE", peng, tk))
            self.q[eng].append(lambda h, tk=tk, peng=peng: h.wait_ge(self.sems[peng], tk.val))

    def _deps(self, eng, reads, writes):
        for tk in getattr(self, "fence", ()):
            self._await(eng, tk)
        for b in reads:
            if b.w is not None:
                self._await(eng, b.w)
        for b in writes:
            if b.w is not None:
                self._await(eng, b.w, raw=False)
            for tk in b.readers:
                self._await(eng, tk)

    def op(self, eng, emitfn, reads=(), writes=()):
        self._deps(eng, reads, writes)
        cell = Cell(eng, self.opidx[eng])
        def run(h, cell=cell, emitfn=emitfn, self_=self):
            ins = emitfn(h)
            if cell.inc:
                ins.then_inc(self_.sems[cell.eng], 1)
        self.q[eng].append(run)
        self.cells[eng].append(cell)
        import traceback as _tb
        fr = _tb.extract_stack()[-2]
        self.log.setdefault(eng, []).append(("op", cell, f"{fr.name}:{fr.lineno}",
                                             [b.name for b in reads], [b.name for b in writes]))
        self.opidx[eng] += 1
        for b in writes:
            b.w = cell; b.readers = []
        for b in reads:
            b.readers.append(cell)
        return cell

    def dma(self, qeng, emitfn, reads=(), writes=()):
        self._deps(qeng, reads, writes)
        name = self.dma_names[qeng][self.dma_rr[qeng]]
        self.dma_rr[qeng] ^= 1
        val = self.dma_cnt.get(name, 0) + 16
        self.dma_cnt[name] = val
        tok = DmaTok(name, val)
        def run(h, emitfn=emitfn, name=name, self_=self):
            ins = emitfn(h)
            ins.then_inc(self_.sems[name], 16)
        self.q[qeng].append(run)
        self.log.setdefault(qeng, []).append(("dma", name, val,
                                              [b.name for b in reads], [b.name for b in writes]))
        self.opidx[qeng] += 1
        for b in writes:
            b.w = tok; b.readers = []
        for b in reads:
            b.readers.append(tok)
        return tok

    def barrier(self, dma=True):
        best = {}
        for b in self.bufs.values():
            toks = list(b.readers)
            if b.w is not None:
                toks.append(b.w)
            for tk in toks:
                if isinstance(tk, DmaTok):
                    key = ("d", tk.sem)
                    cur = best.get(key)
                    if cur is None or tk.val > cur.val:
                        best[key] = tk
                else:
                    key = ("e", tk.eng)
                    cur = best.get(key)
                    if cur is None or tk.idx > cur.idx:
                        best[key] = tk
        for sem, val in self.dma_cnt.items():
            key = ("d", sem)
            cur = best.get(key)
            if cur is None or val > cur.val:
                best[key] = DmaTok(sem, val)
        self.fence = list(best.values())

    def assign_values(self):
        for e in ENGS:
            v = 0
            for c in self.cells[e]:
                if c.inc:
                    v += 1
                    c.val = v


def build_program(stages=("wq", "attn", "ffn1", "attnb", "ffn2")):
    nc = bass.Bass("TRN2", target_bir_lowering=False, detect_race_conditions=False,
                   dynamic_dma_scratch_size=16384)

    xsp = nc.dram_tensor("xsp", [B, SPAN, E], F32, kind="ExternalInput").ap()
    wq_d = nc.dram_tensor("wq", [E, E], F32, kind="ExternalInput").ap()
    wk_d = nc.dram_tensor("wk", [E, E], F32, kind="ExternalInput").ap()
    wv_d = nc.dram_tensor("wv", [E, E], F32, kind="ExternalInput").ap()
    wrp_d = nc.dram_tensor("wrp", [E, 512], F32, kind="ExternalInput").ap()
    wpj_d = nc.dram_tensor("wpj", [E, E], F32, kind="ExternalInput").ap()
    bqc_d = nc.dram_tensor("bqc", [128, 8], F32, kind="ExternalInput").ap()
    brc_d = nc.dram_tensor("brc", [128, 4], F32, kind="ExternalInput").ap()
    bpj_d = nc.dram_tensor("bpj", [1, E], F32, kind="ExternalInput").ap()
    bndr_d = nc.dram_tensor("bndr", [NB, 1024], F32, kind="ExternalInput").ap()
    edge_d = nc.dram_tensor("edge", [1, 1024], F32, kind="ExternalInput").ap()
    winT_d = nc.dram_tensor("winT", [E, E], F32, kind="ExternalInput").ap()
    woutT_d = nc.dram_tensor("woutT", [E, E], F32, kind="ExternalInput").ap()
    binc_d = nc.dram_tensor("binc", [128, 8], F32, kind="ExternalInput").ap()
    bout_d = nc.dram_tensor("bout", [1, E], F32, kind="ExternalInput").ap()
    gwc_d = nc.dram_tensor("gwc", [128, 8], F32, kind="ExternalInput").ap()
    wgT_d = nc.dram_tensor("wgT", [E, INNER], F32, kind="ExternalInput").ap()
    wuT_d = nc.dram_tensor("wuT", [E, INNER], F32, kind="ExternalInput").ap()
    wdT_d = nc.dram_tensor("wdT", [INNER, E], F32, kind="ExternalInput").ap()
    id_d = nc.dram_tensor("idm", [128, 128], BF16, kind="ExternalInput").ap()
    out_d = nc.dram_tensor("out", [2048, E], F32, kind="ExternalOutput").ap()
    import os as _osd
    KDBG = int(_osd.environ.get("KDBG", "0"))
    if KDBG == 1:
        dbgE = nc.dram_tensor("dbgE", [128, 4 * 1152], BF16, kind="ExternalOutput").ap()
        dbgW = nc.dram_tensor("dbgW", [128, 4 * 512], BF16, kind="ExternalOutput").ap()
        dbgA = nc.dram_tensor("dbgA", [128, 4 * 65], F32, kind="ExternalOutput").ap()
        dbgS = nc.dram_tensor("dbgS", [128, 4 * 1024], BF16, kind="ExternalOutput").ap()
    if KDBG == 2:
        dbgR = nc.dram_tensor("dbgR", [128, 4 * 1024], F32, kind="ExternalOutput").ap()
        dbgT = nc.dram_tensor("dbgT", [128, 8 * 512], BF16, kind="ExternalOutput").ap()
        dbgO = nc.dram_tensor("dbgO", [128, 4 * 1024], F32, kind="ExternalOutput").ap()

    cdr = nc.dram_tensor("cdr", [2 * 512 * CROW], BF16, kind="Internal").ap()
    wgQ = nc.dram_tensor("wgQ", [E * INNER], BF16, kind="Internal").ap()
    wuQ = nc.dram_tensor("wuQ", [E * INNER], BF16, kind="Internal").ap()
    wdQ = nc.dram_tensor("wdQ", [INNER * E], BF16, kind="Internal").ap()
    scl_d = nc.dram_tensor("scl", [1, 4], F32, kind="Internal").ap()

    p = P(nc)
    B_ = p.B
    A = bass.AP

    PB = ((nc.sbuf_base + 63) // 64) * 64

    def at(name, shape, dt, off):
        return nc.alloc_sbuf_tensor_at(name, shape, dt, offset=PB + off)

    x_tc = at("x_tc", [128, 16, 1024], F32, 0)
    junk = at("junk", [128, 1024], BF16, 65536)
    ident = at("ident", [128, 128], BF16, 67584)
    zero128 = at("zero128", [128, 128], BF16, 67840)
    ones_row = at("ones_row", [1, 512], BF16, 68096)
    cols = at("cols", [128, 64], F32, 69120)
    bqc = at("bqc_s", [128, 8], F32, 69376)
    brc = at("brc_s", [128, 4], F32, 69408)
    binc = at("binc_s", [128, 8], F32, 69440)
    gwc = at("gwc_s", [128, 8], F32, 69472)
    kcv = at("kcv", [128, 32], F32, 69504)
    wscal = at("wscal", [128, 8], F32, 69632)

    OV = 69696
    qwf = [at(f"qwf{i}", [128, 1024], F32, OV + i * 4096) for i in range(2)]
    qwq = [at(f"qwq{i}", [128, 1024], BF16, OV + 8192 + i * 2048) for i in range(2)]
    qwt = [at(f"qwt{i}", [128, 1024], BF16, OV + 12288 + i * 2048) for i in range(2)]

    xt0 = at("xt0", [128, 1024], F32, OV + 0)
    cst4 = at("cst4", [128, 4, 1024], BF16, OV + 0)
    xt1 = at("xt1", [128, 1024], F32, OV + 4096)
    xlt = at("xlt", [128, 1024], BF16, OV + 8192)
    qpe = at("qpe", [128, 512], BF16, OV + 8192)
    qpo = at("qpo", [128, 512], BF16, OV + 9216)
    xct = at("xct", [128, 2 * 8 * 512], BF16, OV + 10240)
    wslot0 = at("wslot0", [128, 8, 1024], BF16, OV + 26624)
    wslot1 = at("wslot1", [128, 8, 512], BF16, OV + 43008)
    kp = at("kp", [128, 8, 1536], BF16, OV + 51200)
    vt = at("vt", [128, 12, 1040], BF16, OV + 75776)
    q_sb = at("q_sb", [128, 8, 512], BF16, OV + 100736)
    rr = at("rr", [128, 4, 512], BF16, OV + 108928)
    rst = at("rst", [128, 512], BF16, OV + 113024)
    wsb = at("wsb", [128, 4, 512], BF16, OV + 114048)
    esb = at("esb", [128, 4, 1152], BF16, OV + 118144)
    act_ = at("act_", [128, 8, 512], BF16, OV + 118144)
    asb = at("asb", [128, 4, 1024], BF16, OV + 127360)
    brev = at("brev", [128, 1024], BF16, OV + 135552)
    edge_b = at("edge_b", [1, 1024], BF16, OV + 137600)
    bpj = at("bpj_s", [1, 1024], BF16, OV + 139648)

    hb = at("hb", [128, 8, 4096], BF16, OV + 0)
    xqc = at("xqc", [128, 8 * 1024], BF16, OV + 65536)
    xql = [at(f"xql{i}", [128, 1024], F32, OV + 81920 + i * 4096) for i in range(2)]
    xqi = [at(f"xqi{i}", [128, 1024], BF16, OV + 90112 + i * 2048) for i in range(2)]
    rti = [at(f"rti{i}", [128, 512], I32, OV + 94208 + i * 2048) for i in range(2)]
    hot = at("hot", [128, 128, 128], BF16, OV + 65536)
    rtf = [at(f"rtf{i}", [128, 512], F32, OV + 98304 + i * 2048) for i in range(2)]
    rtg = [at(f"rtg{i}", [128, 512], BF16, OV + 102400 + i * 1024) for i in range(2)]
    gbuf = [at(f"gbf{i}", [128, 512], F32, OV + 104448 + i * 2048) for i in range(3)]
    wfs = [at(f"wfs{i}", [128, 8, 512], BF16, OV + 110592 + i * 8192) for i in range(3)]
    wdfs = [at(f"wdfs{i}", [128, 512], BF16, OV + 135168 + i * 1024) for i in range(2)]

    ub = at("ub", [128, 8, 2048], BF16, OV + 0)
    vbb = at("vbb", [128, 8, 512], BF16, OV + 32768)
    xctb = at("xctb", [128, 8, 512], BF16, OV + 40960)
    xltb2 = at("xltb2", [128, 1024], BF16, OV + 49152)
    wib = at("wib", [128, 8, 1024], BF16, OV + 51200)
    wob = at("wob", [128, 8, 1024], BF16, OV + 67584)
    cvt0 = at("cvt0", [128, 512], F32, OV + 83968)
    tts = [at(f"tts{i}", [128, 512], F32, OV + 86016 + i * 2048) for i in range(2)]
    bout = at("bout_s", [1, 1024], BF16, OV + 90112)

    stack = contextlib.ExitStack()
    pbig = [stack.enter_context(nc.psum_tensor(f"pbig{i}", [128, 1024], F32)) for i in range(4)]
    def pv(i, a=0, b2=512):
        return pbig[i // 2][:, (i % 2) * 512 + a:(i % 2) * 512 + b2]
    bank = [B_(f"bank{i}") for i in range(8)]

    for s_ in ("t", "v", "s", "y", "g", "dy0", "dy1", "dg0", "dg1", "ds0", "ds1"):
        p.sems[s_] = stack.enter_context(nc.semaphore("sem_" + s_))

    import os as _os
    KI = _os.environ.get("KI", "xin,smalls,pads").split(",")
    for _b in range(B if "xin" in KI else 0):
        p.dma("y", lambda h, _b=_b: h.dma_start(
            A(x_tc, _b * 4 * 1024, [[16 * 1024, 128], [1024, 4], [1, 1024]]),
            A(xsp.tensor, _b * SPAN * 1024 + OWN0 * 1024,
              [[1024, 128], [128 * 1024, 4], [1, 1024]])),
            writes=[B_("x_tc")])
    p.dma("y", lambda h: h.dma_start(ident[:, :], id_d), writes=[B_("ident")])
    if "smalls" in KI:
        p.dma("g", lambda h: h.dma_start(brev[0:NB, :], bndr_d), writes=[B_("brev")])
        p.dma("g", lambda h: h.dma_start(edge_b[:, :], edge_d), writes=[B_("edge_b")])
        p.dma("g", lambda h: h.dma_start(bpj[:, :], bpj_d), writes=[B_("bpj")])
        p.dma("y", lambda h: h.dma_start(bqc[:, :], bqc_d), writes=[B_("bqc")])
        p.dma("y", lambda h: h.dma_start(brc[:, :], brc_d), writes=[B_("brc")])
        p.dma("y", lambda h: h.dma_start(binc[:, :], binc_d), writes=[B_("binc")])
        p.dma("y", lambda h: h.dma_start(gwc[:, :], gwc_d), writes=[B_("gwc")])
    p.op("v", lambda h: h.memset(ones_row[:, :], 1.0), writes=[B_("ones_row")])
    p.op("v", lambda h: h.memset(zero128[:, :], 0.0), writes=[B_("zero128")])
    if "smalls" in KI:
        p.op("v", lambda h: h.tensor_scalar(bqc[:, :], bqc[:, :], 0.125, None, ALU.mult),
             reads=[B_("bqc")], writes=[B_("bqc")])
    p.op("v", lambda h: h.memset(junk[:, 0:512], NEG), writes=[B_("junk")])
    for par in range(2 if "pads" in KI else 0):
        for side in range(2):
            for rk in range(4):
                off = par * 512 * CROW + rk * 128 * CROW + side * 1536
                p.dma("g", lambda h, off=off: h.dma_start(
                    A(cdr.tensor, off, [[CROW, 128], [1, 512]]), junk[:, 0:512]),
                    reads=[B_("junk")], writes=[B_(f"cpad{par}")])

    def ln_stats(srcap, srcbuf, ci, colbuf):
        p.op("v", lambda h: h.reduce_sum(cols[:, ci + 2:ci + 3], srcap, axis=AX.X),
             reads=[srcbuf], writes=[colbuf])
        p.op("s", lambda h: h.activation(junk[:, :], srcap, AF.Square,
                                         accum_out=cols[:, ci + 3:ci + 4]),
             reads=[srcbuf], writes=[B_("junk"), colbuf])
        p.op("v", lambda h: h.tensor_scalar(cols[:, ci + 2:ci + 3], cols[:, ci + 2:ci + 3],
             1.0 / 1024, None, ALU.mult), reads=[colbuf], writes=[colbuf])
        p.op("v", lambda h: h.tensor_scalar(cols[:, ci + 3:ci + 4], cols[:, ci + 3:ci + 4],
             1.0 / 1024, None, ALU.mult), reads=[colbuf], writes=[colbuf])
        p.op("v", lambda h: h.tensor_tensor(cols[:, ci + 4:ci + 5], cols[:, ci + 2:ci + 3],
             cols[:, ci + 2:ci + 3], ALU.mult), reads=[colbuf], writes=[colbuf])
        p.op("v", lambda h: h.tensor_tensor(cols[:, ci + 3:ci + 4], cols[:, ci + 3:ci + 4],
             cols[:, ci + 4:ci + 5], ALU.subtract), reads=[colbuf], writes=[colbuf])
        p.op("v", lambda h: h.tensor_scalar(cols[:, ci + 3:ci + 4], cols[:, ci + 3:ci + 4],
             1e-5, None, ALU.add), reads=[colbuf], writes=[colbuf])
        p.op("s", lambda h: h.activation(cols[:, ci + 4:ci + 5], cols[:, ci + 3:ci + 4],
             AF.Ln), reads=[colbuf], writes=[colbuf])
        p.op("s", lambda h: h.activation(cols[:, ci:ci + 1], cols[:, ci + 4:ci + 5],
             AF.Exp, scale=-0.5), reads=[colbuf], writes=[colbuf])
        p.op("v", lambda h: h.tensor_tensor(cols[:, ci + 1:ci + 2], cols[:, ci + 2:ci + 3],
             cols[:, ci:ci + 1], ALU.mult), reads=[colbuf], writes=[colbuf])
        p.op("v", lambda h: h.tensor_scalar(cols[:, ci + 1:ci + 2], cols[:, ci + 1:ci + 2],
             -1.0, None, ALU.mult), reads=[colbuf], writes=[colbuf])

    def rnd_int8(srcap, srcbufs, scol, colbuf, dstap, dstbuf, ch):
        k = ch % 2
        ib, fb2, gb_ = rti[k], rtf[k], rtg[k]
        ibb, fbb, gbb = B_(f"rti{k}"), B_(f"rtf{k}"), B_(f"rtg{k}")
        p.op("v", lambda h: h.tensor_scalar(ib[:, :], srcap, scol, 384.0, ALU.mult, ALU.add),
             reads=list(srcbufs) + [colbuf], writes=[ibb])
        p.op("v", lambda h: h.scalar_tensor_tensor(fb2[:, :], srcap, scol, ib[:, :],
             ALU.mult, ALU.subtract), reads=list(srcbufs) + [colbuf, ibb], writes=[fbb])
        p.op("v", lambda h: h.tensor_single_scalar(gb_[:, :], fb2[:, :], -383.5, ALU.is_gt),
             reads=[fbb], writes=[gbb])
        p.op("v", lambda h: h.scalar_tensor_tensor(dstap, ib[:, :], -384.0, gb_[:, :],
             ALU.add, ALU.add), reads=[ibb, gbb], writes=[dstbuf])

    def plan_wquant():
        cw = B_("colW")
        mats = [(wgT_d, wgQ, 0), (wuT_d, wuQ, 1), (wdT_d, wdQ, 2)]
        csz = 128 * 1024
        nch = (E * INNER) // csz
        p.op("v", lambda h: h.memset(kcv[:, 30:31], 1.0), writes=[B_("kcvo")])
        for (src, dst, mi) in mats:
            acc = 36 + mi
            p.op("v", lambda h, acc=acc: h.memset(cols[:, acc:acc + 1], 0.0), writes=[cw])
            for ch in range(nch):
                k = ch % 2
                fb = B_(("xt0", "xt1")[k])
                p.dma("g", lambda h, src=src, ch=ch, k=k: h.dma_start(
                    qwf[k][:, :], A(src.tensor, ch * csz, [[1024, 128], [1, 1024]])),
                    writes=[fb])
                p.op("v", lambda h, k=k: h.reduce_sum(cols[:, 39:40], qwf[k][:, :],
                     axis=AX.X, apply_absolute_value=True), reads=[fb], writes=[cw])
                p.op("v", lambda h, acc=acc: h.tensor_tensor(cols[:, acc:acc + 1],
                     cols[:, acc:acc + 1], cols[:, 39:40], ALU.add), reads=[cw], writes=[cw])
            bk = bank[7]
            p.op("t", lambda h, acc=acc: h.matmul(pv(7, 0, 1)[0:1, :], cols[:, acc:acc + 1],
                 kcv[:, 30:31], start=True, stop=True), reads=[cw, B_("kcvo")], writes=[bk])
            p.op("s", lambda h, acc=acc: h.copy(cols[0:1, 39:40], pv(7, 0, 1)[0:1, :]),
                 reads=[bk], writes=[cw])
            p.op("v", lambda h, acc=acc: h.tensor_scalar(cols[0:1, acc:acc + 1],
                 cols[0:1, 39:40], 1.0 / (E * INNER), None, ALU.mult),
                 reads=[cw], writes=[cw])
            p.op("v", lambda h, acc=acc: h.tensor_scalar(cols[0:1, acc:acc + 1],
                 cols[0:1, acc:acc + 1], 1e-5, None, ALU.max), reads=[cw], writes=[cw])
        p.dma("y", lambda h: h.dma_start(A(scl_d.tensor, 0, [[1, 1], [1, 3]]),
              cols[0:1, 36:39]), reads=[cw], writes=[B_("scl")])
        p.dma("y", lambda h: h.dma_start(A(wscal, 0, [[8, 128], [1, 3]]),
              A(scl_d.tensor, 0, [[0, 128], [1, 3]])),
              reads=[B_("scl")], writes=[B_("wscal")])
        wsc = B_("wscal")
        for mi in range(3):
            p.op("v", lambda h, mi=mi: h.reciprocal(wscal[:, 3 + mi:4 + mi],
                 wscal[:, mi:mi + 1]), reads=[wsc], writes=[wsc])
        p.op("v", lambda h: h.tensor_scalar(wscal[:, 6:7], wscal[:, 3:4], -1.0, None,
             ALU.mult), reads=[wsc], writes=[wsc])
        p.op("v", lambda h: h.tensor_scalar(wscal[:, 7:8], wscal[:, 4:5], -1.0, None,
             ALU.mult), reads=[wsc], writes=[wsc])
        p.op("v", lambda h: h.tensor_scalar(kcv[:, 29:30], wscal[:, 5:6], -1.0, None,
             ALU.mult), reads=[wsc], writes=[B_("kcvq")])
        for (src, dst, mi) in mats:
            scol = wscal[:, 3 + mi:4 + mi]
            nscol = kcv[:, 29:30] if mi == 2 else wscal[:, 6 + mi:7 + mi]
            nsb = B_("kcvq") if mi == 2 else wsc
            for ch in range(nch):
                k = ch % 2
                fb, qb, tb = B_(("xt0", "xt1")[k]), B_(("qpxl", "xct0")[k]), B_("xct0")
                p.dma("g", lambda h, src=src, ch=ch, k=k: h.dma_start(
                    qwf[k][:, :], A(src.tensor, ch * csz, [[1024, 128], [1, 1024]])),
                    writes=[fb])
                p.op("v", lambda h, k=k, scol=scol: h.tensor_scalar(
                    qwq[k][:, :], qwf[k][:, :], scol, 0.5, ALU.mult, ALU.is_gt),
                    reads=[fb, wsc], writes=[qb])
                p.op("v", lambda h, k=k, nscol=nscol, nsb=nsb: h.tensor_scalar(
                    qwt[k][:, :], qwf[k][:, :], nscol, 0.5, ALU.mult, ALU.is_gt),
                    reads=[fb, nsb], writes=[tb])
                p.op("v", lambda h, k=k: h.tensor_tensor(
                    qwq[k][:, :], qwq[k][:, :], qwt[k][:, :], ALU.subtract),
                    reads=[qb, tb], writes=[qb])
                p.dma("g", lambda h, dst=dst, ch=ch, k=k: h.dma_start(
                    A(dst.tensor, ch * csz, [[1024, 128], [1, 1024]]), qwq[k][:, :]),
                    reads=[qb], writes=[B_(f"wQd{mi}")])

    def plan_attn():
        import os
        KB_ = int(os.environ.get("KBATCH", str(B)))
        KH_ = int(os.environ.get("KHEAD", str(H)))
        KPH = os.environ.get("KPH", "p1,p2,p3").split(",")
        KP1 = os.environ.get("KP1", "w,x,lns,lna,tr,xc,k,v,vm,qr").split(",")
        for b in range(KB_):
            if "w" in KP1:
                p.dma("g", lambda h: h.dma_start(
                    A(wslot0, 0, [[8 * 1024, 128], [1024, 8], [1, 1024]]),
                    A(wk_d.tensor, 0, [[1024, 128], [128 * 1024, 8], [1, 1024]])),
                    reads=[B_("wslot0")], writes=[B_("wslot0")])
            for nn in range(3):
                cx = nn % 2
                xcb = B_(f"xct{cx}")
                for stl in range(4):
                    st = nn * 4 + stl
                    k = st % 2
                    xt = (xt0, xt1)[k]
                    xtb = B_(f"xt{k}")
                    ci = k * 8
                    cb = B_(f"colL{k}")
                    p.dma("y", lambda h, b=b, st=st, xt=xt: h.dma_start(
                        xt[:, :], xsp[b, st * 128:(st + 1) * 128, :]), writes=[xtb])
                    if "lns" not in KP1:
                        continue
                    ln_stats(xt[:, :], xtb, ci, cb)
                    if "lna" not in KP1:
                        continue
                    p.op("s", lambda h, xt=xt, ci=ci: h.activation(
                        xlt[:, :], xt[:, :], AF.Identity,
                        bias=cols[:, ci + 1:ci + 2], scale=cols[:, ci:ci + 1]),
                        reads=[xtb, cb], writes=[B_("qpxl")])
                    if "tr" not in KP1:
                        continue
                    for half in range(2):
                        bki = 4 + (stl + half) % 2
                        bk = bank[bki]
                        for kc in range(4):
                            kg = half * 4 + kc
                            p.op("t", lambda h, kg=kg, kc=kc, bki=bki: h.matmul(
                                pv(bki, kc * 128, (kc + 1) * 128),
                                xlt[:, kg * 128:(kg + 1) * 128], ident[:, :],
                                start=True, stop=True),
                                reads=[B_("qpxl"), B_("ident")], writes=[bk])
                        p.op("s", lambda h, cx=cx, stl=stl, half=half, bki=bki: h.copy(
                            A(xct, cx * 8 * 512 + half * 4 * 512 + stl * 128,
                              [[2 * 8 * 512, 128], [512, 4], [1, 128]]),
                            pv(bki).rearrange("p (a c) -> p a c", a=4)),
                            reads=[bk], writes=[xcb])
                for ft in range(8 if "k" in KP1 else 0):
                    bki = ft % 2
                    bk = bank[bki]
                    for kc in range(8):
                        p.op("t", lambda h, ft=ft, kc=kc, bki=bki, cx=cx: h.matmul(
                            pv(bki), wslot0[:, kc, ft * 128:(ft + 1) * 128],
                            A(xct, cx * 8 * 512 + kc * 512, [[8192, 128], [1, 512]]),
                            start=(kc == 0), stop=(kc == 7)),
                            reads=[B_("wslot0"), xcb], writes=[bk])
                    p.op("s", lambda h, ft=ft, nn=nn, bki=bki: h.copy(
                        kp[:, ft, nn * 512:(nn + 1) * 512], pv(bki)),
                        reads=[bk], writes=[B_("kp")])
                for nn2 in range(2 if "v" in KP1 else 0):
                    p.dma("g", lambda h, nn2=nn2: h.dma_start(
                        A(wslot1, 0, [[8 * 512, 128], [512, 8], [1, 512]]),
                        A(wv_d.tensor, nn2 * 512, [[1024, 128], [128 * 1024, 8], [1, 512]])),
                        reads=[B_("wslot1")], writes=[B_("wslot1")])
                    for mtl in range(4):
                        mt = nn * 4 + mtl
                        bki = 2 + mtl % 2
                        bk = bank[bki]
                        for kc in range(8):
                            p.op("t", lambda h, mtl=mtl, kc=kc, bki=bki, cx=cx: h.matmul(
                                pv(bki),
                                A(xct, cx * 8 * 512 + kc * 512 + mtl * 128, [[8192, 128], [1, 128]]),
                                wslot1[:, kc, :], start=(kc == 0), stop=(kc == 7)),
                                reads=[B_("wslot1"), xcb], writes=[bk])
                        p.op("s", lambda h, mt=mt, nn2=nn2, bki=bki: h.copy(
                            A(vt, mt * 1040 + nn2 * 8 * 65, [[12 * 1040, 128], [65, 8], [1, 64]]),
                            pv(bki).rearrange("p (a c) -> p a c", a=8)),
                            reads=[bk], writes=[B_("vt")])
            if "vm" in KP1:
                p.op("v", lambda h: h.memset(
                    A(vt, 64, [[12 * 1040, 128], [1040, 12], [65, 16]]), 1.0),
                    writes=[B_("vt")])
            if "qr" not in KP1:
                continue
            p.dma("g", lambda h: h.dma_start(
                A(wslot0, 0, [[8 * 1024, 128], [1024, 8], [1, 1024]]),
                A(wq_d.tensor, 0, [[1024, 128], [128 * 1024, 8], [1, 1024]])),
                reads=[B_("wslot0")], writes=[B_("wslot0")])
            xcb2 = B_("xct0")
            for ft in range(8):
                bki = ft % 2
                bk = bank[bki]
                for kc in range(8):
                    p.op("t", lambda h, ft=ft, kc=kc, bki=bki: h.matmul(
                        pv(bki), wslot0[:, kc, ft * 128:(ft + 1) * 128],
                        A(xct, kc * 512, [[8192, 128], [1, 512]]),
                        start=(kc == 0), stop=(kc == 7)),
                        reads=[B_("wslot0"), xcb2], writes=[bk])
                p.op("s", lambda h, ft=ft, bki=bki: h.activation(
                    q_sb[:, ft, :], pv(bki), AF.Identity,
                    bias=bqc[:, ft:ft + 1], scale=0.125),
                    reads=[bk, B_("bqc")], writes=[B_("q_sb")])
            p.dma("g", lambda h: h.dma_start(
                A(wslot1, 0, [[8 * 512, 128], [512, 8], [1, 512]]),
                A(wrp_d.tensor, 0, [[512, 128], [128 * 512, 8], [1, 512]])),
                reads=[B_("wslot1")], writes=[B_("wslot1")])
            for rt in range(4):
                bki = 2 + rt % 2
                bk = bank[bki]
                for kc in range(8):
                    p.op("t", lambda h, rt=rt, kc=kc, bki=bki: h.matmul(
                        pv(bki), wslot1[:, kc, rt * 128:(rt + 1) * 128],
                        A(xct, kc * 512, [[8192, 128], [1, 512]]),
                        start=(kc == 0), stop=(kc == 7)),
                        reads=[B_("wslot1"), xcb2], writes=[bk])
                p.op("s", lambda h, rt=rt, bki=bki: h.activation(
                    rr[:, rt, :], pv(bki), AF.Identity, bias=brc[:, rt:rt + 1]),
                    reads=[bk, B_("brc")], writes=[B_("rr")])

            if "p2" not in KPH:
                continue
            p.op("v", lambda h: h.memset(qpe[64:128, :], 0.0), writes=[B_("qpxl")])
            p.op("v", lambda h: h.memset(qpo[0:64, :], 0.0), writes=[B_("qpxl")])
            for hh in range(KH_):
                par = hh % 2
                qpad = (qpe, qpo)[par]
                p.dma("g", lambda h, hh=hh: h.dma_start(
                    rst[0:10, :], rr[32 * (hh % 4):32 * (hh % 4) + 10, hh // 4, :]),
                    reads=[B_("rr")], writes=[B_("rst")])
                a0 = 64 * (hh % 2)
                p.op("v", lambda h, hh=hh, qpad=qpad, a0=a0: h.tensor_copy(
                    qpad[a0:a0 + 64, :], q_sb[a0:a0 + 64, hh // 2, :]),
                    reads=[B_("q_sb")], writes=[B_("qpxl")])
                cdb = B_(f"cdram{par}")
                for ct in range(4):
                    cbf = B_(("xt0", "xt1")[ct // 2])
                    for cc in range(2):
                        bki = 4 + cc
                        bk = bank[bki]
                        p.op("t", lambda h, ct=ct, cc=cc, bki=bki: h.matmul(
                            pv(bki), rst[0:10, ct * 128:(ct + 1) * 128],
                            brev[0:10, cc * 512:(cc + 1) * 512], start=True, stop=True),
                            reads=[B_("rst"), B_("brev")], writes=[bk])
                        p.op("s", lambda h, ct=ct, cc=cc, bki=bki: h.copy(
                            cst4[:, ct, cc * 512:(cc + 1) * 512], pv(bki)),
                            reads=[bk], writes=[cbf])
                p.dma("y", lambda h, par=par: h.dma_start(
                    A(cdr.tensor, par * 512 * CROW + LPAD,
                      [[CROW, 128], [128 * CROW, 4], [1, 1024]]), cst4[:, :, :]),
                    reads=[B_("xt0"), B_("xt1")], writes=[cdb])
                esbb = B_("esb")
                off = par * 512 * CROW + 511
                p.dma("y", lambda h, off=off: h.dma_start(
                    esb[:, :, :],
                    A(cdr.tensor, off, [[CROW - 1, 128], [128 * CROW, 4], [1, 1152]])),
                    reads=[cdb, B_(f"cpad{par}")], writes=[esbb])
                if KDBG == 1 and b == 0 and hh == 0:
                    p.dma("y", lambda h: h.dma_start(dbgE, esb[:, :, :]),
                          reads=[esbb], writes=[B_("dbgE")])
                p.op("t", lambda h: h.matmul(pbig[3][:, 0:260], zero128[:, :],
                     vt[:, 0, 0:260], start=True, stop=False),
                     reads=[B_("zero128"), B_("vt")], writes=[bank[6], bank[7]])
                wsbb = B_("wsb")

                def emit_qke(kb, hh, qpad):
                    sj = (kb // 2) % 2
                    bki2 = 2 * sj + (kb % 2)
                    bk = bank[bki2]
                    qts = [qt for qt in range(4) if 0 <= kb - qt <= 8]
                    p.op("t", lambda h, hh=hh, kb=kb, bki2=bki2, qpad=qpad: h.matmul(
                        pv(bki2), kp[:, hh // 2, kb * 128:(kb + 1) * 128],
                        qpad[:, :], start=True, stop=False),
                        reads=[B_("kp"), B_("qpxl")], writes=[bk])
                    if kb < 8:
                        p.op("t", lambda h, kb=kb, bki2=bki2: h.matmul(
                            pv(bki2), edge_b[0:1, kb * 128:(kb + 1) * 128],
                            ones_row[0:1, :], start=False, stop=False),
                            reads=[B_("edge_b"), B_("ones_row")], writes=[bk])
                    for i, qt in enumerate(qts):
                        p.op("t", lambda h, qt=qt, kb=kb, bki2=bki2,
                             last=(i == len(qts) - 1): h.matmul(
                            pv(bki2, qt * 128, (qt + 1) * 128),
                            esb[:, qt, (kb - qt) * 128:(kb - qt) * 128 + 128], ident[:, :],
                            start=False, stop=last),
                            reads=[esbb, B_("ident")], writes=[bk])

                def emit_exp(kb2):
                    sj = kb2 % 2
                    p.op("s", lambda h, kb2=kb2, sj=sj: h.activation(
                        A(wsb, (kb2 * 2 % 4) * 512, [[4 * 512, 128], [1, 1024]]),
                        pbig[sj][:, :], AF.Exp),
                        reads=[bank[2 * sj], bank[2 * sj + 1]], writes=[wsbb])

                def emit_av(kb, hh):
                    qts = [qt for qt in range(4) if 0 <= kb - qt <= 8]
                    for i, qt in enumerate(qts):
                        p.op("t", lambda h, hh=hh, kb=kb, qt=qt,
                             last=(kb == NKB - 1 and qt == 3): h.matmul(
                            pbig[3][:, qt * 65:qt * 65 + 65],
                            wsb[:, kb % 4, qt * 128:(qt + 1) * 128],
                            vt[:, kb, 65 * hh:65 * hh + 65],
                            start=False, stop=last),
                            reads=[wsbb, B_("vt")], writes=[bank[6], bank[7]])

                for kb2 in range(6):
                    emit_qke(2 * kb2, hh, qpad)
                    emit_qke(2 * kb2 + 1, hh, qpad)
                    emit_exp(kb2)
                    if kb2 >= 1:
                        emit_av(2 * (kb2 - 1), hh)
                        emit_av(2 * (kb2 - 1) + 1, hh)
                emit_av(10, hh)
                emit_av(11, hh)
                if KDBG == 1 and b == 0 and hh == 0:
                    p.dma("y", lambda h: h.dma_start(dbgW, wsb[:, :, :]),
                          reads=[wsbb], writes=[B_("dbgW")])
                if KDBG == 1 and b == 0 and hh == H - 1:
                    pass
                    p.op("s", lambda h: h.copy(junk[:, 0:260].bitcast(F32) if False else xt0[:, 0:260], pbig[3][:, 0:260]),
                         reads=[bank[6], bank[7]], writes=[B_("xt0")])
                    p.dma("y", lambda h: h.dma_start(dbgA, xt0[:, 0:260]),
                          reads=[B_("xt0")], writes=[B_("dbgA")])
                for qt in range(4):
                    p.op("s", lambda h, qt=qt: h.copy(
                        cols[:, 44 + qt:45 + qt], pbig[3][:, qt * 65 + 64:qt * 65 + 65]),
                        reads=[bank[6], bank[7]], writes=[B_("colD")])
                p.op("v", lambda h: h.reciprocal(cols[:, 32:36], cols[:, 44:48]),
                     reads=[B_("colD")], writes=[B_("colD")])
                for qt in range(4):
                    p.op("s", lambda h, qt=qt, hh=hh: h.activation(
                        asb[:, qt, 64 * hh:64 * hh + 64], pbig[3][:, qt * 65:qt * 65 + 64],
                        AF.Identity, scale=cols[:, 32 + qt:33 + qt]),
                        reads=[bank[6], bank[7], B_("colD")], writes=[B_("asb")])

            if KDBG == 1 and b == 0:
                p.dma("y", lambda h: h.dma_start(dbgS, asb[:, :, :]),
                      reads=[B_("asb")], writes=[B_("dbgS")])
            if KDBG == 2 and b == 0:
                p.dma("y", lambda h: h.dma_start(dbgR, x_tc[:, 0:4, :]),
                      reads=[B_("x_tc")], writes=[B_("dbgR")])
            if "p3" not in KPH:
                continue
            p.dma("g", lambda h: h.dma_start(
                A(wslot0, 0, [[8 * 1024, 128], [1024, 8], [1, 1024]]),
                A(wpj_d.tensor, 0, [[1024, 128], [128 * 1024, 8], [1, 1024]])),
                reads=[B_("wslot0")], writes=[B_("wslot0")])
            for hc in range(8):
                bki = 4 + hc % 2
                bk = bank[bki]
                for qt in range(4):
                    p.op("t", lambda h, hc=hc, qt=qt, bki=bki: h.matmul(
                        pv(bki, qt * 128, (qt + 1) * 128),
                        asb[:, qt, hc * 128:(hc + 1) * 128], ident[:, :],
                        start=True, stop=True),
                        reads=[B_("asb"), B_("ident")], writes=[bk])
                p.op("s", lambda h, hc=hc, bki=bki: h.copy(
                    act_[:, hc, :], pv(bki)), reads=[bk], writes=[B_("esb")])
            if KDBG == 2 and b == 0:
                p.dma("y", lambda h: h.dma_start(dbgT, act_[:, :, :]),
                      reads=[B_("esb")], writes=[B_("dbgT")])
            for jt in range(4):
                for cc in range(2):
                    bki = (jt * 2 + cc) % 2
                    bk = bank[bki]
                    for hc in range(8):
                        p.op("t", lambda h, jt=jt, cc=cc, hc=hc, bki=bki: h.matmul(
                            pv(bki), act_[:, hc, jt * 128:(jt + 1) * 128],
                            wslot0[:, hc, cc * 512:(cc + 1) * 512],
                            start=(hc == 0), stop=False),
                            reads=[B_("esb"), B_("wslot0")], writes=[bk])
                    p.op("t", lambda h, cc=cc, bki=bki: h.matmul(
                        pv(bki), ones_row[0:1, 0:128], bpj[0:1, cc * 512:(cc + 1) * 512],
                        start=False, stop=True),
                        reads=[B_("ones_row"), B_("bpj")], writes=[bk])
                    j = b * 4 + jt
                    p.op("s", lambda h, cc=cc, bki=bki: h.copy(
                        xt1[:, bki * 512:(bki + 1) * 512], pv(bki)),
                        reads=[bk], writes=[B_("xt1")])
                    p.op("v", lambda h, j=j, cc=cc, bki=bki: h.tensor_tensor(
                        x_tc[:, j, cc * 512:(cc + 1) * 512],
                        x_tc[:, j, cc * 512:(cc + 1) * 512],
                        xt1[:, bki * 512:(bki + 1) * 512], ALU.add),
                        reads=[B_("xt1"), B_("x_tc")], writes=[B_("x_tc")])
            if KDBG == 2 and b == 0:
                p.dma("y", lambda h: h.dma_start(dbgO, x_tc[:, 0:4, :]),
                      reads=[B_("x_tc")], writes=[B_("dbgO")])

    def plan_ffn(call):
        wsc = B_("wscal")
        cs = B_("colS")
        cu = B_("colU")
        for g in range(2):
            tiles = list(range(8 * g, 8 * g + 8))
            for ii, jt in enumerate(tiles):
                k = jt % 2
                xq_, xi_ = xql[k], xqi[k]
                xqlb, xqib = B_(f"xql{k}"), B_(f"xqi{k}")
                ci = k * 8
                cb = B_(f"colL{k}")
                ln_stats(x_tc[:, jt, :], B_("x_tc"), ci, cb)
                p.op("s", lambda h, jt=jt, xq_=xq_, ci=ci: h.activation(
                    xq_[:, :], x_tc[:, jt, :], AF.Identity,
                    bias=cols[:, ci + 1:ci + 2], scale=cols[:, ci:ci + 1]),
                    reads=[B_("x_tc"), cb], writes=[xqlb])
                p.op("v", lambda h, xq_=xq_, ci=ci: h.reduce_max(
                    cols[:, ci + 5:ci + 6], xq_[:, :], axis=AX.X,
                    apply_absolute_value=True), reads=[xqlb], writes=[cb])
                p.op("v", lambda h, ci=ci: h.tensor_scalar(cols[:, ci + 5:ci + 6],
                     cols[:, ci + 5:ci + 6], 1e-5, None, ALU.max), reads=[cb], writes=[cb])
                p.op("v", lambda h, ci=ci, ii=ii: h.tensor_scalar(cols[:, 40 + ii:41 + ii],
                     cols[:, ci + 5:ci + 6], 1.0 / 127.0, None, ALU.mult),
                     reads=[cb], writes=[cs])
                p.op("v", lambda h, ci=ci: h.reciprocal(cols[:, ci + 6:ci + 7],
                     cols[:, ci + 5:ci + 6]), reads=[cb], writes=[cb])
                p.op("v", lambda h, ci=ci: h.tensor_scalar(cols[:, ci + 6:ci + 7],
                     cols[:, ci + 6:ci + 7], 127.0, None, ALU.mult), reads=[cb], writes=[cb])
                for ch in range(2):
                    sl = slice(ch * 512, (ch + 1) * 512)
                    rnd_int8(xq_[:, sl], [xqlb], cols[:, ci + 6:ci + 7], cb,
                             xi_[:, sl], xqib, ch)
                for half in range(2):
                    bki = 4 + (ii + half) % 2
                    bk = bank[bki]
                    for kc in range(4):
                        kg = half * 4 + kc
                        p.op("t", lambda h, xi_=xi_, kg=kg, kc=kc, bki=bki: h.matmul(
                            pv(bki, kc * 128, (kc + 1) * 128),
                            xi_[:, kg * 128:(kg + 1) * 128], ident[:, :],
                            start=True, stop=True),
                            reads=[xqib, B_("ident")], writes=[bk])
                    p.op("s", lambda h, ii=ii, half=half, bki=bki: h.copy(
                        A(xqc, ii * 1024 + half * 512, [[8192, 128], [1, 512]]),
                        pv(bki)), reads=[bk], writes=[B_("xqc")])
                p.op("v", lambda h, ii=ii: h.tensor_tensor(cols[:, 48 + ii:49 + ii],
                     cols[:, 40 + ii:41 + ii], wscal[:, 0:1], ALU.mult),
                     reads=[cs, wsc], writes=[cs])
                p.op("v", lambda h, ii=ii: h.tensor_tensor(cols[:, 24 + ii:25 + ii],
                     cols[:, 40 + ii:41 + ii], wscal[:, 1:2], ALU.mult),
                     reads=[cs, wsc], writes=[cu])
            for oc in range(8):
                k = oc % 2
                wg_s = wfs[k]
                wu_s = wfs[2]
                wgb, wub = B_(f"wfs{k}"), B_("wfs2")
                p.dma("g", lambda h, oc=oc, wg_s=wg_s: h.dma_start(
                    A(wg_s, 0, [[8 * 512, 128], [512, 8], [1, 512]]),
                    A(wgQ.tensor, oc * 512, [[INNER, 128], [128 * INNER, 8], [1, 512]])),
                    reads=[wgb], writes=[wgb])
                p.dma("g", lambda h, oc=oc, wu_s=wu_s: h.dma_start(
                    A(wu_s, 0, [[8 * 512, 128], [512, 8], [1, 512]]),
                    A(wuQ.tensor, oc * 512, [[INNER, 128], [128 * INNER, 8], [1, 512]])),
                    reads=[wub], writes=[wub])
                for ii in range(8):
                    bg = ii % 2
                    bu = 2 + ii % 2
                    for kc in range(8):
                        p.op("t", lambda h, ii=ii, kc=kc, wg_s=wg_s, bg=bg: h.matmul(
                            pv(bg), A(xqc, ii * 1024 + kc * 128, [[8192, 128], [1, 128]]),
                            wg_s[:, kc, :], start=(kc == 0), stop=(kc == 7)),
                            reads=[B_("xqc"), wgb], writes=[bank[bg]])
                    for kc in range(8):
                        p.op("t", lambda h, ii=ii, kc=kc, wu_s=wu_s, bu=bu: h.matmul(
                            pv(bu), A(xqc, ii * 1024 + kc * 128, [[8192, 128], [1, 128]]),
                            wu_s[:, kc, :], start=(kc == 0), stop=(kc == 7)),
                            reads=[B_("xqc"), wub], writes=[bank[bu]])
                    gb_, ub_ = gbuf[ii % 2], gbuf[2]
                    gbb, ubb = B_(f"gbf{ii % 2}"), B_("gbf2")
                    p.op("s", lambda h, gb_=gb_, ii=ii, bg=bg: h.activation(
                        gb_[:, :], pv(bg), AF.Silu, scale=cols[:, 48 + ii:49 + ii]),
                        reads=[bank[bg], cs], writes=[gbb])
                    p.op("s", lambda h, ub_=ub_, ii=ii, bu=bu: h.activation(
                        ub_[:, :], pv(bu), AF.Identity, scale=cols[:, 24 + ii:25 + ii]),
                        reads=[bank[bu], cu], writes=[ubb])
                    p.op("v", lambda h, gb_=gb_, ub_=ub_, ii=ii, oc=oc: h.tensor_tensor(
                        hb[:, ii, oc * 512:(oc + 1) * 512], gb_[:, :], ub_[:, :], ALU.mult),
                        reads=[gbb, ubb], writes=[B_("hb")])
            for ii in range(8):
                hm = 16 + (ii % 2) * 4
                cb = B_(f"colH{ii % 2}")
                p.op("v", lambda h, ii=ii, hm=hm: h.reduce_max(cols[:, hm:hm + 1],
                     hb[:, ii, :], axis=AX.X, apply_absolute_value=True),
                     reads=[B_("hb")], writes=[cb])
                p.op("v", lambda h, hm=hm: h.tensor_scalar(cols[:, hm:hm + 1],
                     cols[:, hm:hm + 1], 1e-5, None, ALU.max), reads=[cb], writes=[cb])
                p.op("v", lambda h, hm=hm, ii=ii: h.tensor_scalar(cols[:, 56 + ii:57 + ii],
                     cols[:, hm:hm + 1], 1.0 / 127.0, None, ALU.mult), reads=[cb], writes=[cs])
                p.op("v", lambda h, ii=ii: h.tensor_tensor(cols[:, 56 + ii:57 + ii],
                     cols[:, 56 + ii:57 + ii], wscal[:, 2:3], ALU.mult),
                     reads=[cs, wsc], writes=[cs])
                p.op("v", lambda h, hm=hm: h.reciprocal(cols[:, hm + 1:hm + 2],
                     cols[:, hm:hm + 1]), reads=[cb], writes=[cb])
                p.op("v", lambda h, hm=hm: h.tensor_scalar(cols[:, hm + 1:hm + 2],
                     cols[:, hm + 1:hm + 2], 127.0, None, ALU.mult), reads=[cb], writes=[cb])
                for ch in range(8):
                    sl = slice(ch * 512, (ch + 1) * 512)
                    rnd_int8(hb[:, ii, sl], [B_("hb")], cols[:, hm + 1:hm + 2], cb,
                             hb[:, ii, sl], B_("hb"), ch)
            for half in range(2):
                p.barrier(dma=KDMA)
                for ii in range(4):
                    ti = half * 4 + ii
                    for qtr in range(8):
                        bki = 4 + (ii * 8 + qtr) % 2
                        bk = bank[bki]
                        for kc in range(4):
                            oc32 = qtr * 4 + kc
                            p.op("t", lambda h, ti=ti, oc32=oc32, kc=kc, bki=bki: h.matmul(
                                pv(bki, kc * 128, (kc + 1) * 128),
                                hb[:, ti, oc32 * 128:(oc32 + 1) * 128], ident[:, :],
                                start=True, stop=True),
                                reads=[B_("hb"), B_("ident")], writes=[bk])
                        p.op("s", lambda h, ii=ii, qtr=qtr, bki=bki: h.copy(
                            A(hot, (ii * 32 + qtr * 4) * 128, [[16384, 128], [1, 512]]),
                            pv(bki)), reads=[bk], writes=[B_("hot")])
                for cc in range(2):
                    for oc32 in range(32):
                        k = oc32 % 2
                        wd_s = wdfs[k]
                        wdb = B_(f"wdfs{k}")
                        p.dma("g", lambda h, cc=cc, oc32=oc32, wd_s=wd_s: h.dma_start(
                            wd_s[:, :],
                            A(wdQ.tensor, oc32 * 128 * E + cc * 512, [[E, 128], [1, 512]])),
                            reads=[wdb], writes=[wdb])
                        for ii in range(4):
                            p.op("t", lambda h, ii=ii, oc32=oc32, wd_s=wd_s: h.matmul(
                                pv(ii), A(hot, (ii * 32 + oc32) * 128, [[16384, 128], [1, 128]]),
                                wd_s[:, :], start=(oc32 == 0), stop=(oc32 == 31)),
                                reads=[B_("hot"), wdb], writes=[bank[ii]])
                    for ii in range(4):
                        ti = half * 4 + ii
                        jt = tiles[ti]
                        gb_ = gbuf[ii % 2]
                        gbb = B_(f"gbf{ii % 2}")
                        p.op("s", lambda h, ti=ti, ii=ii, gb_=gb_: h.activation(
                            gb_[:, :], pv(ii), AF.Identity, scale=cols[:, 56 + ti:57 + ti]),
                            reads=[bank[ii], cs], writes=[gbb])
                        p.op("v", lambda h, jt=jt, cc=cc, gb_=gb_: h.tensor_tensor(
                            x_tc[:, jt, cc * 512:(cc + 1) * 512],
                            x_tc[:, jt, cc * 512:(cc + 1) * 512], gb_[:, :], ALU.add),
                            reads=[gbb, B_("x_tc")], writes=[B_("x_tc")])
                p.barrier(dma=KDMA)

    def plan_attnb():
        KSL = [0, 2, 3, 1]
        kb_ = B_("kcv")
        def kv(et, s):
            return A(kcv, et * 4 + s, [[32, 128], [1, 1]])
        kall = lambda s: A(kcv, s, [[32, 128], [4, 8], [1, 1]])
        p.op("s", lambda h: h.activation(kall(0), gwc[:, :], AF.Sigmoid),
             reads=[B_("gwc")], writes=[kb_])
        p.op("v", lambda h: h.tensor_scalar(kall(1), kall(0), -1.0, 1.0, ALU.mult, ALU.add),
             reads=[kb_], writes=[kb_])
        p.op("v", lambda h: h.tensor_tensor(kall(2), kall(0), kall(1), ALU.mult),
             reads=[kb_], writes=[kb_])
        p.op("v", lambda h: h.tensor_tensor(kall(3), kall(2), kall(1), ALU.mult),
             reads=[kb_], writes=[kb_])
        p.op("v", lambda h: h.tensor_tensor(kall(1), kall(3), kall(1), ALU.mult),
             reads=[kb_], writes=[kb_])
        p.dma("g", lambda h: h.dma_start(bout[:, :], bout_d), writes=[B_("bout")])
        p.dma("g", lambda h: h.dma_start(
            A(wib, 0, [[8 * 1024, 128], [1024, 8], [1, 1024]]),
            A(winT_d.tensor, 0, [[1024, 128], [128 * 1024, 8], [1, 1024]])),
            writes=[B_("wib")])
        p.dma("g", lambda h: h.dma_start(
            A(wob, 0, [[8 * 1024, 128], [1024, 8], [1, 1024]]),
            A(woutT_d.tensor, 0, [[1024, 128], [128 * 1024, 8], [1, 1024]])),
            writes=[B_("wob")])
        for b in range(B):
            for tt in range(4):
                jt = b * 4 + tt
                k = tt % 2
                ci = k * 8
                cb = B_(f"colL{k}")
                ln_stats(x_tc[:, jt, :], B_("x_tc"), ci, cb)
                p.op("s", lambda h, jt=jt, ci=ci: h.activation(
                    xltb2[:, :], x_tc[:, jt, :], AF.Identity,
                    bias=cols[:, ci + 1:ci + 2], scale=cols[:, ci:ci + 1]),
                    reads=[B_("x_tc"), cb], writes=[B_("xltb2")])
                for half in range(2):
                    bki = 4 + (tt + half) % 2
                    bk = bank[bki]
                    for kc in range(4):
                        kg = half * 4 + kc
                        p.op("t", lambda h, kg=kg, kc=kc, bki=bki: h.matmul(
                            pv(bki, kc * 128, (kc + 1) * 128),
                            xltb2[:, kg * 128:(kg + 1) * 128], ident[:, :],
                            start=True, stop=True),
                            reads=[B_("xltb2"), B_("ident")], writes=[bk])
                    p.op("s", lambda h, tt=tt, half=half, bki=bki: h.copy(
                        A(xctb, half * 4 * 512 + tt * 128, [[8 * 512, 128], [512, 4], [1, 128]]),
                        pv(bki).rearrange("p (a c) -> p a c", a=4)),
                        reads=[bk], writes=[B_("xctb")])
            for et in range(8):
                bki = et % 2
                bk = bank[bki]
                for kc in range(8):
                    p.op("t", lambda h, et=et, kc=kc, bki=bki: h.matmul(
                        pv(bki), wib[:, kc, et * 128:(et + 1) * 128],
                        xctb[:, kc, :], start=(kc == 0), stop=(kc == 7)),
                        reads=[B_("wib"), B_("xctb")], writes=[bk])
                p.op("s", lambda h, et=et, b=b, bki=bki: h.activation(
                    ub[:, et, b * 512:(b + 1) * 512], pv(bki), AF.Silu,
                    bias=binc[:, et:et + 1]), reads=[bk, B_("binc")], writes=[B_("ub")])
        for b in range(B):
            for et in range(8):
                if b == 0:
                    p.op("v", lambda h, et=et: h.tensor_scalar(
                        vbb[:, et, :], ub[:, et, 0:512], kv(et, KSL[0]), None, ALU.mult),
                        reads=[B_("ub"), kb_], writes=[B_("vbb")])
                else:
                    p.op("v", lambda h, et=et, b=b: h.tensor_scalar(
                        cvt0[:, :], ub[:, et, 0:512], kv(et, KSL[b]), None, ALU.mult),
                        reads=[B_("ub"), kb_], writes=[B_("cvt0")])
                    for s_ in range(1, b):
                        p.op("v", lambda h, et=et, b=b, s_=s_: h.scalar_tensor_tensor(
                            cvt0[:, :], ub[:, et, s_ * 512:(s_ + 1) * 512],
                            kv(et, KSL[b - s_]), cvt0[:, :], ALU.mult, ALU.add),
                            reads=[B_("ub"), kb_, B_("cvt0")], writes=[B_("cvt0")])
                    p.op("v", lambda h, et=et, b=b: h.scalar_tensor_tensor(
                        vbb[:, et, :], ub[:, et, b * 512:(b + 1) * 512], kv(et, KSL[0]),
                        cvt0[:, :], ALU.mult, ALU.add),
                        reads=[B_("ub"), kb_, B_("cvt0")], writes=[B_("vbb")])
            for jt4 in range(4):
                for cc in range(2):
                    bki = (jt4 * 2 + cc) % 2
                    bk = bank[bki]
                    for dc in range(8):
                        p.op("t", lambda h, jt4=jt4, cc=cc, dc=dc, bki=bki: h.matmul(
                            pv(bki), vbb[:, dc, jt4 * 128:(jt4 + 1) * 128],
                            wob[:, dc, cc * 512:(cc + 1) * 512],
                            start=(dc == 0), stop=False),
                            reads=[B_("vbb"), B_("wob")], writes=[bk])
                    p.op("t", lambda h, cc=cc, bki=bki: h.matmul(
                        pv(bki), ones_row[0:1, 0:128], bout[0:1, cc * 512:(cc + 1) * 512],
                        start=False, stop=True),
                        reads=[B_("ones_row"), B_("bout")], writes=[bk])
                    ttb_ = tts[bki]
                    ttbb = B_(f"tts{bki}")
                    p.op("s", lambda h, ttb_=ttb_, bki=bki: h.activation(
                        ttb_[:, :], pv(bki), AF.Tanh), reads=[bk], writes=[ttbb])
                    j = b * 4 + jt4
                    p.op("v", lambda h, j=j, cc=cc, ttb_=ttb_: h.tensor_tensor(
                        x_tc[:, j, cc * 512:(cc + 1) * 512],
                        x_tc[:, j, cc * 512:(cc + 1) * 512], ttb_[:, :], ALU.add),
                        reads=[ttbb, B_("x_tc")], writes=[B_("x_tc")])

    import os as _os2
    KBAR = _os2.environ.get("KBAR", "1") == "1"
    KDMA = _os2.environ.get("KDMA", "1") == "1"
    if "wq" in stages:
        plan_wquant()
    if "attn" in stages:
        plan_attn()
    if KBAR:
        p.barrier(dma=KDMA)
    if "ffn1" in stages:
        plan_ffn(0)
    if KBAR:
        p.barrier(dma=KDMA)
    if "attnb" in stages:
        plan_attnb()
    if KBAR:
        p.barrier(dma=KDMA)
    if "ffn2" in stages:
        plan_ffn(1)

    p.dma("y", lambda h: h.dma_start(
        A(out_d.tensor, 0, [[1024, 128], [128 * 1024, 16], [1, 1024]]),
        A(x_tc, 0, [[16 * 1024, 128], [1024, 16], [1, 1024]])),
        reads=[B_("x_tc")], writes=[B_("out")])

    def final_waits(h):
        for name, val in p.dma_cnt.items():
            h.wait_ge(p.sems[name], val)
        return h.nop()
    p.q["y"].append(final_waits)

    p.assign_values()
    globals()["LAST_P"] = p
    with nc.Block() as block:
        @block.tensor
        def _(h):
            for f in p.q["t"]:
                f(h)

        @block.vector
        def _(h):
            for f in p.q["v"]:
                f(h)

        @block.scalar
        def _(h):
            for f in p.q["s"]:
                f(h)

        @block.sync
        def _(h):
            for f in p.q["y"]:
                f(h)

        @block.gpsimd
        def _(h):
            for f in p.q["g"]:
                f(h)

    stack.close()
    return nc


def _stage_inputs(kw):
    X = np.ascontiguousarray(np.asarray(kw["L0_attn_a_X"], np.float32))
    wq = np.ascontiguousarray(np.asarray(kw["L0_attn_a_Wq"], np.float32))
    wk = np.ascontiguousarray(np.asarray(kw["L0_attn_a_Wk"], np.float32))
    wv = np.ascontiguousarray(np.asarray(kw["L0_attn_a_Wv"], np.float32))
    wr = np.asarray(kw["L0_attn_a_Wr"], np.float32)
    wrp = np.zeros((E, 512), np.float32)
    for hh in range(H):
        wrp[:, 32 * hh:32 * hh + 10] = wr[:, 10 * hh:10 * hh + 10]
    wpj = np.ascontiguousarray(np.asarray(kw["L0_attn_a_Wproj"], np.float32))
    bq = np.asarray(kw["L0_attn_a_bq"], np.float32)
    bqc = np.ascontiguousarray(bq.reshape(8, 128).T)
    br = np.asarray(kw["L0_attn_a_br"], np.float32)
    brp = np.zeros((512,), np.float32)
    for hh in range(H):
        brp[32 * hh:32 * hh + 10] = br[10 * hh:10 * hh + 10]
    brc = np.ascontiguousarray(brp.reshape(4, 128).T)
    bpj = np.ascontiguousarray(np.asarray(kw["L0_attn_a_bproj"], np.float32).reshape(1, E))
    bndr = np.ascontiguousarray(np.asarray(kw["L0_attn_a_b_nd"], np.float32)[:, ::-1])
    winT = np.ascontiguousarray(np.asarray(kw["L0_attn_b_W_in"], np.float32).T)
    woutT = np.ascontiguousarray(np.asarray(kw["L0_attn_b_W_out"], np.float32).T)
    binc = np.ascontiguousarray(np.asarray(kw["L0_attn_b_b_in"], np.float32).reshape(8, 128).T)
    bout = np.ascontiguousarray(np.asarray(kw["L0_attn_b_b_out"], np.float32).reshape(1, E))
    gwc = np.ascontiguousarray(np.asarray(kw["L0_attn_b_weight"], np.float32).reshape(8, 128).T)
    wgT = np.ascontiguousarray(np.asarray(kw["L0_ffn_w_gate"], np.float32).T)
    wuT = np.ascontiguousarray(np.asarray(kw["L0_ffn_w_up"], np.float32).T)
    wdT = np.ascontiguousarray(np.asarray(kw["L0_ffn_w_down"], np.float32).T)
    idm = np.eye(128, dtype=bfd)

    maps = []
    for c in range(NCORE):
        lo = 512 * c - 1024
        hi = 512 * c + 512
        xs = np.zeros((B, SPAN, E), np.float32)
        s0 = max(0, lo)
        xs[:, s0 - lo:, :] = X[:, s0:hi, :]
        edge = np.zeros((1, 1024), np.float32)
        npad = max(0, -lo)
        edge[0, :npad] = NEG
        maps.append(dict(
            xsp=np.ascontiguousarray(xs), wq=wq, wk=wk, wv=wv, wrp=wrp, wpj=wpj,
            bqc=bqc, brc=brc, bpj=bpj, bndr=bndr, edge=np.ascontiguousarray(edge),
            winT=winT, woutT=woutT, binc=binc, bout=bout, gwc=gwc,
            wgT=wgT, wuT=wuT, wdT=wdT, idm=idm,
        ))
    return maps


def kernel(**kw):
    maps = _stage_inputs(kw)
    nc = build_program()
    res = run_bass_kernel_spmd(nc, maps, core_ids=list(range(NCORE)))
    out = np.empty((B, T, E), np.float32)
    for c in range(NCORE):
        o = np.asarray(res.results[c]["out"]).reshape(16, 128, E)
        for b in range(B):
            out[b, 512 * c:512 * (c + 1), :] = o[4 * b:4 * b + 4].reshape(512, E)
    return out
